# version 08
# baseline (speedup 1.0000x reference)
import contextlib
import math
import numpy as np
import ml_dtypes

import concourse.bacc as bacc
import concourse.mybir as mybir
import concourse.aot_env as aot_env
from concourse.bass_utils import run_bass_kernel_spmd

F32 = mybir.dt.float32
BF16 = mybir.dt.bfloat16
AF = mybir.ActivationFunctionType
ALU = mybir.AluOpType

B, L, H, N2 = 4, 8192, 1024, 32
T = 128
NCHUNK = L // T
NCORE = 8
HL = 128
BC = B * NCHUNK
PI = math.pi
TWO_PI = 2.0 * math.pi


def _cw_split():
    parts, rem = [], TWO_PI
    for _ in range(2):
        v = np.float32(rem)
        u = np.frombuffer(v.tobytes(), np.uint32)[0] & np.uint32(0xFFFFF000)
        v = float(np.frombuffer(np.uint32(u).tobytes(), np.float32)[0])
        parts.append(v)
        rem -= v
    parts.append(float(np.float32(rem)))
    return parts


C_HI, C_MID, C_LO = _cw_split()

ENGINES = ("tensor", "vector", "scalar", "sync", "gpsimd")
DMA_SLOTS = 6


class _Step:
    __slots__ = ("idx", "eng", "emit", "deps", "ndma", "end", "slot",
                 "slot_end", "tag", "sync_all")


class Chain:
    def __init__(self, nc):
        self.nc = nc
        self.steps = []
        self.sems = {}
        self.slot_sems = {}
        self.counts = {}
        self.slot_state = {}
        self._slot_rr = {}
        self._built_upto = 0
        self._barrier = []
        self._barrier_idx = -1

    def add(self, eng, emit, deps="prev", ndma=0, tag="", sync_all=False):
        if deps == "prev":
            deps = [self.steps[-1]] if self.steps else []
        deps = [d for d in (deps or []) if d is not None]
        st = _Step()
        st.idx, st.eng, st.emit, st.deps, st.ndma, st.tag = (
            len(self.steps), eng, emit, deps, ndma, tag)
        st.sync_all = sync_all
        st.end = st.slot = st.slot_end = None
        if self._barrier and not any(d.idx >= self._barrier_idx for d in deps):
            deps = deps + [b for b in self._barrier if b not in deps]
            st.deps = deps
        if ndma:
            rr = self._slot_rr.get(eng, 0)
            st.slot = rr % DMA_SLOTS
            self._slot_rr[eng] = rr + 1
            key = (eng, st.slot)
            if key not in self.slot_sems:
                self.slot_sems[key] = self.nc.alloc_semaphore(
                    f"dma_{eng}_{st.slot}")
                self.slot_state[key] = 0
            st.slot_end = self.slot_state[key] + 16 * ndma
            self.slot_state[key] = st.slot_end
        else:
            if eng not in self.sems:
                self.sems[eng] = self.nc.alloc_semaphore(f"ch_{eng}")
                self.counts[eng] = 0
            self.counts[eng] += 1
            st.end = self.counts[eng]
        self.steps.append(st)
        return st

    def section_barrier(self):
        self._barrier = self.frontier()
        self._barrier_idx = len(self.steps)

    def gap(self, e, inst, eng):
        key = "gap_" + eng
        if key not in self.sems:
            self.sems[key] = self.nc.alloc_semaphore(key)
            self.counts[key] = 0
        self.counts[key] += 1
        inst.then_inc(self.sems[key], 1)
        e.wait_ge(self.sems[key], self.counts[key])

    def wait_value(self, st):
        if st.ndma:
            return (self.slot_sems[(st.eng, st.slot)], st.slot_end)
        return (self.sems[st.eng], st.end)

    def build_section(self, block):
        todo = self.steps[self._built_upto:]
        self._built_upto = len(self.steps)
        per_eng = {e: [] for e in ENGINES}
        for st in todo:
            per_eng[st.eng].append(st)

        def mk(mysteps):
            def run(e):
                for st in mysteps:
                    waits = {}
                    for d in st.deps:
                        if (d.eng == st.eng and not d.ndma and not st.ndma
                                and not st.sync_all):
                            continue
                        sem, val = self.wait_value(d)
                        if waits.get(id(sem), (None, -1))[1] < val:
                            waits[id(sem)] = (sem, val)
                    if st.ndma:
                        key = (st.eng, st.slot)
                        prev = st.slot_end - 16 * st.ndma
                        if prev > 0:
                            sem = self.slot_sems[key]
                            if waits.get(id(sem), (None, -1))[1] < prev:
                                waits[id(sem)] = (sem, prev)
                    for sem, val in waits.values():
                        e.wait_ge(sem, val)
                    out = st.emit(e)
                    if st.ndma:
                        insts = out if isinstance(out, (list, tuple)) else [out]
                        assert len(insts) == st.ndma, (st.idx, st.tag, len(insts))
                        sem = self.slot_sems[(st.eng, st.slot)]
                        for ins in insts:
                            ins.then_inc(sem, 16)
                    else:
                        assert out is not None, (st.idx, st.tag)
                        out.then_inc(self.sems[st.eng], 1)
            return run

        for eng in ENGINES:
            if per_eng[eng]:
                getattr(block, eng)(mk(per_eng[eng]))

    def frontier(self):
        last = {}
        for st in self.steps:
            last[(st.eng, st.slot if st.ndma else None)] = st
        return list(last.values())


def _prep_phase1_inputs(i, ins):
    h0 = 128 * i
    u = np.asarray(ins["L0_attn_u"], np.float32)
    log_dt = np.asarray(ins["L0_attn_log_dt"], np.float32)
    lwre = np.asarray(ins["L0_attn_log_w_real"], np.float32).reshape(N2)
    wim = np.asarray(ins["L0_attn_w_imag"], np.float32).reshape(N2)
    c_re = np.asarray(ins["L0_attn_C_real"], np.float32).reshape(H, N2)
    c_im = np.asarray(ins["L0_attn_C_imag"], np.float32).reshape(H, N2)
    D = np.asarray(ins["L0_attn_D"], np.float32).reshape(H)

    d = {}
    uc = u[:, :, h0:h0 + 128].reshape(B, NCHUNK, T, HL)
    d["u_conv"] = np.ascontiguousarray(uc.transpose(2, 3, 0, 1)).reshape(
        T, HL * BC)

    def repA(val_hn):
        a = np.broadcast_to(np.asarray(val_hn, np.float32)[:, None, :],
                            (HL, 2, N2)).reshape(HL * 64)
        return np.ascontiguousarray(np.broadcast_to(a[None, :], (T, HL * 64)))

    ld_h = np.broadcast_to(log_dt[h0:h0 + 128, None], (HL, N2))
    d["tA_logdt"] = repA(ld_h)
    d["tA_lwre"] = repA(np.broadcast_to(lwre[None, :], (HL, N2)))
    d["tA_wim"] = repA(np.broadcast_to(wim[None, :], (HL, N2)))

    def repB(val_hn):
        v = np.asarray(val_hn, np.float32).reshape(64, 2, N2)
        out = np.empty((2, 2, N2, 64, 129), np.float32)
        for hp in range(2):
            out[hp] = np.broadcast_to(v[:, hp, :].T[None, :, :, None],
                                      (2, N2, 64, 129))
        return np.ascontiguousarray(out.reshape(128, 64 * 129))

    d["tB_logdt"] = repB(ld_h)
    d["tB_lwre"] = repB(np.broadcast_to(lwre[None, :], (HL, N2)))
    d["tB_wim"] = repB(np.broadcast_to(wim[None, :], (HL, N2)))

    def colB(mat_hn):
        v = np.asarray(mat_hn, np.float32).reshape(64, 2, N2)
        out = np.empty((2, 2, N2, 64), np.float32)
        for hp in range(2):
            out[hp] = np.broadcast_to(v[:, hp, :].T[None], (2, N2, 64))
        return np.ascontiguousarray(out.reshape(128, 64))

    d["cB_re"] = colB(c_re[h0:h0 + 128])
    d["cB_im"] = colB(c_im[h0:h0 + 128])
    wcol = np.broadcast_to(wim[None, None, :], (2, 2, N2)).reshape(128, 1)
    d["wB_wim"] = np.ascontiguousarray(wcol)
    lcol = np.broadcast_to(lwre[None, None, :], (2, 2, N2)).reshape(128, 1)
    d["wB_lwre"] = np.ascontiguousarray(lcol)

    d["al_logdt_col"] = np.ascontiguousarray(log_dt[h0:h0 + 128, None])
    d["al_wim"] = np.ascontiguousarray(
        np.broadcast_to(wim[None, None, :], (HL, B, N2)).reshape(HL, 128))
    d["al_lwre"] = np.ascontiguousarray(
        np.broadcast_to(lwre[None, None, :], (HL, B, N2)).reshape(HL, 128))
    d["D_rep"] = np.ascontiguousarray(
        np.broadcast_to(D[None, h0:h0 + 128], (128, 128)))
    return d


def _col_layout(vec, nb):
    v = np.asarray(vec, np.float32).reshape(nb, 128)
    return np.ascontiguousarray(v.T)


def _prep_phase2_inputs(i, ins):
    b_c, half, k4 = i >> 1, i & 1, i % 4
    d = {}
    u = np.asarray(ins["L0_attn_u"], np.float32)
    d["u_tok"] = np.ascontiguousarray(
        u[b_c, half * 4096:(half + 1) * 4096, :].T)
    w_out = np.asarray(ins["L0_attn_W_out"], np.float32)
    d["w_outT"] = np.ascontiguousarray(w_out.T)
    d["b_out_col"] = _col_layout(ins["L0_attn_b_out"], 8)
    d["ln_g_col"] = _col_layout(ins["L0_ffn_ln_g"], 8)
    d["ln_b_col"] = _col_layout(ins["L0_ffn_ln_b"], 8)
    d["aw1"] = np.ascontiguousarray(np.asarray(ins["L0_ffn_aw1"], np.float32))
    d["aw2"] = np.ascontiguousarray(np.asarray(ins["L0_ffn_aw2"], np.float32))
    d["ab1_col"] = _col_layout(ins["L0_ffn_ab1"], 32)
    d["ab2_col"] = _col_layout(ins["L0_ffn_ab2"], 8)
    gw = np.asarray(ins["L0_ffn_gate_w"], np.float32).reshape(1024, 6)
    d["gw_cols"] = np.ascontiguousarray(gw.reshape(8, 128, 6).transpose(
        1, 0, 2).reshape(128, 48))
    gb = np.zeros((8, 1), np.float32)
    gb[0:6, 0] = np.asarray(ins["L0_ffn_gate_b"], np.float32).reshape(6)
    d["gb_col"] = gb
    ew1 = np.asarray(ins["L0_ffn_ew1"], np.float32).reshape(6144, 2048)
    d["ew1_shard"] = np.ascontiguousarray(ew1[i * 768:(i + 1) * 768])
    ew2 = np.asarray(ins["L0_ffn_ew2"], np.float32).reshape(12288, 1024)
    d["ew2_shard"] = np.ascontiguousarray(ew2[i * 1536:(i + 1) * 1536])
    eb1 = np.asarray(ins["L0_ffn_eb1"], np.float32).reshape(6, 16, 128)
    d["eb1_col"] = np.ascontiguousarray(
        eb1.transpose(2, 0, 1).reshape(128, 96))
    eb2 = np.zeros((8, 1024), np.float32)
    eb2[0:6] = np.asarray(ins["L0_ffn_eb2"], np.float32).reshape(6, 1024)
    d["eb2_nat"] = eb2
    gens = np.asarray(ins["L0_ffn_gens"], np.float32)
    ga = gens[k4].T if k4 in (0, 2) else gens[k4]
    d["gens_a"] = np.ascontiguousarray(ga)
    d["gens_at"] = np.ascontiguousarray(ga.T)
    gc = np.asarray(ins["L0_ffn_gc_w"], np.float32).reshape(2048)
    d["gc1_col"] = _col_layout(gc[0:1024], 8)
    d["gc2_col"] = _col_layout(gc[1024:2048], 8)
    d["gcb"] = np.asarray(ins["L0_ffn_gc_b"], np.float32).reshape(1, 1)
    return d


def prep_inputs(i, ins):
    d = _prep_phase1_inputs(i, ins)
    d.update(_prep_phase2_inputs(i, ins))
    return d


_FACT = [math.factorial(n) for n in range(13)]
_CEYE_COEF = [1.0, 0.5, 1.0 / 6, 1.0 / 24, 1.0 / 120, 1.0 / 720,
              1.0 / 5040, 1.0 / 40320, 1.0 / 362880, 1.0 / 3628800,
              1.0 / 39916800, 1.0 / 479001600]


def _sincos(ch, th, sc, oc, os, sl, tag, ki=None):
    assert ki is not None

    def reduce_into(v, out, extra):
        off = (extra + PI) / TWO_PI
        v.tensor_scalar(sc[sl], th[sl], 1.0 / TWO_PI, off, ALU.mult, ALU.add)
        v.tensor_copy(ki[sl], sc[sl])
        v.tensor_copy(sc[sl], ki[sl])
        v.tensor_single_scalar(out[sl], th[sl], extra + PI, ALU.add)
        v.scalar_tensor_tensor(out[sl], sc[sl], -C_HI, out[sl], ALU.mult, ALU.add)
        v.scalar_tensor_tensor(out[sl], sc[sl], -C_MID, out[sl], ALU.mult, ALU.add)
        v.scalar_tensor_tensor(out[sl], sc[sl], -C_LO, out[sl], ALU.mult, ALU.add)
        v.tensor_single_scalar(out[sl], out[sl], PI, ALU.subtract)
        v.tensor_single_scalar(sc[sl], out[sl], -PI, ALU.is_lt)
        v.scalar_tensor_tensor(out[sl], sc[sl], TWO_PI, out[sl], ALU.mult, ALU.add)
        v.tensor_single_scalar(out[sl], out[sl], 3.141592, ALU.min)
        return v.tensor_single_scalar(out[sl], out[sl], -3.141592, ALU.max)

    ch.add("vector", lambda v: reduce_into(v, os, 0.0), tag=tag + "_sred")
    ch.add("scalar", lambda s: s.activation(os[sl], os[sl], AF.Sin),
           tag=tag + "_sin")
    ch.add("vector", lambda v: reduce_into(v, oc, PI / 2), tag=tag + "_cred")
    ch.add("scalar", lambda s: s.activation(oc[sl], oc[sl], AF.Sin),
           tag=tag + "_cos")


def _rows(t, lo, hi, cols=None):
    ap = t.ap() if hasattr(t, "ap") else t
    if cols is None:
        return ap[lo:hi, :]
    return ap[lo:hi, cols]


def build_program(phase1_only=False, upto=None, variant=None):
    nc = bacc.Bacc("TRN2", target_bir_lowering=False,
                   detect_race_conditions=False)
    p = type("P", (), {})()

    def din(name, shape):
        setattr(p, name, nc.dram_tensor(name, list(shape), F32,
                                        kind="ExternalInput"))
    din("u_conv", (T, HL * BC))
    for nm in ("tA_logdt", "tA_lwre", "tA_wim"):
        din(nm, (128, 8192))
    for nm in ("tB_logdt", "tB_lwre", "tB_wim"):
        din(nm, (128, 8256))
    din("cB_re", (128, 64))
    din("cB_im", (128, 64))
    din("wB_wim", (128, 1))
    din("wB_lwre", (128, 1))
    din("al_logdt_col", (128, 1))
    din("al_wim", (128, 128))
    din("al_lwre", (128, 128))
    din("D_rep", (128, 128))

    din("u_tok", (1024, 4096))
    din("w_outT", (1024, 1024))
    din("b_out_col", (128, 8))
    din("ln_g_col", (128, 8))
    din("ln_b_col", (128, 8))
    din("aw1", (1024, 4096))
    din("aw2", (4096, 1024))
    din("ab1_col", (128, 32))
    din("ab2_col", (128, 8))
    din("gw_cols", (128, 48))
    din("gb_col", (8, 1))
    din("ew1_shard", (768, 2048))
    din("ew2_shard", (1536, 1024))
    din("eb1_col", (128, 96))
    din("eb2_nat", (8, 1024))
    din("gens_a", (1024, 1024))
    din("gens_at", (1024, 1024))
    din("gc1_col", (128, 8))
    din("gc2_col", (128, 8))
    din("gcb", (1, 1))

    p.eye_f32 = nc.inline_tensor(np.eye(128, dtype=np.float32), name="c_eye_f")
    p.eye_bf = nc.inline_tensor(np.eye(128, dtype=ml_dtypes.bfloat16),
                                name="c_eye_b")
    p.triu_f32 = nc.inline_tensor(
        np.ascontiguousarray(np.triu(np.ones((128, 128), np.float32))),
        name="c_triu")
    p.tcol = nc.inline_tensor(
        np.ascontiguousarray((127.0 - np.arange(128, dtype=np.float32))[:, None]),
        name="c_tcol")
    xr = np.broadcast_to(np.arange(129, dtype=np.float32)[None, None, :],
                         (128, 32, 129)).reshape(128, 32 * 129)
    p.xramp = nc.inline_tensor(np.ascontiguousarray(xr), name="c_xramp")
    ce = np.zeros((128, 128 * 12), np.float32)
    for j, cf in enumerate(_CEYE_COEF):
        ce[:, j * 128:(j + 1) * 128] = cf * np.eye(128, dtype=np.float32)
    p.ceye = nc.inline_tensor(ce.astype(ml_dtypes.bfloat16), name="c_ceye")
    es = np.zeros((6, 768), np.float32)
    for e in range(6):
        es[e, e * 128:(e + 1) * 128] = 1.0
    p.esel = nc.inline_tensor(es.astype(ml_dtypes.bfloat16), name="c_esel")
    p.ones_cb = nc.inline_tensor(
        np.ones((128, 1), ml_dtypes.bfloat16), name="c_onescb")
    p.ones_cf = nc.inline_tensor(
        np.ones((128, 1), np.float32), name="c_onescf")
    p.ones_rf = nc.inline_tensor(
        np.ones((1, 128), np.float32), name="c_onesrf")
    p.eps_t = nc.inline_tensor(np.full((8, 1), 1e-5, np.float32),
                               name="c_eps")

    p.z_hbm = nc.dram_tensor("z_hbm", [64, HL * BC], F32)
    p.p_hbm = nc.dram_tensor("p_hbm", [128, 8256], BF16)
    p.q_hbm = nc.dram_tensor("q_hbm", [128, 8256], BF16)
    p.cc_a2in = nc.dram_tensor("cc_a2in", [1024, 4096], BF16)
    p.cc_a2out = nc.dram_tensor("cc_a2out", [1024, 4096], BF16)
    p.x1b = nc.dram_tensor("x1b", [1024, 4096], BF16)
    p.xnb = nc.dram_tensor("xnb", [1024, 4096], BF16)
    p.abelb = nc.dram_tensor("abelb", [1024, 4096], BF16)
    p.w6_hbm = nc.dram_tensor("w6_hbm", [32, 4096], BF16)
    p.cc_e1in = nc.dram_tensor("cc_e1in", [768, 2048], BF16)
    p.cc_e1out = nc.dram_tensor("cc_e1out", [6144, 2048], BF16)
    p.cc_e2in = nc.dram_tensor("cc_e2in", [1536, 1024], BF16)
    p.cc_e2out = nc.dram_tensor("cc_e2out", [12288, 1024], BF16)
    p.gA_hbm = nc.dram_tensor("gA_hbm", [1024, 1024], BF16)
    p.gAt_hbm = nc.dram_tensor("gAt_hbm", [1024, 1024], BF16)
    for nm in ("exA2", "exA2t", "exA3", "exA4", "exA4t",
               "exS0", "exS1", "exS2", "exW", "exP1", "exP2", "exR"):
        setattr(p, nm, nc.dram_tensor(nm, [1024, 1024], BF16))
    p.cc_Ein = nc.dram_tensor("cc_Ein", [1024, 1024], BF16)
    p.cc_Eout = nc.dram_tensor("cc_Eout", [1024 * NCORE, 1024], BF16)
    p.out_ot = nc.dram_tensor("out_ot", [1024, 4096], F32,
                              kind="ExternalOutput")

    if phase1_only:
        p.dbg_g = nc.dram_tensor("dbg_g", [1024, 4096], BF16,
                                 kind="ExternalOutput")
    if upto is not None:
        p.dbg_f = nc.dram_tensor("dbg_f", [128, 32768], F32,
                                 kind="ExternalOutput")
        p.dbg_b = nc.dram_tensor("dbg_b", [128, 36992], BF16,
                                 kind="ExternalOutput")

    def _finish_dump(srcs):
        def ed(s):
            return [s.dma_start(out=d, in_=sr) for d, sr in srcs]
        std = ch.add("sync", ed, deps=ch.frontier(), ndma=len(srcs), tag="dump")
        ch.add("gpsimd", lambda gp: gp.engine_nop(), deps=[std], tag="fin")

    ch = Chain(nc)

    lstack = []

    def allocL(name, shape, dt):
        t = nc.alloc_sbuf_tensor(name, shape, dt, side="left")
        lstack.append(t)
        return t

    eye_f = allocL("eye_f", [128, 128], F32)
    triu_f = allocL("triu_f", [128, 128], F32)
    eye_b = allocL("eye_b", [128, 128], BF16)
    tcol = allocL("tcol", [128, 1], F32)
    drep = allocL("drep", [128, 128], F32)
    a_re = allocL("a_re", [128, 128], F32)
    a_im = allocL("a_im", [128, 128], F32)
    epsc = allocL("epsc", [8, 1], F32)
    stkA = contextlib.ExitStack()
    tabA = stkA.enter_context(nc.sbuf_tensor("tabA", [128, 8192], BF16))

    def e_consts(s):
        return [
            s.dma_start(out=eye_f[:], in_=p.eye_f32[:, :]),
            s.dma_start(out=triu_f[:], in_=p.triu_f32[:, :]),
            s.dma_start(out=eye_b[:], in_=p.eye_bf[:, :]),
            s.dma_start(out=tcol[:], in_=p.tcol[:, :]),
            s.dma_start(out=drep[:], in_=p.D_rep[:, :]),
            s.dma_start(out=epsc[:], in_=p.eps_t[:, :]),
        ]
    st_consts = ch.add("sync", e_consts, deps=[], ndma=6, tag="consts")

    with contextlib.ExitStack() as _ctx:
        v0 = _ctx.enter_context(nc.sbuf_tensor("v0", [128, 4224], F32))
        v1 = _ctx.enter_context(nc.sbuf_tensor("v1", [128, 4224], F32))
        v2 = _ctx.enter_context(nc.sbuf_tensor("v2", [128, 4224], F32))
        v3 = _ctx.enter_context(nc.sbuf_tensor("v3", [128, 4224], F32))
        v4 = _ctx.enter_context(nc.sbuf_tensor("v4", [128, 4224], F32))
        pk = _ctx.enter_context(nc.sbuf_tensor("pk", [128, 4128], BF16))
        xramp = _ctx.enter_context(nc.sbuf_tensor("xramp", [128, 4128], F32))
        ki = _ctx.enter_context(nc.sbuf_tensor("ki", [128, 4224], mybir.dt.int32))
        dbgt = _ctx.enter_context(nc.sbuf_tensor("dbgt", [128, 512], F32))
        cwre = _ctx.enter_context(nc.sbuf_tensor("cwre", [128, 64], F32))
        cwim = _ctx.enter_context(nc.sbuf_tensor("cwim", [128, 64], F32))
        cw0 = _ctx.enter_context(nc.sbuf_tensor("cw0", [128, 64], F32))
        cw1 = _ctx.enter_context(nc.sbuf_tensor("cw1", [128, 64], F32))
        cw2 = _ctx.enter_context(nc.sbuf_tensor("cw2", [128, 64], F32))
        cwa = _ctx.enter_context(nc.sbuf_tensor("cwa", [128, 32], F32))
        cwb = _ctx.enter_context(nc.sbuf_tensor("cwb", [128, 32], F32))
        dbg0 = _ctx.enter_context(nc.sbuf_tensor("dbg0", [128, 32], F32))
        cre_sb = _ctx.enter_context(nc.sbuf_tensor("cre_sb", [128, 64], F32))
        cim_sb = _ctx.enter_context(nc.sbuf_tensor("cim_sb", [128, 64], F32))
        wrecol = _ctx.enter_context(nc.sbuf_tensor("wrecol", [128, 1], F32))
        wimcol = _ctx.enter_context(nc.sbuf_tensor("wimcol", [128, 1], F32))
        alw = _ctx.enter_context(nc.sbuf_tensor("alw", [128, 128], F32))
        allw = _ctx.enter_context(nc.sbuf_tensor("allw", [128, 128], F32))
        aldt = _ctx.enter_context(nc.sbuf_tensor("aldt", [128, 1], F32))
        alm = _ctx.enter_context(nc.sbuf_tensor("alm", [128, 128], F32))
        blk0 = _ctx.enter_context(nc.Block())

        st_s0in = ch.add("sync", lambda s: [
            s.dma_start(out=xramp[:], in_=p.xramp[:, :]),
            s.dma_start(out=cre_sb[:], in_=p.cB_re[:, :]),
            s.dma_start(out=cim_sb[:], in_=p.cB_im[:, :]),
            s.dma_start(out=wrecol[:], in_=p.wB_lwre[:, :]),
            s.dma_start(out=wimcol[:], in_=p.wB_wim[:, :]),
            s.dma_start(out=alw[:], in_=p.al_wim[:, :]),
            s.dma_start(out=allw[:], in_=p.al_lwre[:, :]),
            s.dma_start(out=aldt[:], in_=p.al_logdt_col[:, :]),
        ], deps=[], ndma=8, tag="s0in")

        ch.add("scalar", lambda s: s.activation(wrecol[:], wrecol[:], AF.Exp),
               deps=[st_s0in], tag="wre_e")
        st_wren = ch.add("vector", lambda v: v.tensor_single_scalar(
            wrecol[:], wrecol[:], -1.0, ALU.mult), tag="wre_n")

        def e_al1(v):
            return v.tensor_single_scalar(alm[:], allw[:], aldt[:, 0:1], ALU.add)
        ch.add("vector", e_al1, deps=[st_s0in], tag="al1")
        ch.add("scalar", lambda s: s.activation(alm[:], alm[:], AF.Exp),
               tag="al2")
        ch.add("scalar", lambda s: s.activation(aldt[:], aldt[:], AF.Exp),
               tag="al3")
        def e_al4(v):
            v.tensor_single_scalar(alm[:], alm[:], float(T), ALU.mult)
            v.tensor_single_scalar(allw[:], alw[:], aldt[:, 0:1], ALU.mult)
            return v.tensor_single_scalar(allw[:], allw[:], float(T), ALU.mult)
        ch.add("vector", e_al4, tag="al4")
        ch.add("scalar", lambda s: s.activation(alm[:], alm[:], AF.Exp,
                                                scale=-1.0), tag="al5")
        _sincos(ch, allw, alw, a_re, a_im, (slice(None), slice(0, 128)), "al", ki=ki)
        def e_al6(v):
            v.tensor_tensor(a_re[:], a_re[:], alm[:], ALU.mult)
            return v.tensor_tensor(a_im[:], a_im[:], alm[:], ALU.mult)
        ch.add("vector", e_al6, tag="al6")

        st_half_done = None
        for hf in range(2):
            lo = hf * 4096
            sl = (slice(None), slice(0, 4096))
            st_in = ch.add("sync", (lambda s, lo=lo: [
                s.dma_start(out=v0[:, 0:4096], in_=p.tA_logdt[:, lo:lo + 4096]),
                s.dma_start(out=v1[:, 0:4096], in_=p.tA_lwre[:, lo:lo + 4096]),
                s.dma_start(out=v2[:, 0:4096], in_=p.tA_wim[:, lo:lo + 4096]),
            ]), deps=[st_consts, st_half_done], ndma=3, tag=f"tAin{hf}")

            def eA1(v, sl=sl):
                return v.tensor_tensor(v3[sl], v0[sl], v1[sl], ALU.add)
            ch.add("vector", eA1, deps=[st_in], tag=f"tA1.{hf}")
            ch.add("scalar", lambda s, sl=sl: s.activation(v3[sl], v3[sl], AF.Exp),
                   tag=f"tA2.{hf}")
            def eA3(v, sl=sl):
                return v.tensor_single_scalar(v3[sl], v3[sl], tcol[:, 0:1], ALU.mult)
            ch.add("vector", eA3, tag=f"tA3.{hf}")
            ch.add("scalar", lambda s, sl=sl: s.activation(v3[sl], v3[sl], AF.Exp,
                                                           scale=-1.0),
                   tag=f"tA4.{hf}")
            ch.add("scalar", lambda s, sl=sl: s.activation(v0[sl], v0[sl], AF.Exp),
                   tag=f"tA5.{hf}")
            def eA6(v, sl=sl):
                v.tensor_tensor(v2[sl], v0[sl], v2[sl], ALU.mult)
                return v.tensor_single_scalar(v2[sl], v2[sl], tcol[:, 0:1], ALU.mult)
            ch.add("vector", eA6, tag=f"tA6.{hf}")
            _sincos(ch, v2, v4, v0, v1, sl, f"tA7.{hf}", ki=ki)
            def eA8(v, lo=lo, sl=sl):
                v.tensor_single_scalar(v0[sl], v0[sl], 2.0, ALU.mult)
                i8 = v.tensor_single_scalar(v1[sl], v1[sl], -2.0, ALU.mult)
                ch.gap(v, i8, "vector")
                def sv(t, r):
                    return t.ap()[:, lo:lo + 4096].rearrange(
                        "p (h r n) -> p h r n", r=2, n=32)[:, :, r, :]
                def sw(t, r):
                    return t.ap()[:, 0:4096].rearrange(
                        "p (h r n) -> p h r n", r=2, n=32)[:, :, r, :]
                v.tensor_tensor(sv(tabA, 0), sw(v3, 0), sw(v0, 0), ALU.mult)
                return v.tensor_tensor(sv(tabA, 1), sw(v3, 1), sw(v1, 1),
                                       ALU.mult)
            st_half_done = ch.add("vector", eA8, tag=f"tA8.{hf}")

        st_prev_p = None
        for hf in range(2):
            lo = hf * 4128
            sl = (slice(None), slice(0, 4128))
            st_in = ch.add("sync", (lambda s, lo=lo: [
                s.dma_start(out=v0[:, 0:4128], in_=p.tB_logdt[:, lo:lo + 4128]),
                s.dma_start(out=v1[:, 0:4128], in_=p.tB_lwre[:, lo:lo + 4128]),
                s.dma_start(out=v2[:, 0:4128], in_=p.tB_wim[:, lo:lo + 4128]),
            ]), deps=[st_s0in, st_half_done if hf == 0 else st_prev_p],
                ndma=3, tag=f"tBin{hf}")

            def eB1(v):
                return v.tensor_tensor(v3[sl], v0[sl], v1[sl], ALU.add)
            ch.add("vector", eB1, deps=[st_in], tag=f"tB1.{hf}")
            ch.add("scalar", lambda s: s.activation(v3[sl], v3[sl], AF.Exp),
                   tag=f"tB2.{hf}")
            ch.add("scalar", lambda s: s.activation(v0[sl], v0[sl], AF.Exp),
                   tag=f"tB3.{hf}")
            def eB4(v):
                v.tensor_tensor(v2[sl], v0[sl], v2[sl], ALU.mult)
                return v.tensor_tensor(v2[sl], v2[sl], xramp[sl], ALU.mult)
            ch.add("vector", eB4, tag=f"tB4.{hf}")
            _sincos(ch, v2, v4, v0, v1, sl, f"tB5.{hf}", ki=ki)
            def eB6(v):
                return v.tensor_tensor(v2[sl], v3[sl], xramp[sl], ALU.mult)
            ch.add("vector", eB6, tag=f"tB6.{hf}")
            ch.add("scalar", lambda s: s.activation(v4[sl], v2[sl], AF.Exp),
                   tag=f"tB7.{hf}")
            st_radp = ch.add("scalar",
                             lambda s: s.activation(v2[sl], v2[sl], AF.Exp,
                                                    scale=-1.0),
                             tag=f"tB8.{hf}")
            def eQ(v):
                v.tensor_single_scalar(v4[sl], v4[sl], 2.0, ALU.mult)
                for r0, r1 in ((0, 32), (64, 96)):
                    v.tensor_tensor(pk.ap()[r0:r1, 0:4128], v4.ap()[r0:r1, 0:4128],
                                    v0.ap()[r0:r1, 0:4128], ALU.mult)
                last = None
                for r0, r1 in ((32, 64), (96, 128)):
                    last = v.tensor_tensor(pk.ap()[r0:r1, 0:4128],
                                           v4.ap()[r0:r1, 0:4128],
                                           v1.ap()[r0:r1, 0:4128], ALU.mult)
                return last
            st_q = ch.add("vector", eQ,
                          deps=[ch.steps[-1]] + ([st_prev_po] if hf else []),
                          tag=f"tBQ.{hf}")
            st_qo = ch.add("sync", (lambda s, lo=lo: [
                s.dma_start(out=p.q_hbm[:, lo:lo + 4128], in_=pk[:, 0:4128])]),
                deps=[st_q], ndma=1, tag=f"tBQo.{hf}")

            def eAB(v):
                v.tensor_tensor(v4[sl], v2[sl], v0[sl], ALU.mult)
                return v.tensor_tensor(v2[sl], v2[sl], v1[sl], ALU.mult)
            st_ab = ch.add("vector", eAB, deps=[st_qo, st_radp], tag=f"tBAB.{hf}")

            def eCw(v, hf=hf):
                cs = slice(hf * 32, hf * 32 + 32)
                zz = slice(0, 32)
                A1 = v4.ap()[:, 0:4128].rearrange("p (h x) -> p h x", x=129)[:, :, 1]
                B1 = v2.ap()[:, 0:4128].rearrange("p (h x) -> p h x", x=129)[:, :, 1]
                ch.gap(v, v.tensor_copy(cw0[:, zz], A1), 'vector')
                ch.gap(v, v.tensor_copy(cw1[:, zz], B1), 'vector')
                ch.gap(v, v.tensor_single_scalar(cwa[:, zz], cw0[:, zz], 0.0, ALU.mult), 'vector')
                ch.gap(v, v.tensor_single_scalar(cwb[:, zz], cwa[:, zz], wimcol[:, 0:1], ALU.add), 'vector')
                ch.gap(v, v.tensor_single_scalar(cwa[:, zz], cwa[:, zz], wrecol[:, 0:1], ALU.add), 'vector')
                return v.tensor_single_scalar(cw0[:, zz], cw0[:, zz], 1.0,
                                              ALU.subtract)
            ch.add("vector", eCw, deps=[st_ab, st_wren], sync_all=True,
                   tag=f"tBcw.{hf}")

            def eCw2(v, hf=hf):
                cs = slice(hf * 32, hf * 32 + 32)
                zz = slice(0, 32)
                ch.gap(v, v.tensor_tensor(cw2[:, zz], cwa[:, zz], cwa[:, zz], ALU.mult), 'vector')
                ch.gap(v, v.tensor_tensor(cwre[:, zz], cwb[:, zz], cwb[:, zz], ALU.mult), 'vector')
                ch.gap(v, v.tensor_tensor(cw2[:, zz], cw2[:, zz], cwre[:, zz], ALU.add), 'vector')
                ch.gap(v, v.reciprocal(cw2[:, zz], cw2[:, zz]), 'vector')
                ch.gap(v, v.tensor_tensor(cwre[:, zz], cre_sb[:, cs], cw0[:, zz], ALU.mult), 'vector')
                ch.gap(v, v.tensor_tensor(dbg0[:, zz], cim_sb[:, cs], cw1[:, zz], ALU.mult), 'vector')
                ch.gap(v, v.tensor_tensor(cwre[:, zz], cwre[:, zz], dbg0[:, zz], ALU.subtract), 'vector')
                ch.gap(v, v.tensor_tensor(cwim[:, zz], cre_sb[:, cs], cw1[:, zz], ALU.mult), 'vector')
                ch.gap(v, v.tensor_tensor(dbg0[:, zz], cim_sb[:, cs], cw0[:, zz], ALU.mult), 'vector')
                ch.gap(v, v.tensor_tensor(cwim[:, zz], cwim[:, zz], dbg0[:, zz], ALU.add), 'vector')
                ch.gap(v, v.tensor_tensor(cw0[:, zz], cwre[:, zz], cwa[:, zz], ALU.mult), 'vector')
                ch.gap(v, v.tensor_tensor(dbg0[:, zz], cwim[:, zz], cwb[:, zz], ALU.mult), 'vector')
                ch.gap(v, v.tensor_tensor(cw0[:, zz], cw0[:, zz], dbg0[:, zz], ALU.add), 'vector')
                ch.gap(v, v.tensor_tensor(cw0[:, zz], cw0[:, zz], cw2[:, zz], ALU.mult), 'vector')
                ch.gap(v, v.tensor_tensor(cw1[:, zz], cwim[:, zz], cwa[:, zz], ALU.mult), 'vector')
                ch.gap(v, v.tensor_tensor(dbg0[:, zz], cwre[:, zz], cwb[:, zz], ALU.mult), 'vector')
                ch.gap(v, v.tensor_tensor(cw1[:, zz], cw1[:, zz], dbg0[:, zz], ALU.subtract), 'vector')
                ch.gap(v, v.tensor_tensor(cw1[:, zz], cw1[:, zz], cw2[:, zz], ALU.mult), 'vector')
                ch.gap(v, v.tensor_copy(cwre[:, zz], cw0[:, zz]), 'vector')
                ch.gap(v, v.tensor_copy(cwim[:, zz], cw1[:, zz]), 'vector')
                ch.gap(v, v.tensor_single_scalar(cw0[:, zz], cwre[:, zz], -1.0, ALU.mult), 'vector')
                return v.tensor_single_scalar(cw1[:, zz], cwim[:, zz], -1.0,
                                              ALU.mult)
            ch.add("vector", eCw2, deps="prev", sync_all=True, tag=f"tBcw2.{hf}")

            def ePrep(v):
                d0 = v0.ap()[:, 0:4128].rearrange("p (b x) -> p b x", x=129)
                d1 = v1.ap()[:, 0:4128].rearrange("p (b x) -> p b x", x=129)
                v.tensor_copy(d0[:, :, 0], cwre.ap()[:, 0:32])
                i1 = v.tensor_copy(d1[:, :, 0], cwim.ap()[:, 0:32])
                ch.gap(v, i1, "vector")
                k = 1
                while k < 129:
                    m = min(k, 129 - k)
                    v.tensor_copy(d0[:, :, k:k + m], d0[:, :, 0:m])
                    last = v.tensor_copy(d1[:, :, k:k + m], d1[:, :, 0:m])
                    ch.gap(v, last, "vector")
                    k *= 2
                return v.tensor_copy(d0[:, :, 0], d0[:, :, 0])
            ch.add("vector", ePrep, deps="prev", sync_all=True, tag=f"tBPr.{hf}")

            def eP(v):
                last = None
                for r0, r1 in ((0, 32), (64, 96)):
                    rr = slice(r0, r1)
                    xs = slice(0, 4128)
                    v.tensor_tensor(v3.ap()[rr, xs], v0.ap()[rr, xs],
                                    v4.ap()[rr, xs], ALU.mult)
                    v.tensor_tensor(pk.ap()[rr, xs], v1.ap()[rr, xs],
                                    v2.ap()[rr, xs], ALU.mult)
                    v.tensor_tensor(pk.ap()[rr, xs], v3.ap()[rr, xs],
                                    pk.ap()[rr, xs], ALU.subtract)
                for r0, r1 in ((32, 64), (96, 128)):
                    rr = slice(r0, r1)
                    xs = slice(0, 4128)
                    v.tensor_tensor(v3.ap()[rr, xs], v0.ap()[rr, xs],
                                    v2.ap()[rr, xs], ALU.mult)
                    v.tensor_tensor(pk.ap()[rr, xs], v1.ap()[rr, xs],
                                    v4.ap()[rr, xs], ALU.mult)
                    last = v.tensor_tensor(pk.ap()[rr, xs], v3.ap()[rr, xs],
                                           pk.ap()[rr, xs], ALU.add)
                return last
            st_p = ch.add("vector", eP, deps="prev", sync_all=True,
                          tag=f"tBP.{hf}")
            st_prev_p = st_p
            st_prev_po = ch.add("sync", (lambda s, lo=lo: [
                s.dma_start(out=p.p_hbm[:, lo:lo + 4128], in_=pk[:, 0:4128])]),
                deps=[st_p], ndma=1, tag=f"tBPo.{hf}")

        if upto == "s0":
            _finish_dump([
                (p.dbg_b.ap()[:, 0:8192], tabA.ap()[:, :]),
                (p.dbg_b.ap()[:, 8192:16448], p.p_hbm.ap()[:, :]),
                (p.dbg_b.ap()[:, 16448:24704], p.q_hbm.ap()[:, :]),
                (p.dbg_f.ap()[:, 0:4128], v4.ap()[:, 0:4128]),
                (p.dbg_f.ap()[:, 4128:8256], v2.ap()[:, 0:4128]),
                (p.dbg_f.ap()[:, 8256:8288], cwre.ap()[:, 0:32]),
                (p.dbg_f.ap()[:, 8320:8352], cwim.ap()[:, 0:32]),
                (p.dbg_f.ap()[:, 8384:8448], cre_sb.ap()[:, :]),
                (p.dbg_f.ap()[:, 8448:12576], v0.ap()[:, 0:4128]),
                (p.dbg_f.ap()[:, 12576:16704], v1.ap()[:, 0:4128]),
                (p.dbg_f.ap()[:, 16704:20832], v3.ap()[:, 0:4128]),
            ])
            ch.build_section(blk0)
            return nc
        with nc.named_scope("s0"):
            ch.build_section(blk0)

    ch.section_barrier()
    st_tabA_done = ch.steps[-1]
    with contextlib.ExitStack() as _ctx:
        ubf = _ctx.enter_context(nc.sbuf_tensor("ubf", [128, 3 * 1024], BF16))
        uf32 = _ctx.enter_context(nc.sbuf_tensor("uf32", [128, 2048], F32))
        zstg = _ctx.enter_context(nc.sbuf_tensor("zstg", [64, 2 * 1024], F32))
        zp0 = _ctx.enter_context(nc.psum_tensor([64, 1024], F32))
        zp1 = _ctx.enter_context(nc.psum_tensor([64, 1024], F32))
        blk1 = _ctx.enter_context(nc.Block())

        zps = (zp0, zp1)
        st_u, st_mm, st_ev, st_zo = {}, {}, {}, {}
        for g in range(32):
            bsl = slice((g % 3) * 1024, (g % 3) * 1024 + 1024)
            st_uf = ch.add("sync", (lambda sy, g=g: [
                sy.dma_start(out=uf32[:, (g % 2) * 1024:(g % 2) * 1024 + 1024],
                             in_=p.u_conv[:, g * 1024:(g + 1) * 1024])]),
                deps=[st_u.get(g - 2)], ndma=1, tag=f"u1f.{g}")
            st_u[g] = ch.add("vector", (lambda v, g=g, bsl=bsl: v.tensor_copy(
                ubf[:, bsl], uf32[:, (g % 2) * 1024:(g % 2) * 1024 + 1024])),
                deps=[st_uf, st_ev.get(g - 3)], tag=f"u1.{g}")
            zp = zps[g % 2]
            def emm(t, g=g, zp=zp):
                last = None
                for hh in range(4):
                    h = 4 * g + hh
                    base = (g % 3) * 1024 + hh * 256
                    last = t.matmul(
                        zp[:, hh * 256:(hh + 1) * 256],
                        tabA.ap()[:, h * 64:(h + 1) * 64],
                        ubf.ap()[:, base:base + 256],
                        start=True, stop=True)
                return last
            st_mm[g] = ch.add("tensor", emm,
                              deps=[st_u[g], st_tabA_done if g == 0 else None,
                                    st_ev.get(g - 2)],
                              tag=f"z_mm.{g}")
            ssl = slice((g % 2) * 1024, (g % 2) * 1024 + 1024)
            st_ev[g] = ch.add("scalar", (lambda s, zp=zp, ssl=ssl:
                              s.copy(zstg[:, ssl], zp[:, 0:1024])),
                              deps=[st_mm[g], st_zo.get(g - 2)], tag=f"z_ev.{g}")
            st_zo[g] = ch.add("sync", (lambda s, g=g, ssl=ssl: [
                s.dma_start(out=p.z_hbm[:, g * 1024:(g + 1) * 1024],
                            in_=zstg[:, ssl])]),
                deps=[st_ev[g]], ndma=1, tag=f"z_o.{g}")
        if upto == "s1":
            _finish_dump([(p.dbg_f.ap()[0:64, :], p.z_hbm.ap()[:, :])])
            ch.build_section(blk1)
            return nc
        with nc.named_scope("s1"):
            ch.build_section(blk1)
    stkA.close()

    ch.section_barrier()
    stkS = contextlib.ExitStack()
    smov = stkS.enter_context(nc.sbuf_tensor("smov", [64, HL * BC], BF16))
    with contextlib.ExitStack() as _ctx:
        zall = _ctx.enter_context(nc.sbuf_tensor("zall", [128, 16384], F32))
        ewstg = _ctx.enter_context(nc.sbuf_tensor("ewstg", [128, 4096], F32))
        ewcst = _ctx.enter_context(nc.sbuf_tensor("ewcst", [128, 4096],
                                                  BF16))
        s0t = _ctx.enter_context(nc.sbuf_tensor("s0", [128, 256], F32))
        s1t = _ctx.enter_context(nc.sbuf_tensor("s1", [128, 256], F32))
        rt0 = _ctx.enter_context(nc.sbuf_tensor("rt0", [128, 128], F32))
        rt1 = _ctx.enter_context(nc.sbuf_tensor("rt1", [128, 128], F32))
        rt2 = _ctx.enter_context(nc.sbuf_tensor("rt2", [128, 128], F32))
        tp0 = _ctx.enter_context(nc.psum_tensor([64, 512], F32))
        tp1 = _ctx.enter_context(nc.psum_tensor([64, 512], F32))
        blk2 = _ctx.enter_context(nc.Block())

        zlast = st_zo[31]
        zdeps = [st_zo[g] for g in range(26, 32)]
        st_zrb = []
        for rq in range(4):
            def erb(s, rq=rq):
                dms = []
                for rr in range(16):
                    r = rq * 16 + rr
                    src = p.z_hbm.ap().rearrange(
                        "o (h b c) -> o h b c", h=128, b=4, c=64)[r]
                    dstap = zall.ap().rearrange("h (b r2 c) -> h b r2 c",
                                                b=4, r2=64)[:, :, r, :]
                    dms.append(s.dma_start(out=dstap, in_=src))
                return dms
            st_zrb.append(ch.add("sync", erb, deps=zdeps, ndma=16,
                                 tag=f"zrb.{rq}"))

        def ezero(v):
            ap = smov.ap().rearrange("r (h b c) -> r h b c", h=128, b=4)[
                :, :, :, 0]
            return v.memset(ap, 0.0)
        st_sz = ch.add("vector", ezero, deps=[], tag="smov_zero")

        def replane(t, r):
            return t.ap().rearrange("h (b r2 n) -> h b r2 n", b=4, r2=2)[
                :, :, r, :]
        def zplane(c, r):
            return zall.ap().rearrange(
                "h (b rr n c) -> h b rr n c", b=4, rr=2, n=32)[:, :, r, :, c]
        def aplane(t):
            return t.ap().rearrange("h (b n) -> h b n", b=4)

        st_vec, st_tr, st_cp = {}, {}, {}
        st_init = ch.add("vector", lambda v: v.memset(s1t[:], 0.0),
                         deps=[], tag="s_init")
        for c in range(NCHUNK):
            cur = (s0t, s1t)[c % 2]
            prv = (s0t, s1t)[1 - c % 2]
            def erec(v, c=c, cur=cur, prv=prv):
                pre, pim = replane(prv, 0), replane(prv, 1)
                cre, cim = replane(cur, 0), replane(cur, 1)
                ar, ai = aplane(a_re), aplane(a_im)
                t0, t1, t2 = (rt0.ap().rearrange("h (b n) -> h b n", b=4),
                              rt1.ap().rearrange("h (b n) -> h b n", b=4),
                              rt2.ap().rearrange("h (b n) -> h b n", b=4))
                i = v.tensor_tensor(t0, pre, ar, ALU.mult)
                i = v.tensor_tensor(t1, pim, ai, ALU.mult)
                ch.gap(v, i, "vector")
                i = v.tensor_tensor(t0, t0, t1, ALU.add)
                i = v.tensor_tensor(t1, pim, ar, ALU.mult)
                i = v.tensor_tensor(t2, pre, ai, ALU.mult)
                ch.gap(v, i, "vector")
                i = v.tensor_tensor(t1, t1, t2, ALU.subtract)
                ch.gap(v, i, "vector")
                v.tensor_tensor(cre, t0, zplane(c, 0), ALU.add)
                return v.tensor_tensor(cim, t1, zplane(c, 1), ALU.add)
            deps = [st_init if c == 0 else st_vec[c - 1]]
            if c == 0:
                deps += st_zrb
            if c >= 2:
                deps.append(st_tr.get(c - 2))
            st_vec[c] = ch.add("vector", erec, deps=deps, sync_all=True,
                               tag=f"rec.{c}")

            if c < NCHUNK - 1:
                tp = (tp0, tp1)[c % 2]
                def etr(t, cur=cur, tp=tp):
                    last = None
                    for b in range(4):
                        last = t.transpose(tp[:, b * 128:(b + 1) * 128],
                                           cur.ap()[:, b * 64:(b + 1) * 64],
                                           eye_f[:])
                    return last
                st_tr[c] = ch.add("tensor", etr,
                                  deps=[st_vec[c], st_cp.get(c - 2)],
                                  tag=f"tr.{c}")
                def ecp(s, c=c, tp=tp):
                    last = None
                    for b in range(4):
                        dst = smov.ap().rearrange("r (h b2 c2) -> r h b2 c2",
                                                  h=128, b2=4)[:, :, b, c + 1]
                        last = s.copy(dst, tp[:, b * 128:(b + 1) * 128])
                    return last
                st_cp[c] = ch.add("scalar", ecp, deps=[st_tr[c], st_sz],
                                  tag=f"cp.{c}")

        st_ewo = {}
        ewblocks = ([(p.ew1_shard, p.cc_e1in, rb, 2048) for rb in range(6)]
                    + [(p.ew2_shard, p.cc_e2in, rb, 1024) for rb in range(12)])
        for k, (esrc, edst, rb, w) in enumerate(ewblocks):
            ssl = slice((k % 2) * 2048, (k % 2) * 2048 + w)
            sti = ch.add("sync", (lambda s, esrc=esrc, rb=rb, ssl=ssl: [
                s.dma_start(out=ewstg[:, ssl],
                            in_=esrc.ap()[rb * 128:(rb + 1) * 128, :])]),
                deps=[st_ewo.get(k - 2)], ndma=1, tag=f"ewi.{k}")
            stc = ch.add("scalar", (lambda s, ssl=ssl: s.copy(
                ewcst[:, ssl], ewstg[:, ssl])),
                deps=[sti, st_ewo.get(k - 2)], tag=f"ewc.{k}")
            st_ewo[k] = ch.add("sync", (lambda s, edst=edst, rb=rb, ssl=ssl: [
                s.dma_start(out=edst.ap()[rb * 128:(rb + 1) * 128, :],
                            in_=ewcst[:, ssl])]), deps=[stc], ndma=1,
                tag=f"ewo.{k}")

        def eag1(gp):
            return gp.collective_compute(
                "AllGather", ALU.bypass, replica_groups=[list(range(NCORE))],
                ins=[p.cc_e1in.ap().opt()], outs=[p.cc_e1out.ap().opt()])
        ch.add("gpsimd", eag1, deps=[st_ewo[k] for k in range(6)],
               tag="ew1ag")

        def eag2(gp):
            return gp.collective_compute(
                "AllGather", ALU.bypass, replica_groups=[list(range(NCORE))],
                ins=[p.cc_e2in.ap().opt()], outs=[p.cc_e2out.ap().opt()])
        ch.add("gpsimd", eag2, deps=[st_ewo[k] for k in range(6, 18)],
               tag="ew2ag")

        if upto == "s2":
            _finish_dump([(p.dbg_b.ap()[0:64, 0:32768], smov.ap()[:, :])])
            ch.build_section(blk2)
            return nc
        with nc.named_scope("s2"):
            ch.build_section(blk2)

    ch.section_barrier()
    last_cp = st_cp[NCHUNK - 2]
    last_vec = st_vec[NCHUNK - 1]
    with contextlib.ExitStack() as _ctx:
        punp = _ctx.enter_context(nc.sbuf_tensor("punp", [64, 16512], BF16))
        qunp = _ctx.enter_context(nc.sbuf_tensor("qunp", [64, 16512], BF16))
        gsb = _ctx.enter_context(nc.sbuf_tensor("gsb", [128, 16384], BF16))
        ubf2 = _ctx.enter_context(nc.sbuf_tensor("ubf2", [128, 3 * 1024], BF16))
        uf32b = _ctx.enter_context(nc.sbuf_tensor("uf32b", [128, 2048], F32))
        ktb = _ctx.enter_context(nc.sbuf_tensor("kt", [128, 2 * 128], BF16))
        ked = _ctx.enter_context(nc.sbuf_tensor("ked", [128, 128], BF16))
        ktdbg = _ctx.enter_context(nc.sbuf_tensor("ktdbg", [128, 256], BF16))
        kp0 = _ctx.enter_context(nc.psum_tensor([128, 512], F32))
        kp1 = _ctx.enter_context(nc.psum_tensor([128, 512], F32))
        cv0 = _ctx.enter_context(nc.psum_tensor([128, 1024], F32))
        cv1 = _ctx.enter_context(nc.psum_tensor([128, 1024], F32))
        blk3 = _ctx.enter_context(nc.Block())

        st_pq = ch.add("sync", lambda s: [
            s.dma_start(
                out=punp.ap().rearrange("r (h2 x) -> r h2 x", x=129)[:, 0::2, :],
                in_=p.p_hbm.ap()[0:64, :].rearrange("r (h2 x) -> r h2 x", x=129)),
            s.dma_start(
                out=punp.ap().rearrange("r (h2 x) -> r h2 x", x=129)[:, 1::2, :],
                in_=p.p_hbm.ap()[64:128, :].rearrange("r (h2 x) -> r h2 x",
                                                      x=129)),
            s.dma_start(
                out=qunp.ap().rearrange("r (h2 x) -> r h2 x", x=129)[:, 0::2, :],
                in_=p.q_hbm.ap()[0:64, :].rearrange("r (h2 x) -> r h2 x", x=129)),
            s.dma_start(
                out=qunp.ap().rearrange("r (h2 x) -> r h2 x", x=129)[:, 1::2, :],
                in_=p.q_hbm.ap()[64:128, :].rearrange("r (h2 x) -> r h2 x",
                                                      x=129)),
        ], deps=ch.frontier(), ndma=4, tag="pq_in")

        kps = (kp0, kp1)
        cvs = (cv0, cv1)
        st_u2, st_kg, st_msk, st_cv, st_gv = {}, {}, {}, {}, {}
        glast_flush = []
        for g in range(32):
            bsl = slice((g % 3) * 1024, (g % 3) * 1024 + 1024)
            st_uf2 = ch.add("sync", (lambda sy, g=g: [
                sy.dma_start(out=uf32b[:, (g % 2) * 1024:(g % 2) * 1024 + 1024],
                             in_=p.u_conv[:, g * 1024:(g + 1) * 1024])]),
                deps=[st_u2.get(g - 2)], ndma=1, tag=f"u2f.{g}")
            st_u2[g] = ch.add("vector", (lambda v, g=g, bsl=bsl: v.tensor_copy(
                ubf2[:, bsl], uf32b[:, (g % 2) * 1024:(g % 2) * 1024 + 1024])),
                deps=[st_uf2, st_gv.get(g - 3)], tag=f"u2.{g}")
            kp = kps[g % 2]
            def ekg(t, g=g, kp=kp):
                last = None
                for hh in range(4):
                    h = 4 * g + hh
                    last = t.matmul(
                        kp[:, hh * 128:(hh + 1) * 128],
                        qunp.ap()[:, h * 129:h * 129 + 128],
                        punp.ap()[:, h * 129:h * 129 + 128],
                        start=True, stop=True)
                return last
            st_kg[g] = ch.add("tensor", ekg,
                              deps=[st_pq if g == 0 else None,
                                    st_msk.get(g - 2)],
                              tag=f"kg.{g}")
            cv = cvs[g % 2]
            prev_cv_consumer = st_gv.get(g - 2)
            substeps = []
            for hh in range(4):
                def emsk1(v, g=g, hh=hh, kp=kp):
                    h = 4 * g + hh
                    kslot = ktb.ap()[:, (hh % 2) * 128:(hh % 2) * 128 + 128]
                    v.tensor_tensor(kslot, kp[:, hh * 128:(hh + 1) * 128],
                                    triu_f[:], ALU.mult)
                    ch.gap(v, v.tensor_single_scalar(ked[:], eye_b[:],
                                                     drep[:, h:h + 1], ALU.mult),
                           "vector")
                    i = v.tensor_tensor(kslot, kslot, ked[:], ALU.add)
                    ch.gap(v, i, "vector")
                    return v.tensor_copy(ked[0:1, 0:1], kslot[0:1, 0:1])
                dmsk = [st_kg[g]]
                if substeps:
                    dmsk.append(substeps[-1])
                if hh >= 2:
                    dmsk.append(substeps[2 * (hh - 2) + 1])
                stm = ch.add("vector", emsk1, deps=dmsk, tag=f"msk.{g}.{hh}")
                if upto == "s3" and hh == 0 and g in (0, 8):
                    def estash(v, g=g):
                        return v.tensor_copy(
                            ktdbg.ap()[:, (0 if g == 0 else 128):(128 if g == 0 else 256)],
                            ktb.ap()[:, 0:128])
                    stm = ch.add("vector", estash, deps=[stm], sync_all=True,
                                 tag=f"ktstash.{g}")
                def ecv1(t, g=g, hh=hh, cv=cv):
                    h = 4 * g + hh
                    kslot = ktb.ap()[:, (hh % 2) * 128:(hh % 2) * 128 + 128]
                    csl = cv[:, (hh % 2) * 512:(hh % 2) * 512 + 256]
                    base = (g % 3) * 1024 + hh * 256
                    if variant == "noaout":
                        return t.matmul(csl, kslot,
                                        ubf2.ap()[:, base:base + 256],
                                        start=True, stop=True)
                    if variant == "uonly":
                        return t.matmul(csl, eye_b[:],
                                        ubf2.ap()[:, base:base + 256],
                                        start=True, stop=True)
                    t.matmul(csl, kslot, ubf2.ap()[:, base:base + 256],
                             start=True, stop=False)
                    return t.matmul(
                        csl, punp.ap()[:, h * 129 + 1:h * 129 + 129],
                        smov.ap()[:, h * 256:(h + 1) * 256],
                        start=False, stop=True)
                dcv = [stm, st_u2[g], last_cp if g == 0 else None,
                       last_vec if g == 0 else None]
                if hh >= 2:
                    dcv.append(st_gv[(g, hh - 2)])
                elif prev_cv_consumer is not None:
                    dcv.append(prev_cv_consumer)
                stc = ch.add("tensor", ecv1, deps=dcv, tag=f"cv.{g}.{hh}")
                def egv1(s, g=g, hh=hh, cv=cv):
                    h = 4 * g + hh
                    csl = cv[:, (hh % 2) * 512:(hh % 2) * 512 + 256]
                    if variant == "uonly":
                        dst0 = gsb.ap().rearrange("t (bc hl) -> t bc hl",
                                                  hl=64)[:, :, h % 64]
                        return s.activation(dst0, csl, AF.Copy)
                    dst = gsb.ap().rearrange("t (bc hl) -> t bc hl", hl=64)[
                        :, :, h % 64]
                    return s.activation(dst, csl, AF.Gelu)
                stg = ch.add("scalar", egv1, deps=[stc], tag=f"gv.{g}.{hh}")
                st_gv[(g, hh)] = stg
                substeps += [stm, stc]
                st_msk[g] = stm
            st_gv[g] = st_gv[(g, 3)]
            if g in (15, 31):
                half = 0 if g == 15 else 1
                def efl(s, half=half):
                    src = gsb.ap().rearrange("t (j c2 hl) -> t j c2 hl",
                                             j=8, c2=32)
                    dst = p.cc_a2in.ap().rearrange(
                        "(j t) (c2 hl) -> j t c2 hl", t=128, hl=128)[
                        :, :, :, half * 64:half * 64 + 64]
                    return [s.dma_start(out=dst[j], in_=src[:, j])
                            for j in range(8)]
                fdeps = [st_gv[(gg, hhh)] for gg in range(half * 16, g + 1)
                         for hhh in range(4)]
                glast_flush.append(ch.add("sync", efl, deps=fdeps[-8:] + [
                    st_gv[(half * 16, 0)]], ndma=8, tag=f"gflush.{half}"))
                glast_flush[-1].deps = [st_gv[(g, 3)]]

        if upto == "s3":
            def edpq(gp):
                return [gp.dma_start(out=p.dbg_f.ap()[0:64, 0:16512],
                                     in_=punp.ap()[:, :]),
                        gp.dma_start(out=p.dbg_f.ap()[64:128, 0:16512],
                                     in_=qunp.ap()[:, :]),
                        gp.dma_start(out=p.dbg_f.ap()[:, 20000:20256],
                                     in_=ktdbg.ap()[:, :])]
            ch.add("gpsimd", edpq, deps=ch.frontier(), ndma=3, tag="dpq")
            _finish_dump([(p.dbg_b.ap()[:, 0:32768], p.cc_gin.ap()[:, :])])
            ch.build_section(blk3)
            return nc

        def eag(gp):
            with aot_env.allow_ici_all_to_all():
                return gp.collective_compute(
                    "AllToAll", ALU.bypass,
                    replica_groups=[list(range(NCORE))],
                    ins=[p.cc_a2in.ap().opt()], outs=[p.cc_a2out.ap().opt()])
        st_ag = ch.add("gpsimd", eag, deps=glast_flush, tag="g_a2a")
        p.st_ag = st_ag

        if phase1_only:
            def edbg(s):
                return [s.dma_start(out=p.dbg_g[:, :], in_=p.cc_a2out[:, :])]
            st_dbg = ch.add("sync", edbg, deps=[st_ag], ndma=1, tag="dbg_g")
            ch.add("gpsimd", lambda gp: gp.engine_nop(), deps=[st_dbg],
                   tag="fin")
        with nc.named_scope("s3"):
            ch.build_section(blk3)
    stkS.close()

    if phase1_only:
        return nc

    ch.section_barrier()
    stkG = contextlib.ExitStack()
    g_mov = stkG.enter_context(nc.sbuf_tensor("g_mov", [128, 32768], BF16))
    graw = stkG.enter_context(nc.sbuf_tensor("graw", [128, 8192], BF16))
    wT_sb = stkG.enter_context(nc.sbuf_tensor("wT_sb", [128, 8192], BF16))
    wstg = stkG.enter_context(nc.sbuf_tensor("wstg", [128, 2048], F32))
    with contextlib.ExitStack() as _ctx:
        ycb = _ctx.enter_context(nc.sbuf_tensor("ycb", [128, 8192], BF16))
        xcb = _ctx.enter_context(nc.sbuf_tensor("xcb", [128, 8192], BF16))
        ostg = _ctx.enter_context(nc.sbuf_tensor("ostg", [128, 4 * 512], BF16))
        gstg = _ctx.enter_context(nc.sbuf_tensor("gstg", [128, 2 * 1024], F32))
        cstg = _ctx.enter_context(nc.sbuf_tensor("cstg", [128, 2 * 1024], BF16))
        cchk = _ctx.enter_context(nc.sbuf_tensor("cchk", [128, 2 * 2048], BF16))
        schk = _ctx.enter_context(nc.sbuf_tensor("schk", [128, 2 * 512], BF16))
        ceye = _ctx.enter_context(nc.sbuf_tensor("ceye", [128, 1536], BF16))
        pp0 = _ctx.enter_context(nc.psum_tensor([128, 512], F32))
        pp1 = _ctx.enter_context(nc.psum_tensor([128, 512], F32))
        pp2 = _ctx.enter_context(nc.psum_tensor([128, 512], F32))
        gtp0 = _ctx.enter_context(nc.psum_tensor([128, 512], BF16))
        gtp1 = _ctx.enter_context(nc.psum_tensor([128, 512], BF16))
        blk4 = _ctx.enter_context(nc.Block())

        st_wc = {}
        for hb in range(8):
            wsl = slice((hb % 2) * 1024, (hb % 2) * 1024 + 1024)
            sti = ch.add("sync", (lambda s, hb=hb, wsl=wsl: [
                s.dma_start(out=wstg[:, wsl],
                            in_=p.w_outT.ap()[hb * 128:(hb + 1) * 128, :])]),
                deps=[st_wc.get(hb - 2)], ndma=1, tag=f"w_i{hb}")
            st_wc[hb] = ch.add("scalar", (lambda s, hb=hb, wsl=wsl:
                               s.copy(wT_sb[:, hb * 1024:(hb + 1) * 1024],
                                      wstg[:, wsl])),
                               deps=[sti], tag=f"w_c{hb}")
        p.st_wT = st_wc[7]

        gtps = (gtp0, gtp1)
        st_gr, st_gtr, st_gev = {}, {}, {}
        for isl in range(8):
            gsl = slice((isl % 2) * 4096, (isl % 2) * 4096 + 4096)
            st_gr[isl] = ch.add("sync", (lambda s, isl=isl, gsl=gsl: [
                s.dma_start(out=graw[:, gsl],
                            in_=p.cc_a2out.ap()[
                                isl * 128:(isl + 1) * 128, 0:4096])]),
                deps=[p.st_ag, st_gev.get((isl - 2, 7))], ndma=1,
                tag=f"g_i{isl}")
            for g4 in range(8):
                gt = gtps[g4 % 2]

                def etr(t, isl=isl, g4=g4, gt=gt, gsl=gsl):
                    last = None
                    for j in range(4):
                        c2 = g4 * 4 + j
                        last = t.transpose(
                            gt[:, j * 128:(j + 1) * 128],
                            graw.ap()[:, gsl.start + c2 * 128:
                                      gsl.start + c2 * 128 + 128],
                            eye_b[:])
                    return last
                st_gtr[(isl, g4)] = ch.add(
                    "tensor", etr,
                    deps=[st_gr[isl], st_gev.get((isl, g4 - 2)),
                          st_gev.get((isl - 1, g4 + 6))],
                    tag=f"g_t{isl}.{g4}")
                st_gev[(isl, g4)] = ch.add(
                    "scalar", (lambda s, isl=isl, g4=g4, gt=gt:
                               s.copy(g_mov[:, isl * 4096 + g4 * 512:
                                            isl * 4096 + g4 * 512 + 512],
                                      gt[:, :])),
                    deps=[st_gtr[(isl, g4)]], tag=f"g_e{isl}.{g4}")
        p.g_mov_done = st_gev[(7, 7)]

        st_ce = ch.add("sync", lambda s: [
            s.dma_start(out=ceye[:], in_=p.ceye[:, :])], deps=[], ndma=1,
            tag="ceye")

        st_gc = {}
        for ori, (src, dst) in enumerate(
                ((p.gens_a, p.gA_hbm), (p.gens_at, p.gAt_hbm))):
            for cb in range(8):
                sl = slice((cb % 2) * 1024, (cb % 2) * 1024 + 1024)
                sti = ch.add("sync", (lambda s, src=src, cb=cb, sl=sl: [
                    s.dma_start(out=gstg[:, sl],
                                in_=src.ap()[cb * 128:(cb + 1) * 128, :])]),
                    deps=[st_gc.get((ori, cb - 2))], ndma=1,
                    tag=f"gin.{ori}.{cb}")
                stc = ch.add("scalar", (lambda s, sl=sl: s.activation(
                    cstg[:, sl], gstg[:, sl], AF.Copy)),
                    deps=[sti, st_gc.get((ori, cb - 2))], tag=f"gcst.{ori}.{cb}")
                st_gc[(ori, cb)] = ch.add("sync", (
                    lambda s, dst=dst, cb=cb, sl=sl: [
                        s.dma_start(out=dst.ap()[cb * 128:(cb + 1) * 128, :],
                                    in_=cstg[:, sl])]),
                    deps=[stc], ndma=1, tag=f"gout.{ori}.{cb}")
        st_gens = [st_gc[(0, 7)], st_gc[(1, 7)], st_gc[(0, 6)], st_gc[(1, 6)]]

        def rowmaj(t):
            return t.ap().rearrange("(dk q) n -> q dk n", q=128)

        def dbg3(lo):
            return p.dbg_b.ap()[:, lo:lo + 8192].rearrange(
                "q (dk n) -> q dk n", n=1024)

        prev_ev = []

        def emit_product(xh, yh, oh, addh=None, deps=(), tag=""):
            def apx(t):
                return t.ap() if callable(getattr(t, "ap", None)) else t
            xh, yh = apx(xh), apx(yh)
            entry_prev = list(prev_ev)
            st_y = ch.add("sync", (lambda s, yh=yh: [
                s.dma_start(out=ycb.ap().rearrange("q (dk n) -> q dk n",
                                                   n=1024),
                            in_=yh.rearrange("(dk q) n -> q dk n", q=128))]),
                deps=list(deps) + entry_prev, ndma=1, tag=tag + "_y")
            st_xf = ch.add("sync", (lambda s, xh=xh: [
                s.dma_start(out=xcb.ap().rearrange("q (dk m) -> q dk m",
                                                   m=1024),
                            in_=xh.rearrange("(dk q) m -> q dk m", q=128))]),
                deps=list(deps) + entry_prev, ndma=1, tag=tag + "_x")
            last_mm, st_ev, st_od = {}, {}, {}
            for ob in range(8):
                for nh in range(2):
                    gi = 2 * ob + nh
                    pp = (pp0, pp1)[gi % 2]
                    sta = None
                    asl = slice(nh * 512, nh * 512 + 512)
                    if addh is not None:
                        sta = ch.add("sync", (
                            lambda s, ob=ob, nh=nh, asl=asl: [
                                s.dma_start(
                                    out=schk[:, asl],
                                    in_=addh.ap()[ob * 128:(ob + 1) * 128,
                                                  nh * 512:(nh + 1) * 512])]),
                            deps=(list(deps) + [last_mm.get(2 * (ob - 1) + nh)]
                                  + (entry_prev if ob < 1 else [])),
                            ndma=1, tag=f"{tag}_a{ob}.{nh}")

                    def emm(t, ob=ob, nh=nh, pp=pp, asl=asl,
                            has_add=(addh is not None)):
                        last = None
                        for dk in range(8):
                            last = t.matmul(
                                pp[:, :],
                                xcb.ap()[:, dk * 1024 + ob * 128:
                                         dk * 1024 + ob * 128 + 128],
                                ycb.ap()[:, dk * 1024 + nh * 512:
                                         dk * 1024 + nh * 512 + 512],
                                start=(dk == 0),
                                stop=(dk == 7 and not has_add))
                        if has_add:
                            last = t.matmul(pp[:, :], ceye.ap()[:, 0:128],
                                            schk.ap()[:, asl],
                                            start=False, stop=True)
                        return last
                    mdeps = [st_y, st_xf, last_mm.get(gi - 2),
                             st_ev.get(gi - 2)]
                    if gi < 2:
                        mdeps += entry_prev
                    if sta is not None:
                        mdeps.append(sta)
                    stm = ch.add("tensor", emm, deps=mdeps,
                                 tag=f"{tag}_mm{ob}.{nh}")
                    last_mm[gi] = stm
                    osl = slice((gi % 4) * 512, (gi % 4) * 512 + 512)
                    ste = ch.add("scalar", (lambda s, pp=pp, osl=osl:
                                 s.copy(ostg[:, osl], pp[:, :])),
                                 deps=[stm, st_od.get(gi - 4)],
                                 tag=f"{tag}_ev{ob}.{nh}")
                    st_ev[gi] = ste
                    st_od[gi] = ch.add("sync", (
                        lambda s, ob=ob, nh=nh, osl=osl: [
                            s.dma_start(out=oh.ap()[ob * 128:(ob + 1) * 128,
                                                    nh * 512:(nh + 1) * 512],
                                        in_=ostg[:, osl])]),
                        deps=[ste], ndma=1, tag=f"{tag}_o{ob}.{nh}")
            prev_ev[:] = [st_ev[14], st_ev[15]]
            return [st_od[j] for j in range(16)]

        dA2 = emit_product(p.gAt_hbm, p.gA_hbm, p.exA2, deps=st_gens,
                           tag="pA2")
        dA2t = emit_product(p.gA_hbm, p.gAt_hbm, p.exA2t, deps=st_gens,
                            tag="pA2t")
        dA3 = emit_product(p.exA2t, p.gA_hbm, p.exA3, deps=dA2t, tag="pA3")
        dA4 = emit_product(p.exA2t, p.exA2, p.exA4, deps=dA2t + dA2,
                           tag="pA4")
        dA4t = emit_product(p.exA2, p.exA2t, p.exA4t, deps=dA2t + dA2,
                            tag="pA4t")

        combo_deps = dA2 + dA3 + dA4 + dA4t + st_gens
        st_cmb = {}
        srcs_chunks = (p.gA_hbm, p.exA2, p.exA3, p.exA4)
        for rb in range(8):
            for nh in range(2):
                ci = rb * 2 + nh
                csl = slice((ci % 2) * 2048, (ci % 2) * 2048 + 2048)
                st_ci = ch.add("sync", (lambda s, rb=rb, nh=nh, csl=csl: [
                    s.dma_start(out=cchk[:, csl.start + j * 512:
                                         csl.start + j * 512 + 512],
                                in_=srcs_chunks[j].ap()[
                                    rb * 128:(rb + 1) * 128,
                                    nh * 512:(nh + 1) * 512])
                    for j in range(4)]),
                    deps=combo_deps + [st_cmb.get(("m", ci - 2))], ndma=4,
                    tag=f"cmb_i{ci}")
                diag = (rb // 4 == nh)
                doff = (rb % 4) * 128

                def ecmm(t, csl=csl, diag=diag, doff=doff):
                    def A(j):
                        return cchk.ap()[:, csl.start + j * 512:
                                         csl.start + j * 512 + 512]

                    def ident(j):
                        return ceye.ap()[:, j * 128:(j + 1) * 128]
                    last = None
                    for pp, terms, icf in (
                            (pp0, ((0, 0), (1, 1), (2, 2)), 0),
                            (pp1, ((4, 0), (5, 1), (6, 2)), 3),
                            (pp2, ((8, 0), (9, 1), (10, 2), (11, 3)), 7)):
                        for j, (cf, aj) in enumerate(terms):
                            last = t.matmul(pp[:, :], ident(cf), A(aj),
                                            start=(j == 0),
                                            stop=(j == len(terms) - 1
                                                  and not diag))
                        if diag:
                            last = t.matmul(pp[:, doff:doff + 128],
                                            ident(icf), eye_b[:],
                                            start=False, stop=True)
                    return last
                mdeps = [st_ci, st_ce, st_cmb.get(("e", ci - 1))]
                if ci < 1:
                    mdeps += prev_ev
                stm = ch.add("tensor", ecmm, deps=mdeps, tag=f"cmb_m{ci}")
                st_cmb[("m", ci)] = stm

                def ecev(s, ci=ci):
                    s.copy(ostg[:, 0:512], pp0[:, :])
                    s.copy(ostg[:, 512:1024], pp1[:, :])
                    return s.copy(ostg[:, 1024:1536], pp2[:, :])
                ste = ch.add("scalar", ecev,
                             deps=[stm, st_cmb.get(("o", ci - 1))],
                             tag=f"cmb_e{ci}")
                st_cmb[("e", ci)] = ste
                sto = ch.add("sync", (lambda s, rb=rb, nh=nh: [
                    s.dma_start(out=dst.ap()[rb * 128:(rb + 1) * 128,
                                             nh * 512:(nh + 1) * 512],
                                in_=ostg[:, j * 512:(j + 1) * 512])
                    for j, dst in enumerate((p.exS0, p.exS1, p.exS2))]),
                    deps=[ste], ndma=3, tag=f"cmb_o{ci}")
                st_cmb[("o", ci)] = sto
        prev_ev[:] = [st_cmb[("e", 15)]]
        combo_done = [st_cmb[("o", ci)] for ci in range(16)]

        dW = emit_product(p.exA4t, p.exS2, p.exW, addh=p.exS1,
                          deps=combo_done + dA4t, tag="pW")
        dU = emit_product(p.exA4t, p.exW, p.cc_Ein, addh=p.exS0,
                          deps=dW + combo_done, tag="pU")

        def eeag(gp):
            return gp.collective_compute(
                "AllGather", ALU.bypass, replica_groups=[list(range(NCORE))],
                ins=[p.cc_Ein.ap().opt()], outs=[p.cc_Eout.ap().opt()])
        st_eag = ch.add("gpsimd", eeag, deps=dU + [p.st_ag], tag="e_ag")

        dP1 = emit_product(p.cc_Eout.ap()[1024:2048, :],
                           p.cc_Eout.ap()[0:1024, :], p.exP1,
                           deps=[st_eag], tag="pP1")
        dP2 = emit_product(p.cc_Eout.ap()[2048:3072, :],
                           p.cc_Eout.ap()[3072:4096, :], p.exP2,
                           deps=[st_eag], tag="pP2")
        dR = emit_product(p.exP1, p.exP2, p.exR, deps=dP1 + dP2, tag="pR")

        if upto == "s4":
            _finish_dump([
                (dbg3(0), rowmaj(p.exR)),
                (dbg3(8192),
                 p.cc_Eout.ap()[0:1024, :].rearrange("(dk q) n -> q dk n",
                                                     q=128)),
                (dbg3(16384), rowmaj(p.exA2)),
                (dbg3(24576), rowmaj(p.exS1)),
            ])
            ch.build_section(blk4)
            return nc
        with nc.named_scope("s4"):
            ch.build_section(blk4)

    ch.section_barrier()
    with contextlib.ExitStack() as _ctx:
        u_t = _ctx.enter_context(nc.sbuf_tensor("u_t", [128, 1024], F32))
        v_f = _ctx.enter_context(nc.sbuf_tensor("v_f", [128, 4096], F32))
        x1f = _ctx.enter_context(nc.sbuf_tensor("x1f", [128, 4096], F32))
        vb = _ctx.enter_context(nc.sbuf_tensor("vb", [128, 1024], BF16))
        sqt = _ctx.enter_context(nc.sbuf_tensor("sqt", [128, 1024], BF16))
        x1bf = _ctx.enter_context(nc.sbuf_tensor("x1bf", [128, 1024], BF16))
        xnf = _ctx.enter_context(nc.sbuf_tensor("xnf", [128, 1024], F32))
        xnbf = _ctx.enter_context(nc.sbuf_tensor("xnbf", [128, 1024], BF16))
        gw_sb = _ctx.enter_context(nc.sbuf_tensor("gw_sb", [128, 48], F32))
        rows1 = _ctx.enter_context(nc.sbuf_tensor("rows1", [1, 2048], F32))
        rows2 = _ctx.enter_context(nc.sbuf_tensor("rows2", [1, 2048], F32))
        rrow1 = _ctx.enter_context(nc.sbuf_tensor("rrow1", [1, 512], F32))
        rrow2 = _ctx.enter_context(nc.sbuf_tensor("rrow2", [1, 512], F32))
        lg_sb = _ctx.enter_context(nc.sbuf_tensor("lg_sb", [32, 512], F32))
        lgt = _ctx.enter_context(nc.sbuf_tensor("lgt", [128, 8], F32))
        mxt = _ctx.enter_context(nc.sbuf_tensor("mxt", [128, 8], F32))
        negm = _ctx.enter_context(nc.sbuf_tensor("negm", [128, 8], F32))
        ex6 = _ctx.enter_context(nc.sbuf_tensor("ex6", [128, 8], F32))
        exd = _ctx.enter_context(nc.sbuf_tensor("exd", [128, 8], F32))
        msk6 = _ctx.enter_context(nc.sbuf_tensor("msk6", [128, 8], F32))
        dn8 = _ctx.enter_context(nc.sbuf_tensor("dn8", [128, 8], F32))
        w6f = _ctx.enter_context(nc.sbuf_tensor("w6f", [128, 32], F32))
        w6row = _ctx.enter_context(nc.sbuf_tensor("w6row", [32, 4096], BF16))
        boutc = _ctx.enter_context(nc.sbuf_tensor("boutc", [128, 8], F32))
        lngc = _ctx.enter_context(nc.sbuf_tensor("lngc", [128, 8], F32))
        lnbc = _ctx.enter_context(nc.sbuf_tensor("lnbc", [128, 8], F32))
        gb_sb = _ctx.enter_context(nc.sbuf_tensor("gb_sb", [8, 1], F32))
        onesrf = _ctx.enter_context(nc.sbuf_tensor("onesrf", [1, 128], F32))
        onescb = _ctx.enter_context(nc.sbuf_tensor("onescb", [128, 1], BF16))
        at0 = _ctx.enter_context(nc.psum_tensor([128, 512], F32))
        at1 = _ctx.enter_context(nc.psum_tensor([128, 512], F32))
        murepP = _ctx.enter_context(nc.psum_tensor([128, 512], F32))
        rrepP = _ctx.enter_context(nc.psum_tensor([128, 512], F32))
        sS = _ctx.enter_context(nc.psum_tensor([1, 512], F32))
        sQ = _ctx.enter_context(nc.psum_tensor([1, 512], F32))
        lgP = _ctx.enter_context(nc.psum_tensor([8, 512], F32))
        tpk = _ctx.enter_context(nc.psum_tensor([128, 160], F32))
        blk5 = _ctx.enter_context(nc.Block())

        ats = (at0, at1)

        def prevref(d, tg, ob):
            if ob >= 2:
                return d.get((tg, ob - 2))
            return d.get((tg - 1, ob + 6))

        st_c5 = ch.add("sync", lambda s: [
            s.dma_start(out=boutc[:], in_=p.b_out_col[:, :]),
            s.dma_start(out=lngc[:], in_=p.ln_g_col[:, :]),
            s.dma_start(out=lnbc[:], in_=p.ln_b_col[:, :]),
            s.dma_start(out=gw_sb[:], in_=p.gw_cols[:, :]),
            s.dma_start(out=gb_sb[:], in_=p.gb_col[:, :]),
            s.dma_start(out=onesrf[:], in_=p.ones_rf[:, :]),
            s.dma_start(out=onescb[:], in_=p.ones_cb[:, :]),
        ], deps=[], ndma=7, tag="s5c")

        def einit(v):
            v.memset(lg_sb.ap()[0:32, :], 0.0)
            v.memset(w6f.ap()[:, 6:32], 0.0)
            return v.memset(lgt[:], -1.0e30)
        st_init5 = ch.add("vector", einit, deps=[], tag="s5init")

        st_wT = p.st_wT
        g_mov_done = p.g_mov_done

        st_at, st_v, st_vb, st_sq, st_st1 = {}, {}, {}, {}, {}
        st_x1, st_st2, st_xn, st_lgmm = {}, {}, {}, {}
        st_x1b_d, st_xnb_d, st_ut = {}, {}, {}
        st_rm1 = st_rm2 = st_bc1 = st_bc2 = None
        prev_rm2 = prev_xn_last = prev_lgev = None
        last_w6ev = None
        for tg in range(8):
            for ob in range(8):
                usl = slice((ob % 2) * 512, (ob % 2) * 512 + 512)
                st_ut[(tg, ob)] = ch.add("sync", (
                    lambda s, tg=tg, ob=ob, usl=usl: [
                        s.dma_start(out=u_t[:, usl],
                                    in_=p.u_tok.ap()[
                                        ob * 128:(ob + 1) * 128,
                                        tg * 512:(tg + 1) * 512])]),
                    deps=[prevref(st_v, tg, ob)],
                    ndma=1, tag=f"u5.{tg}.{ob}")
                at = ats[ob % 2]

                def eat(t, tg=tg, ob=ob, at=at):
                    last = None
                    for hb in range(8):
                        last = t.matmul(
                            at[:, :],
                            wT_sb.ap()[:, hb * 1024 + ob * 128:
                                       hb * 1024 + ob * 128 + 128],
                            g_mov.ap()[:, hb * 4096 + tg * 512:
                                       hb * 4096 + tg * 512 + 512],
                            start=(hb == 0), stop=(hb == 7))
                    return last
                adeps = [st_wT, g_mov_done, prevref(st_v, tg, ob)]
                st_at[(tg, ob)] = ch.add("tensor", eat, deps=adeps,
                                         tag=f"at.{tg}.{ob}")

                def ev(v, tg=tg, ob=ob, at=at, usl=usl):
                    return v.scalar_tensor_tensor(
                        v_f[:, ob * 512:(ob + 1) * 512], at[:, :],
                        boutc[:, ob:ob + 1], u_t[:, usl],
                        ALU.add, ALU.add)
                st_v[(tg, ob)] = ch.add(
                    "vector", ev, deps=[st_at[(tg, ob)], st_ut[(tg, ob)]],
                    tag=f"v.{tg}.{ob}")
                st_vb[(tg, ob)] = ch.add(
                    "vector", (lambda v, tg=tg, ob=ob, usl=usl: v.tensor_copy(
                        vb[:, usl], v_f[:, ob * 512:(ob + 1) * 512])),
                    deps=[st_v[(tg, ob)], prevref(st_st1, tg, ob)],
                    tag=f"vb.{tg}.{ob}")
                st_sq[(tg, ob)] = ch.add(
                    "scalar", (lambda s, tg=tg, ob=ob, usl=usl: s.activation(
                        sqt[:, usl], v_f[:, ob * 512:(ob + 1) * 512],
                        AF.Square)),
                    deps=[st_v[(tg, ob)], prevref(st_st1, tg, ob),
                          prevref(st_st2, tg, ob)],
                    tag=f"sq.{tg}.{ob}")

                def est1(t, tg=tg, ob=ob, usl=usl):
                    t.matmul(sS[:, :], onescb.ap()[:, 0:1], vb.ap()[:, usl],
                             start=(ob == 0), stop=(ob == 7))
                    return t.matmul(sQ[:, :], onescb.ap()[:, 0:1],
                                    sqt.ap()[:, usl],
                                    start=(ob == 0), stop=(ob == 7))
                sdeps = [st_vb[(tg, ob)], st_sq[(tg, ob)], st_c5]
                if ob == 0:
                    sdeps.append(prev_rm2)
                st_st1[(tg, ob)] = ch.add("tensor", est1, deps=sdeps,
                                          tag=f"st1.{tg}.{ob}")

            def erm(v, rows=rows1):
                r = rows.ap()
                i = v.tensor_single_scalar(r[0:1, 0:512], sS[0:1, :],
                                           1.0 / 1024, ALU.mult)
                i = v.tensor_single_scalar(r[0:1, 512:1024], sQ[0:1, :],
                                           1.0 / 1024, ALU.mult)
                ch.gap(v, i, "vector")
                i = v.tensor_tensor(r[0:1, 1024:1536], r[0:1, 0:512],
                                    r[0:1, 0:512], ALU.mult)
                ch.gap(v, i, "vector")
                return v.tensor_tensor(r[0:1, 1024:1536], r[0:1, 512:1024],
                                       r[0:1, 1024:1536], ALU.subtract)
            st_rm1 = ch.add("vector", erm, deps=[st_st1[(tg, 7)]],
                            sync_all=True, tag=f"rm1.{tg}")
            st_sd1 = ch.add("scalar", (lambda s, rows=rows1: s.activation(
                rows.ap()[0:1, 1536:2048], rows.ap()[0:1, 1024:1536],
                AF.Sqrt, bias=epsc.ap()[0:1, 0:1])),
                deps=[st_rm1], tag=f"sd1.{tg}")
            st_rc1 = ch.add("vector", (lambda v, rows=rows1: v.reciprocal(
                rrow1.ap()[0:1, :], rows.ap()[0:1, 1536:2048])),
                deps=[st_sd1], sync_all=True, tag=f"rc1.{tg}")

            def ebc(t, rows=rows1):
                t.matmul(murepP[:, :], onesrf.ap()[0:1, :],
                         rows.ap()[0:1, 0:512], start=True, stop=True)
                return t.matmul(rrepP[:, :], onesrf.ap()[0:1, :],
                                rrow1.ap()[0:1, :], start=True, stop=True)
            st_bc1 = ch.add("tensor", ebc,
                            deps=[st_rc1, st_rm1, prev_xn_last],
                            tag=f"bc1.{tg}")

            for ob in range(8):
                usl = slice((ob % 2) * 512, (ob % 2) * 512 + 512)

                def ex1(v, tg=tg, ob=ob, usl=usl):
                    vs = v_f[:, ob * 512:(ob + 1) * 512]
                    xs = x1f[:, ob * 512:(ob + 1) * 512]
                    i = v.tensor_tensor(xs, vs, murepP[:, :], ALU.subtract)
                    ch.gap(v, i, "vector")
                    i = v.tensor_tensor(xs, xs, rrepP[:, :], ALU.mult)
                    ch.gap(v, i, "vector")
                    return v.tensor_copy(x1bf[:, usl], xs)
                st_x1[(tg, ob)] = ch.add(
                    "vector", ex1,
                    deps=[st_bc1, st_st2.get((tg, ob - 2)),
                          prevref(st_x1b_d, tg, ob)],
                    sync_all=True, tag=f"x1.{tg}.{ob}")
                sq2deps = [st_x1[(tg, ob)], st_st2.get((tg, ob - 2))]
                if ob < 2:
                    sq2deps.append(st_st1.get((tg, ob + 6)))
                st_sq2 = ch.add(
                    "scalar", (lambda s, tg=tg, ob=ob, usl=usl: s.activation(
                        sqt[:, usl], x1f[:, ob * 512:(ob + 1) * 512],
                        AF.Square)),
                    deps=sq2deps, tag=f"sq2.{tg}.{ob}")

                def est2(t, tg=tg, ob=ob, usl=usl):
                    t.matmul(sS[:, :], onescb.ap()[:, 0:1], x1bf.ap()[:, usl],
                             start=(ob == 0), stop=(ob == 7))
                    return t.matmul(sQ[:, :], onescb.ap()[:, 0:1],
                                    sqt.ap()[:, usl],
                                    start=(ob == 0), stop=(ob == 7))
                s2deps = [st_x1[(tg, ob)], st_sq2]
                if ob == 0:
                    s2deps.append(st_rm1)
                st_st2[(tg, ob)] = ch.add("tensor", est2, deps=s2deps,
                                          tag=f"st2.{tg}.{ob}")
                st_x1b_d[(tg, ob)] = ch.add("sync", (
                    lambda s, tg=tg, ob=ob, usl=usl: [
                        s.dma_start(out=p.x1b.ap()[ob * 128:(ob + 1) * 128,
                                                   tg * 512:(tg + 1) * 512],
                                    in_=x1bf[:, usl])]),
                    deps=[st_x1[(tg, ob)]], ndma=1, tag=f"x1d.{tg}.{ob}")

            def erm2(v, rows=rows2):
                r = rows.ap()
                i = v.tensor_single_scalar(r[0:1, 0:512], sS[0:1, :],
                                           1.0 / 1024, ALU.mult)
                i = v.tensor_single_scalar(r[0:1, 512:1024], sQ[0:1, :],
                                           1.0 / 1024, ALU.mult)
                ch.gap(v, i, "vector")
                i = v.tensor_tensor(r[0:1, 1024:1536], r[0:1, 0:512],
                                    r[0:1, 0:512], ALU.mult)
                ch.gap(v, i, "vector")
                return v.tensor_tensor(r[0:1, 1024:1536], r[0:1, 512:1024],
                                       r[0:1, 1024:1536], ALU.subtract)
            st_rm2 = ch.add("vector", erm2, deps=[st_st2[(tg, 7)]],
                            sync_all=True, tag=f"rm2.{tg}")
            st_sd2 = ch.add("scalar", (lambda s, rows=rows2: s.activation(
                rows.ap()[0:1, 1536:2048], rows.ap()[0:1, 1024:1536],
                AF.Sqrt, bias=epsc.ap()[0:1, 0:1])),
                deps=[st_rm2], tag=f"sd2.{tg}")
            st_rc2 = ch.add("vector", (lambda v, rows=rows2: v.reciprocal(
                rrow2.ap()[0:1, :], rows.ap()[0:1, 1536:2048])),
                deps=[st_sd2], sync_all=True, tag=f"rc2.{tg}")

            def ebc2(t, rows=rows2):
                t.matmul(murepP[:, :], onesrf.ap()[0:1, :],
                         rows.ap()[0:1, 0:512], start=True, stop=True)
                return t.matmul(rrepP[:, :], onesrf.ap()[0:1, :],
                                rrow2.ap()[0:1, :], start=True, stop=True)
            st_bc2 = ch.add("tensor", ebc2,
                            deps=[st_rc2, st_rm2, st_x1[(tg, 7)]],
                            tag=f"bc2.{tg}")

            for ob in range(8):
                usl = slice((ob % 2) * 512, (ob % 2) * 512 + 512)

                def exn(v, tg=tg, ob=ob, usl=usl):
                    xs = x1f[:, ob * 512:(ob + 1) * 512]
                    i = v.tensor_tensor(xnf[:, usl], xs, murepP[:, :],
                                        ALU.subtract)
                    ch.gap(v, i, "vector")
                    i = v.tensor_tensor(xnf[:, usl], xnf[:, usl], rrepP[:, :],
                                        ALU.mult)
                    ch.gap(v, i, "vector")
                    i = v.tensor_scalar(xnf[:, usl], xnf[:, usl],
                                        lngc[:, ob:ob + 1], lnbc[:, ob:ob + 1],
                                        ALU.mult, ALU.add)
                    ch.gap(v, i, "vector")
                    return v.tensor_copy(xnbf[:, usl], xnf[:, usl])
                st_xn[(tg, ob)] = ch.add(
                    "vector", exn,
                    deps=[st_bc2, prevref(st_lgmm, tg, ob),
                          prevref(st_xnb_d, tg, ob)],
                    sync_all=True, tag=f"xn.{tg}.{ob}")

                def elg(t, tg=tg, ob=ob, usl=usl):
                    return t.matmul(lgP[0:6, :], gw_sb.ap()[:, ob * 6:
                                                            ob * 6 + 6],
                                    xnf.ap()[:, usl],
                                    start=(ob == 0), stop=(ob == 7))
                lgdeps = [st_xn[(tg, ob)]]
                if ob == 0:
                    lgdeps.append(prev_lgev)
                st_lgmm[(tg, ob)] = ch.add("tensor", elg, deps=lgdeps,
                                           tag=f"lg.{tg}.{ob}")
                st_xnb_d[(tg, ob)] = ch.add("sync", (
                    lambda s, tg=tg, ob=ob, usl=usl: [
                        s.dma_start(out=p.xnb.ap()[ob * 128:(ob + 1) * 128,
                                                   tg * 512:(tg + 1) * 512],
                                    in_=xnbf[:, usl])]),
                    deps=[st_xn[(tg, ob)]], ndma=1, tag=f"xnd.{tg}.{ob}")
            prev_xn_last = st_xn[(tg, 7)]
            prev_rm2 = st_rm2

            st_lgev = ch.add("vector", (lambda v: v.tensor_single_scalar(
                lg_sb.ap()[0:6, :], lgP[0:6, :],
                gb_sb.ap()[0:6, 0:1], ALU.add)),
                deps=[st_lgmm[(tg, 7)], st_c5, st_init5, last_w6ev],
                tag=f"lgev.{tg}")
            prev_lgev = st_lgev
            for tt in range(4):
                def etp(t, tt=tt):
                    return t.transpose(tpk[:, 0:32],
                                       lg_sb.ap()[0:32, tt * 128:
                                                  tt * 128 + 128],
                                       eye_f.ap()[0:32, 0:32])
                st_tp = ch.add("tensor", etp,
                               deps=[st_lgev, last_w6ev], tag=f"tp.{tg}.{tt}")

                def etk1(v):
                    i = v.tensor_copy(lgt[:, 0:6], tpk[:, 0:6])
                    ch.gap(v, i, "vector")
                    i = v.max(mxt[:], lgt[:])
                    ch.gap(v, i, "vector")
                    i = v.tensor_single_scalar(negm[:], mxt[:], -1.0,
                                               ALU.mult)
                    ch.gap(v, i, "vector")
                    return v.tensor_single_scalar(msk6[:, 0:6], lgt[:, 0:6],
                                                  mxt[:, 1:2], ALU.is_ge)
                st_tk1 = ch.add("vector", etk1, deps=[st_tp, st_init5],
                                sync_all=True, tag=f"tk1.{tg}.{tt}")

                def etk2(s):
                    s.activation(ex6[:, 0:6], lgt[:, 0:6], AF.Exp,
                                 bias=negm[:, 0:1])
                    return s.activation(exd[:], mxt[:], AF.Exp,
                                        bias=negm[:, 0:1])
                st_tk2 = ch.add("scalar", etk2, deps=[st_tk1],
                                tag=f"tk2.{tg}.{tt}")

                def etk3(v):
                    i = v.tensor_tensor(w6f[:, 0:6], msk6[:, 0:6],
                                        ex6[:, 0:6], ALU.mult)
                    i = v.tensor_single_scalar(dn8[:], exd[:], 1.0, ALU.add)
                    ch.gap(v, i, "vector")
                    i = v.reciprocal(dn8[:], dn8[:])
                    ch.gap(v, i, "vector")
                    return v.tensor_single_scalar(w6f[:, 0:6],
                                                  w6f[:, 0:6],
                                                  dn8[:, 1:2], ALU.mult)
                st_tk3 = ch.add("vector", etk3, deps=[st_tk2],
                                sync_all=True, tag=f"tk3.{tg}.{tt}")

                def etp2(t):
                    return t.transpose(tpk[0:32, 32:160], w6f[:], eye_f[:])
                st_tp2 = ch.add("tensor", etp2, deps=[st_tk3],
                                tag=f"tp2.{tg}.{tt}")

                def ew6(v, tg=tg, tt=tt):
                    return v.tensor_copy(
                        w6row.ap()[0:6, (tg * 4 + tt) * 128:
                                   (tg * 4 + tt) * 128 + 128],
                        tpk[0:6, 32:160])
                last_w6ev = ch.add("vector", ew6, deps=[st_tp2],
                                   tag=f"w6e.{tg}.{tt}")

        st_w6d = ch.add("sync", (lambda s: [
            s.dma_start(out=p.w6_hbm[:, :], in_=w6row[:])]),
            deps=[last_w6ev], ndma=1, tag="w6d")

        if upto == "s5":
            _finish_dump([
                (p.dbg_b.ap()[:, 0:32768].rearrange("q (dk n) -> q dk n",
                                                    n=4096),
                 p.xnb.ap().rearrange("(dk q) n -> q dk n", q=128)),
                (p.dbg_b.ap()[0:32, 32768:36864], p.w6_hbm.ap()[:, :]),
            ])
            ch.add("gpsimd", (lambda gp: [gp.dma_start(
                out=p.dbg_f.ap().rearrange("q (dk n) -> q dk n", n=4096),
                in_=p.x1b.ap().rearrange("(dk q) n -> q dk n", q=128))]),
                deps=ch.frontier(), ndma=1, tag="x1dump")
            ch.build_section(blk5)
            return nc
        with nc.named_scope("s5"):
            ch.build_section(blk5)
    stkG.close()

    ch.section_barrier()
    with contextlib.ExitStack() as _ctx:
        aw1_sb = _ctx.enter_context(nc.sbuf_tensor("aw1_sb", [128, 32768],
                                                   BF16))
        aw2_sb = _ctx.enter_context(nc.sbuf_tensor("aw2_sb", [128, 32768],
                                                   BF16))
        awstg = _ctx.enter_context(nc.sbuf_tensor("awstg", [128, 6144], F32))
        xn_tg = _ctx.enter_context(nc.sbuf_tensor("xn_tg", [128, 4096], BF16))
        a1_sb = _ctx.enter_context(nc.sbuf_tensor("a1_sb", [128, 16384],
                                                  BF16))
        abst = _ctx.enter_context(nc.sbuf_tensor("abst", [128, 1024], BF16))
        ab1c = _ctx.enter_context(nc.sbuf_tensor("ab1c", [128, 32], F32))
        ab2c = _ctx.enter_context(nc.sbuf_tensor("ab2c", [128, 8], F32))
        pr0 = _ctx.enter_context(nc.psum_tensor([128, 512], F32))
        pr1 = _ctx.enter_context(nc.psum_tensor([128, 512], F32))
        cn0 = _ctx.enter_context(nc.psum_tensor([128, 512], F32))
        cn1 = _ctx.enter_context(nc.psum_tensor([128, 512], F32))
        blk6 = _ctx.enter_context(nc.Block())

        st_c6 = ch.add("sync", lambda s: [
            s.dma_start(out=ab1c[:], in_=p.ab1_col[:, :]),
            s.dma_start(out=ab2c[:], in_=p.ab2_col[:, :]),
        ], deps=[], ndma=2, tag="s6c")

        st_w1c, st_w2c = {}, {}
        for ck in range(16):
            dblk, hh = ck >> 1, ck & 1
            asl = slice((ck % 2) * 2048, (ck % 2) * 2048 + 2048)
            sti = ch.add("sync", (lambda s, dblk=dblk, hh=hh, asl=asl: [
                s.dma_start(out=awstg[:, asl],
                            in_=p.aw1.ap()[dblk * 128:(dblk + 1) * 128,
                                           hh * 2048:(hh + 1) * 2048])]),
                deps=[st_w1c.get(ck - 2)], ndma=1, tag=f"aw1i.{ck}")
            st_w1c[ck] = ch.add("scalar", (
                lambda s, dblk=dblk, hh=hh, asl=asl: s.copy(
                    aw1_sb[:, dblk * 4096 + hh * 2048:
                           dblk * 4096 + hh * 2048 + 2048],
                    awstg[:, asl])),
                deps=[sti], tag=f"aw1c.{ck}")
        for fb in range(32):
            asl = slice(4096 + (fb % 2) * 1024, 4096 + (fb % 2) * 1024 + 1024)
            sti = ch.add("gpsimd", (lambda s, fb=fb, asl=asl: [
                s.dma_start(out=awstg[:, asl],
                            in_=p.aw2.ap()[fb * 128:(fb + 1) * 128, :])]),
                deps=[st_w2c.get(fb - 2)],
                ndma=1, tag=f"aw2i.{fb}")
            st_w2c[fb] = ch.add("vector", (lambda v, fb=fb, asl=asl:
                                v.tensor_copy(
                                    aw2_sb[:, fb * 1024:(fb + 1) * 1024],
                                    awstg[:, asl])),
                                deps=[sti], tag=f"aw2c.{fb}")
        aw1_done, aw2_done = st_w1c[15], st_w2c[31]

        st_xn6, st_pr, st_prev, st_cn, st_cev, st_abd = {}, {}, {}, {}, {}, {}
        for tg in range(8):
            st_xn6[tg] = ch.add("sync", (lambda s, tg=tg: [
                s.dma_start(out=xn_tg[:, dblk * 512:(dblk + 1) * 512],
                            in_=p.xnb.ap()[dblk * 128:(dblk + 1) * 128,
                                           tg * 512:(tg + 1) * 512])
                for dblk in range(8)]),
                deps=[st_pr.get((tg - 1, 31))], ndma=8, tag=f"xn6.{tg}")
            for fb in range(32):
                pr = (pr0, pr1)[fb % 2]

                def epr(t, tg=tg, fb=fb, pr=pr):
                    last = None
                    for dblk in range(8):
                        last = t.matmul(
                            pr[:, :],
                            aw1_sb.ap()[:, dblk * 4096 + fb * 128:
                                        dblk * 4096 + fb * 128 + 128],
                            xn_tg.ap()[:, dblk * 512:(dblk + 1) * 512],
                            start=(dblk == 0), stop=(dblk == 7))
                    return last
                prdeps = [st_xn6[tg], st_prev.get((tg, fb - 2))]
                if fb == 0:
                    prdeps += [aw1_done, st_cn.get((tg - 1, 7))]
                if tg == 0 and fb < 2:
                    prdeps.append(st_prev.get((tg - 1, 30 + fb)))
                st_pr[(tg, fb)] = ch.add("tensor", epr, deps=prdeps,
                                         tag=f"pr.{tg}.{fb}")
                st_prev[(tg, fb)] = ch.add("scalar", (
                    lambda s, tg=tg, fb=fb, pr=pr: s.activation(
                        a1_sb[:, fb * 512:(fb + 1) * 512], pr[:, :],
                        AF.Gelu, bias=ab1c.ap()[:, fb:fb + 1])),
                    deps=[st_pr[(tg, fb)], st_c6], tag=f"prev.{tg}.{fb}")
            for ob in range(8):
                cn = (cn0, cn1)[ob % 2]

                def ecn(t, tg=tg, ob=ob, cn=cn):
                    last = None
                    for fb in range(32):
                        last = t.matmul(
                            cn[:, :],
                            aw2_sb.ap()[:, fb * 1024 + ob * 128:
                                        fb * 1024 + ob * 128 + 128],
                            a1_sb.ap()[:, fb * 512:(fb + 1) * 512],
                            start=(fb == 0), stop=(fb == 31))
                    return last
                cndeps = [st_prev[(tg, 31)], st_cev.get((tg, ob - 2))]
                if ob == 0:
                    cndeps.append(aw2_done)
                st_cn[(tg, ob)] = ch.add("tensor", ecn, deps=cndeps,
                                         tag=f"cn.{tg}.{ob}")
                bsl = slice((ob % 2) * 512, (ob % 2) * 512 + 512)
                st_cev[(tg, ob)] = ch.add("vector", (
                    lambda v, ob=ob, cn=cn, bsl=bsl: v.tensor_single_scalar(
                        abst[:, bsl], cn[:, :], ab2c.ap()[:, ob:ob + 1],
                        ALU.add)),
                    deps=[st_cn[(tg, ob)], st_c6, st_abd.get((tg, ob - 2))],
                    tag=f"cev.{tg}.{ob}")
                st_abd[(tg, ob)] = ch.add("sync", (
                    lambda s, tg=tg, ob=ob, bsl=bsl: [
                        s.dma_start(out=p.abelb.ap()[ob * 128:(ob + 1) * 128,
                                                     tg * 512:(tg + 1) * 512],
                                    in_=abst[:, bsl])]),
                    deps=[st_cev[(tg, ob)]], ndma=1, tag=f"abd.{tg}.{ob}")

        if upto == "s6":
            _finish_dump([
                (p.dbg_b.ap()[:, 0:32768].rearrange("q (dk n) -> q dk n",
                                                    n=4096),
                 p.abelb.ap().rearrange("(dk q) n -> q dk n", q=128)),
            ])
            ch.build_section(blk6)
            return nc
        with nc.named_scope("s6"):
            ch.build_section(blk6)

    ch.section_barrier()
    with contextlib.ExitStack() as _ctx:
        ewc1 = _ctx.enter_context(nc.sbuf_tensor("ewc1", [128, 16384], BF16))
        ewc2 = _ctx.enter_context(nc.sbuf_tensor("ewc2", [128, 16384], BF16))
        xn_sp = _ctx.enter_context(nc.sbuf_tensor("xn_sp", [128, 8192], BF16))
        u16 = _ctx.enter_context(nc.sbuf_tensor("u16", [128, 8192], BF16))
        nb_acc = _ctx.enter_context(nc.sbuf_tensor("nb_acc", [128, 8192],
                                                   F32))
        w6s = _ctx.enter_context(nc.sbuf_tensor("w6s", [32, 4096], BF16))
        esel_sb = _ctx.enter_context(nc.sbuf_tensor("esel_sb", [8, 768],
                                                    BF16))
        eb2_sb = _ctx.enter_context(nc.sbuf_tensor("eb2_sb", [8, 1024], BF16))
        eb1c = _ctx.enter_context(nc.sbuf_tensor("eb1c", [128, 96], F32))
        wrep_f = _ctx.enter_context(nc.sbuf_tensor("wrep_f", [128, 1024],
                                                   F32))
        nbr_b = _ctx.enter_context(nc.sbuf_tensor("nbr_b", [128, 4096],
                                                  BF16))
        abel_t = _ctx.enter_context(nc.sbuf_tensor("abel_t", [128, 4096],
                                                   BF16))
        x1_t = _ctx.enter_context(nc.sbuf_tensor("x1_t", [128, 1024], BF16))
        nbb = _ctx.enter_context(nc.sbuf_tensor("nbb", [128, 4096], BF16))
        z_f = _ctx.enter_context(nc.sbuf_tensor("z_f", [128, 4096], F32))
        sqf = _ctx.enter_context(nc.sbuf_tensor("sqf", [128, 512], F32))
        gcstg = _ctx.enter_context(nc.sbuf_tensor("gcstg", [128, 16], F32))
        gc1b = _ctx.enter_context(nc.sbuf_tensor("gc1b", [128, 8], BF16))
        gc2b = _ctx.enter_context(nc.sbuf_tensor("gc2b", [128, 8], BF16))
        gcbf = _ctx.enter_context(nc.sbuf_tensor("gcbf", [1, 1], F32))
        growf = _ctx.enter_context(nc.sbuf_tensor("growf", [1, 512], F32))
        onesrf2 = _ctx.enter_context(nc.sbuf_tensor("onesrf2", [1, 128], F32))
        onescf = _ctx.enter_context(nc.sbuf_tensor("onescf", [128, 1], F32))
        rowsF = _ctx.enter_context(nc.sbuf_tensor("rowsF", [1, 2048], F32))
        rrowF = _ctx.enter_context(nc.sbuf_tensor("rrowF", [1, 512], F32))
        ot = _ctx.enter_context(nc.sbuf_tensor("ot", [128, 1024], F32))
        mp0 = _ctx.enter_context(nc.psum_tensor([128, 512], F32))
        mp1 = _ctx.enter_context(nc.psum_tensor([128, 512], F32))
        me0 = _ctx.enter_context(nc.psum_tensor([128, 512], F32))
        me1 = _ctx.enter_context(nc.psum_tensor([128, 512], F32))
        wrepP = _ctx.enter_context(nc.psum_tensor([128, 512], F32))
        gateP = _ctx.enter_context(nc.psum_tensor([8, 512], F32))
        sF1 = _ctx.enter_context(nc.psum_tensor([1, 512], F32))
        sF2 = _ctx.enter_context(nc.psum_tensor([1, 512], F32))
        blk7 = _ctx.enter_context(nc.Block())

        mps = (mp0, mp1)
        mes = (me0, me1)

        st_c7 = ch.add("sync", lambda s: [
            s.dma_start(out=w6s[:], in_=p.w6_hbm[:, :]),
            s.dma_start(out=esel_sb.ap()[0:6, :], in_=p.esel[:, :]),
            s.dma_start(out=z_f.ap()[0:8, 0:1024], in_=p.eb2_nat[:, :]),
            s.dma_start(out=eb1c[:], in_=p.eb1_col[:, :]),
            s.dma_start(out=gcstg[:, 0:8], in_=p.gc1_col[:, :]),
            s.dma_start(out=gcstg[:, 8:16], in_=p.gc2_col[:, :]),
            s.dma_start(out=gcbf[:], in_=p.gcb[:, :]),
            s.dma_start(out=onesrf2[:], in_=p.ones_rf[:, :]),
            s.dma_start(out=onescf[:], in_=p.ones_cf[:, :]),
        ], deps=[], ndma=9, tag="s7c")

        def ec7(v):
            i = v.tensor_copy(eb2_sb[:], z_f.ap()[0:8, 0:1024])
            i = v.tensor_copy(gc1b[:], gcstg[:, 0:8])
            return v.tensor_copy(gc2b[:], gcstg[:, 8:16])
        st_cc7 = ch.add("vector", ec7, deps=[st_c7], tag="s7cc")

        st_e1, st_e2, st_mp, st_pev, st_cm, st_acc = {}, {}, {}, {}, {}, {}
        st_wrp, st_wev = {}, {}
        st_xn7, st_rld, st_rot, st_mix, st_sqF, st_stF = {}, {}, {}, {}, {}, {}
        st_apl, st_od, st_x1t, st_abt, st_nbb = {}, {}, {}, {}, {}
        st_grep, st_bcF, st_rev = {}, {}, {}
        out_dmas = []
        for s in range(4):
            st_xn7[s] = ch.add("sync", (lambda sy, s=s: [
                sy.dma_start(out=xn_sp[:, dblk * 1024:(dblk + 1) * 1024],
                             in_=p.xnb.ap()[dblk * 128:(dblk + 1) * 128,
                                            s * 1024:(s + 1) * 1024])
                for dblk in range(8)]),
                deps=[st_mp.get((s - 1, 5, 1, 15))], ndma=8, tag=f"xn7.{s}")
            for e in range(6):
                e1deps = [st_mp.get((s, e - 1, 1, 15)),
                          st_mp.get((s - 1, 5, 1, 15)) if e == 0 else None]
                st_e1a = ch.add("sync", (lambda sy, e=e: [
                    sy.dma_start(out=ewc1[:, dblk * 2048:(dblk + 1) * 2048],
                                 in_=p.cc_e1out.ap()[
                                     e * 1024 + dblk * 128:
                                     e * 1024 + dblk * 128 + 128, :])
                    for dblk in range(4)]),
                    deps=e1deps, ndma=4, tag=f"ew1a.{s}.{e}")
                st_e1b = ch.add("scalar", (lambda sy, e=e: [
                    sy.dma_start(out=ewc1[:, dblk * 2048:(dblk + 1) * 2048],
                                 in_=p.cc_e1out.ap()[
                                     e * 1024 + dblk * 128:
                                     e * 1024 + dblk * 128 + 128, :])
                    for dblk in range(4, 8)]),
                    deps=e1deps, ndma=4, tag=f"ew1b.{s}.{e}")
                st_e1[(s, e)] = st_e1a
                e2deps = [st_cm.get((s, e - 1, 1, 7)),
                          st_cm.get((s - 1, 5, 1, 7)) if e == 0 else None]
                st_e2a = ch.add("gpsimd", (lambda sy, e=e: [
                    sy.dma_start(out=ewc2[:, fb * 1024:(fb + 1) * 1024],
                                 in_=p.cc_e2out.ap()[
                                     e * 2048 + fb * 128:
                                     e * 2048 + fb * 128 + 128, :])
                    for fb in range(8)]),
                    deps=e2deps, ndma=8, tag=f"ew2a.{s}.{e}")
                st_e2b = ch.add("scalar", (lambda sy, e=e: [
                    sy.dma_start(out=ewc2[:, fb * 1024:(fb + 1) * 1024],
                                 in_=p.cc_e2out.ap()[
                                     e * 2048 + fb * 128:
                                     e * 2048 + fb * 128 + 128, :])
                    for fb in range(8, 16)]),
                    deps=e2deps, ndma=8, tag=f"ew2b.{s}.{e}")
                st_e2[(s, e)] = st_e2a
                for sub2 in range(2):
                    tsl = slice(s * 1024 + sub2 * 512,
                                s * 1024 + sub2 * 512 + 512)
                    wsl = slice(sub2 * 512, sub2 * 512 + 512)

                    def ewr(t, e=e, tsl=tsl):
                        return t.matmul(wrepP[:, :],
                                        esel_sb.ap()[0:6, e * 128:
                                                     e * 128 + 128],
                                        w6s.ap()[0:6, tsl],
                                        start=True, stop=True)
                    if sub2 == 1:
                        prev_wev = st_wev.get((s, e, 0))
                    elif e > 0:
                        prev_wev = st_wev.get((s, e - 1, 1))
                    else:
                        prev_wev = st_wev.get((s - 1, 5, 1))
                    wdeps = [st_c7, prev_wev]
                    if e == 0 and sub2 == 0:
                        wdeps.append(st_mix.get((s - 1, 1, 7)))
                    st_wrp[(s, e, sub2)] = ch.add("tensor", ewr, deps=wdeps,
                                                  tag=f"wr.{s}.{e}.{sub2}")
                    st_wev[(s, e, sub2)] = ch.add("scalar", (
                        lambda sc, wsl=wsl: sc.copy(wrep_f[:, wsl],
                                                    wrepP[:, :])),
                        deps=[st_wrp[(s, e, sub2)],
                              st_acc.get((s, e - 1, sub2, 7)),
                              st_acc.get((s - 1, 5, sub2, 7))
                              if e == 0 else None],
                        tag=f"we.{s}.{e}.{sub2}")

                    for fb in range(16):
                        mp = mps[fb % 2]

                        def epr7(t, e=e, fb=fb, mp=mp, sub2=sub2, s=s):
                            last = None
                            for dblk in range(8):
                                last = t.matmul(
                                    mp[:, :],
                                    ewc1.ap()[:, dblk * 2048 + fb * 128:
                                              dblk * 2048 + fb * 128 + 128],
                                    xn_sp.ap()[:, dblk * 1024 + sub2 * 512:
                                               dblk * 1024 + sub2 * 512
                                               + 512],
                                    start=(dblk == 0), stop=(dblk == 7))
                            return last
                        pdeps = [st_e1a, st_e1b, st_xn7[s],
                                 st_pev.get((s, e, sub2, fb // 2 - 1))]
                        if fb < 2:
                            pdeps += [st_pev.get((s, e, sub2 - 1, 7)),
                                      st_pev.get((s, e - 1, 1, 7)),
                                      st_pev.get((s - 1, 5, 1, 7))
                                      if e == 0 else None,
                                      st_rev.get((s - 1, 1, 6))
                                      if e == 0 else None,
                                      st_rev.get((s - 1, 1, 7))
                                      if e == 0 else None]
                        st_mp[(s, e, sub2, fb)] = ch.add(
                            "tensor", epr7, deps=pdeps,
                            tag=f"mp.{s}.{e}.{sub2}.{fb}")
                        if fb % 2 == 1:
                            fp = fb // 2

                            def epe7(sc, e=e, fp=fp):
                                i = None
                                for j in range(2):
                                    fbj = 2 * fp + j
                                    i = sc.activation(
                                        u16[:, fbj * 512:(fbj + 1) * 512],
                                        mps[j][:, :], AF.Gelu,
                                        bias=eb1c.ap()[:, e * 16 + fbj:
                                                       e * 16 + fbj + 1])
                                return i
                            st_pev[(s, e, sub2, fp)] = ch.add(
                                "scalar", epe7,
                                deps=[st_mp[(s, e, sub2, fb)], st_c7],
                                tag=f"pe.{s}.{e}.{sub2}.{fp}")

                    for ob in range(8):
                        me = mes[ob % 2]

                        def ecn7(t, e=e, ob=ob, me=me, tsl=tsl):
                            first = True
                            if e == 0:
                                t.matmul(me[:, :],
                                         eb2_sb.ap()[0:6, ob * 128:
                                                     ob * 128 + 128],
                                         w6s.ap()[0:6, tsl],
                                         start=True, stop=False)
                                first = False
                            last = None
                            for fb in range(16):
                                last = t.matmul(
                                    me[:, :],
                                    ewc2.ap()[:, fb * 1024 + ob * 128:
                                              fb * 1024 + ob * 128 + 128],
                                    u16.ap()[:, fb * 512:(fb + 1) * 512],
                                    start=first and fb == 0, stop=(fb == 15))
                            return last
                        cdeps = [st_pev[(s, e, sub2, 7)], st_e2a, st_e2b,
                                 st_acc.get((s, e, sub2, ob - 2))]
                        if ob < 2:
                            cdeps += [st_acc.get((s, e, sub2 - 1, 6 + ob)),
                                      st_acc.get((s, e - 1, 1, 6 + ob)),
                                      st_apl.get((s - 1, 1, 6 + ob))
                                      if e == 0 else None]
                        st_cm[(s, e, sub2, ob)] = ch.add(
                            "tensor", ecn7, deps=cdeps,
                            tag=f"cm.{s}.{e}.{sub2}.{ob}")
                        asl = slice(ob * 1024 + sub2 * 512,
                                    ob * 1024 + sub2 * 512 + 512)

                        def eacc(v, e=e, me=me, asl=asl, wsl=wsl):
                            if e == 0:
                                return v.tensor_tensor(
                                    nb_acc[:, asl], me[:, :], wrep_f[:, wsl],
                                    ALU.mult)
                            i = v.tensor_tensor(z_f[:, 0:512], me[:, :],
                                                wrep_f[:, wsl], ALU.mult)
                            ch.gap(v, i, "vector")
                            return v.tensor_tensor(nb_acc[:, asl],
                                                   nb_acc[:, asl],
                                                   z_f[:, 0:512], ALU.add)
                        st_acc[(s, e, sub2, ob)] = ch.add(
                            "vector", eacc,
                            deps=[st_cm[(s, e, sub2, ob)],
                                  st_wev[(s, e, sub2)]],
                            sync_all=True, tag=f"ac.{s}.{e}.{sub2}.{ob}")

            st_rld[s] = ch.add("sync", (lambda sy: [
                sy.dma_start(out=u16.ap().rearrange("q (dk o) -> q dk o",
                                                    o=1024),
                             in_=p.exR.ap().rearrange("(dk q) o -> q dk o",
                                                      q=128))]),
                deps=[st_cm[(s, 5, 1, 7)]], ndma=1, tag=f"rld.{s}")
            for sub2 in range(2):
                tb = s * 1024 + sub2 * 512
                wsl = slice(sub2 * 512, sub2 * 512 + 512)
                for hq in range(2):
                    def enbb(v, sub2=sub2, hq=hq):
                        last = None
                        for dblk in range(4 * hq, 4 * hq + 4):
                            last = v.tensor_copy(
                                nbb[:, dblk * 512:(dblk + 1) * 512],
                                nb_acc[:, dblk * 1024 + sub2 * 512:
                                       dblk * 1024 + sub2 * 512 + 512])
                        return last
                    st_nbb[(s, sub2, hq)] = ch.add(
                        "vector", enbb,
                        deps=[st_acc[(s, 5, sub2, 7)],
                              st_rot.get((s - 1, 1, 7)) if sub2 == 0
                              else st_rot.get((s, 0, 7))],
                        tag=f"nbb.{s}.{sub2}.{hq}")
                st_abt[(s, sub2)] = ch.add("sync", (
                    lambda sy, s=s, sub2=sub2, tb=tb: [
                        sy.dma_start(out=abel_t[:, ob * 512:(ob + 1) * 512],
                                     in_=p.abelb.ap()[
                                         ob * 128:(ob + 1) * 128,
                                         tb:tb + 512])
                        for ob in range(8)]),
                    deps=[st_mix.get((s, sub2 - 1, 7)),
                          st_mix.get((s - 1, 1, 7)) if sub2 == 0 else None,
                          st_grep.get((s, sub2 - 1)),
                          st_grep.get((s - 1, 1)) if sub2 == 0 else None],
                    ndma=8, tag=f"abt.{s}.{sub2}")

                for ob in range(8):
                    mp = mps[ob % 2]

                    def erot(t, ob=ob, mp=mp):
                        last = None
                        for dblk in range(8):
                            last = t.matmul(
                                mp[:, :],
                                u16.ap()[:, dblk * 1024 + ob * 128:
                                         dblk * 1024 + ob * 128 + 128],
                                nbb.ap()[:, dblk * 512:(dblk + 1) * 512],
                                start=(dblk == 0), stop=(dblk == 7))
                        return last
                    st_rot[(s, sub2, ob)] = ch.add(
                        "tensor", erot,
                        deps=[st_rld[s], st_nbb[(s, sub2, 0)],
                              st_nbb[(s, sub2, 1)],
                              st_rev.get((s, sub2, ob - 2))],
                        tag=f"rt.{s}.{sub2}.{ob}")
                    st_rev[(s, sub2, ob)] = ch.add("scalar", (
                        lambda sc, ob=ob, mp=mp: sc.copy(
                            nbr_b[:, ob * 512:(ob + 1) * 512], mp[:, :])),
                        deps=[st_rot[(s, sub2, ob)],
                              st_mix.get((s, sub2 - 1, ob)),
                              st_mix.get((s - 1, 1, ob))
                              if sub2 == 0 else None],
                        tag=f"rv.{s}.{sub2}.{ob}")

                def egt(t, sub2=sub2):
                    for ob in range(8):
                        t.matmul(gateP[0:1, :], gc1b.ap()[:, ob:ob + 1],
                                 abel_t.ap()[:, ob * 512:(ob + 1) * 512],
                                 start=(ob == 0), stop=False)
                    last = None
                    for ob in range(8):
                        last = t.matmul(
                            gateP[0:1, :], gc2b.ap()[:, ob:ob + 1],
                            nbr_b.ap()[:, ob * 512:(ob + 1) * 512],
                            start=False, stop=(ob == 7))
                    return last
                st_gt = ch.add("tensor", egt,
                               deps=[st_abt[(s, sub2)],
                                     st_rev[(s, sub2, 7)], st_cc7,
                                     st_grep.get((s, sub2 - 1)),
                                     st_grep.get((s - 1, 1))
                                     if sub2 == 0 else None],
                               tag=f"gt.{s}.{sub2}")
                st_sg = ch.add("scalar", (lambda sc: sc.activation(
                    growf[0:1, :], gateP[0:1, :], AF.Sigmoid,
                    bias=gcbf.ap()[0:1, 0:1])),
                    deps=[st_gt, st_c7], tag=f"sg.{s}.{sub2}")
                st_grep[(s, sub2)] = ch.add("tensor", (lambda t: t.matmul(
                    wrepP[:, :], onesrf2.ap()[0:1, :], growf[0:1, :],
                    start=True, stop=True)),
                    deps=[st_sg, st_wev[(s, 5, sub2)],
                          st_acc[(s, 5, sub2, 7)]],
                    tag=f"gr.{s}.{sub2}")

                for ob in range(8):
                    xsl = slice((ob % 2) * 512, (ob % 2) * 512 + 512)
                    st_x1t[(s, sub2, ob)] = ch.add("sync", (
                        lambda sy, s=s, sub2=sub2, ob=ob, xsl=xsl, tb=tb: [
                            sy.dma_start(out=x1_t[:, xsl],
                                         in_=p.x1b.ap()[
                                             ob * 128:(ob + 1) * 128,
                                             tb:tb + 512])]),
                        deps=[st_mix.get((s, sub2, ob - 2)),
                              st_mix.get((s, sub2 - 1, 6 + ob))
                              if ob < 2 else None,
                              st_mix.get((s - 1, 1, 6 + ob))
                              if ob < 2 and sub2 == 0 else None],
                        ndma=1, tag=f"x1t.{s}.{sub2}.{ob}")

                    def emix(v, ob=ob, xsl=xsl):
                        zsl = slice(ob * 512, ob * 512 + 512)
                        absl = slice(ob * 512, ob * 512 + 512)
                        i = v.tensor_tensor(z_f[:, zsl], nbr_b[:, absl],
                                            abel_t[:, absl], ALU.subtract)
                        ch.gap(v, i, "vector")
                        i = v.tensor_tensor(z_f[:, zsl], z_f[:, zsl],
                                            wrepP[:, :], ALU.mult)
                        ch.gap(v, i, "vector")
                        i = v.tensor_tensor(z_f[:, zsl], z_f[:, zsl],
                                            abel_t[:, absl], ALU.add)
                        ch.gap(v, i, "vector")
                        return v.scalar_tensor_tensor(
                            z_f[:, zsl], x1_t[:, xsl], 2.0, z_f[:, zsl],
                            ALU.mult, ALU.add)
                    st_mix[(s, sub2, ob)] = ch.add(
                        "vector", emix,
                        deps=[st_rev[(s, sub2, ob)], st_grep[(s, sub2)],
                              st_x1t[(s, sub2, ob)], st_abt[(s, sub2)],
                              st_stF.get((s, sub2 - 1, ob)),
                              st_stF.get((s - 1, 1, ob)) if sub2 == 0
                              else None,
                              st_apl.get((s, sub2 - 1, ob)),
                              st_apl.get((s - 1, 1, ob)) if sub2 == 0
                              else None],
                        sync_all=True, tag=f"mx.{s}.{sub2}.{ob}")
                    st_sqF[(s, sub2, ob)] = ch.add("scalar", (
                        lambda sc, ob=ob: sc.activation(
                            sqf[:, 0:512], z_f[:, ob * 512:(ob + 1) * 512],
                            AF.Square)),
                        deps=[st_mix[(s, sub2, ob)], st_cc7,
                              st_stF.get((s, sub2, ob - 1)),
                              (st_stF.get((s, sub2 - 1, 7))
                               or st_stF.get((s - 1, 1, 7)))
                              if ob == 0 else None],
                        tag=f"sqF.{s}.{sub2}.{ob}")

                    def estF(t, ob=ob, xsl=xsl):
                        t.matmul(sF1[:, :], onescf.ap()[:, 0:1],
                                 z_f.ap()[:, ob * 512:(ob + 1) * 512],
                                 start=(ob == 0), stop=(ob == 7))
                        return t.matmul(sF2[:, :], onescf.ap()[:, 0:1],
                                        sqf.ap()[:, 0:512],
                                        start=(ob == 0), stop=(ob == 7))
                    stFdeps = [st_mix[(s, sub2, ob)], st_sqF[(s, sub2, ob)],
                               st_c7]
                    if ob == 0:
                        stFdeps += [st_bcF.get((s, sub2 - 1)),
                                    st_bcF.get((s - 1, 1))
                                    if sub2 == 0 else None]
                    st_stF[(s, sub2, ob)] = ch.add("tensor", estF,
                                                   deps=stFdeps,
                                                   tag=f"sF.{s}.{sub2}.{ob}")

                def ermF(v):
                    r = rowsF.ap()
                    i = v.tensor_single_scalar(r[0:1, 0:512], sF1[0:1, :],
                                               1.0 / 1024, ALU.mult)
                    i = v.tensor_single_scalar(r[0:1, 512:1024], sF2[0:1, :],
                                               1.0 / 1024, ALU.mult)
                    ch.gap(v, i, "vector")
                    i = v.tensor_tensor(r[0:1, 1024:1536], r[0:1, 0:512],
                                        r[0:1, 0:512], ALU.mult)
                    ch.gap(v, i, "vector")
                    return v.tensor_tensor(r[0:1, 1024:1536],
                                           r[0:1, 512:1024],
                                           r[0:1, 1024:1536], ALU.subtract)
                st_rmF = ch.add("vector", ermF, deps=[st_stF[(s, sub2, 7)]],
                                sync_all=True, tag=f"rmF.{s}.{sub2}")
                st_sdF = ch.add("scalar", (lambda sc: sc.activation(
                    rowsF.ap()[0:1, 1536:2048], rowsF.ap()[0:1, 1024:1536],
                    AF.Sqrt, bias=epsc.ap()[0:1, 0:1])),
                    deps=[st_rmF], tag=f"sdF.{s}.{sub2}")
                st_rcF = ch.add("vector", (lambda v: v.reciprocal(
                    rrowF.ap()[0:1, :], rowsF.ap()[0:1, 1536:2048])),
                    deps=[st_sdF], sync_all=True, tag=f"rcF.{s}.{sub2}")

                def ebcF(t):
                    t.matmul(me0[:, :], onesrf2.ap()[0:1, :],
                             rowsF.ap()[0:1, 0:512], start=True, stop=True)
                    return t.matmul(me1[:, :], onesrf2.ap()[0:1, :],
                                    rrowF.ap()[0:1, :], start=True, stop=True)
                st_bcF[(s, sub2)] = ch.add(
                    "tensor", ebcF,
                    deps=[st_rcF, st_rmF, st_acc[(s, 5, 1, 7)]],
                    tag=f"bcF.{s}.{sub2}")

                for ob in range(8):
                    osl = slice((ob % 2) * 512, (ob % 2) * 512 + 512)

                    def eapl(v, ob=ob, osl=osl):
                        zsl = slice(ob * 512, ob * 512 + 512)
                        i = v.tensor_tensor(ot[:, osl], z_f[:, zsl],
                                            me0[:, :], ALU.subtract)
                        ch.gap(v, i, "vector")
                        return v.tensor_tensor(ot[:, osl], ot[:, osl],
                                               me1[:, :], ALU.mult)
                    st_apl[(s, sub2, ob)] = ch.add(
                        "vector", eapl,
                        deps=[st_bcF[(s, sub2)], st_od.get((s, sub2, ob - 2)),
                              st_od.get((s, sub2 - 1, 6 + ob))
                              if ob < 2 else None,
                              st_od.get((s - 1, 1, 6 + ob))
                              if ob < 2 and sub2 == 0 else None],
                        sync_all=True, tag=f"ap.{s}.{sub2}.{ob}")
                    st_od[(s, sub2, ob)] = ch.add("sync", (
                        lambda sy, s=s, sub2=sub2, ob=ob, osl=osl, tb=tb: [
                            sy.dma_start(out=p.out_ot.ap()[
                                ob * 128:(ob + 1) * 128, tb:tb + 512],
                                in_=ot[:, osl])]),
                        deps=[st_apl[(s, sub2, ob)]], ndma=1,
                        tag=f"od.{s}.{sub2}.{ob}")
                    out_dmas.append(st_od[(s, sub2, ob)])

        if upto == "s7nb":
            _finish_dump([(p.dbg_f.ap()[:, 0:8192], nb_acc.ap()[:, :])])
            ch.build_section(blk7)
            return nc

        ch.add("gpsimd", lambda gp: gp.engine_nop(), deps=out_dmas,
               tag="fin")
        with nc.named_scope("s7"):
            ch.build_section(blk7)

    return nc


def phase1_kernel(**inputs):
    in_maps = [_prep_phase1_inputs(i, inputs) for i in range(NCORE)]
    nc = build_program(phase1_only=True)
    nc.finalize()
    res = run_bass_kernel_spmd(nc, in_maps, core_ids=list(range(NCORE)))
    return res.results[0]["dbg_g"]


def kernel(**inputs):
    in_maps = [prep_inputs(i, inputs) for i in range(NCORE)]
    nc = build_program()
    nc.finalize()
    res = run_bass_kernel_spmd(nc, in_maps, core_ids=list(range(NCORE)))
    out = np.empty((B, L, H), np.float32)
    for i in range(NCORE):
        bc, half = i >> 1, i & 1
        out[bc, half * 4096:(half + 1) * 4096, :] = np.ascontiguousarray(
            np.asarray(res.results[i]["out_ot"]).T)
    return out


# version 09
# speedup vs baseline: 1.0240x; 1.0240x over previous
import contextlib
import math
import numpy as np
import ml_dtypes

import concourse.bacc as bacc
import concourse.mybir as mybir
import concourse.aot_env as aot_env
from concourse.bass_utils import run_bass_kernel_spmd

F32 = mybir.dt.float32
BF16 = mybir.dt.bfloat16
AF = mybir.ActivationFunctionType
ALU = mybir.AluOpType

B, L, H, N2 = 4, 8192, 1024, 32
T = 128
NCHUNK = L // T
NCORE = 8
HL = 128
BC = B * NCHUNK
PI = math.pi
TWO_PI = 2.0 * math.pi


def _cw_split():
    parts, rem = [], TWO_PI
    for _ in range(2):
        v = np.float32(rem)
        u = np.frombuffer(v.tobytes(), np.uint32)[0] & np.uint32(0xFFFFF000)
        v = float(np.frombuffer(np.uint32(u).tobytes(), np.float32)[0])
        parts.append(v)
        rem -= v
    parts.append(float(np.float32(rem)))
    return parts


C_HI, C_MID, C_LO = _cw_split()

ENGINES = ("tensor", "vector", "scalar", "sync", "gpsimd")
DMA_SLOTS = 6


class _Step:
    __slots__ = ("idx", "eng", "emit", "deps", "ndma", "end", "slot",
                 "slot_end", "tag", "sync_all")


class Chain:
    def __init__(self, nc):
        self.nc = nc
        self.steps = []
        self.sems = {}
        self.slot_sems = {}
        self.counts = {}
        self.slot_state = {}
        self._slot_rr = {}
        self._built_upto = 0
        self._barrier = []
        self._barrier_idx = -1

    def add(self, eng, emit, deps="prev", ndma=0, tag="", sync_all=False):
        if deps == "prev":
            deps = [self.steps[-1]] if self.steps else []
        deps = [d for d in (deps or []) if d is not None]
        st = _Step()
        st.idx, st.eng, st.emit, st.deps, st.ndma, st.tag = (
            len(self.steps), eng, emit, deps, ndma, tag)
        st.sync_all = sync_all
        st.end = st.slot = st.slot_end = None
        if self._barrier and not any(d.idx >= self._barrier_idx for d in deps):
            deps = deps + [b for b in self._barrier if b not in deps]
            st.deps = deps
        if ndma:
            rr = self._slot_rr.get(eng, 0)
            st.slot = rr % DMA_SLOTS
            self._slot_rr[eng] = rr + 1
            key = (eng, st.slot)
            if key not in self.slot_sems:
                self.slot_sems[key] = self.nc.alloc_semaphore(
                    f"dma_{eng}_{st.slot}")
                self.slot_state[key] = 0
            st.slot_end = self.slot_state[key] + 16 * ndma
            self.slot_state[key] = st.slot_end
        else:
            if eng not in self.sems:
                self.sems[eng] = self.nc.alloc_semaphore(f"ch_{eng}")
                self.counts[eng] = 0
            self.counts[eng] += 1
            st.end = self.counts[eng]
        self.steps.append(st)
        return st

    def section_barrier(self):
        self._barrier = self.frontier()
        self._barrier_idx = len(self.steps)

    def gap(self, e, inst, eng):
        key = "gap_" + eng
        if key not in self.sems:
            self.sems[key] = self.nc.alloc_semaphore(key)
            self.counts[key] = 0
        self.counts[key] += 1
        inst.then_inc(self.sems[key], 1)
        e.wait_ge(self.sems[key], self.counts[key])

    def wait_value(self, st):
        if st.ndma:
            return (self.slot_sems[(st.eng, st.slot)], st.slot_end)
        return (self.sems[st.eng], st.end)

    def build_section(self, block):
        todo = self.steps[self._built_upto:]
        self._built_upto = len(self.steps)
        per_eng = {e: [] for e in ENGINES}
        for st in todo:
            per_eng[st.eng].append(st)

        def mk(mysteps):
            def run(e):
                for st in mysteps:
                    waits = {}
                    for d in st.deps:
                        if (d.eng == st.eng and not d.ndma and not st.ndma
                                and not st.sync_all):
                            continue
                        sem, val = self.wait_value(d)
                        if waits.get(id(sem), (None, -1))[1] < val:
                            waits[id(sem)] = (sem, val)
                    if st.ndma:
                        key = (st.eng, st.slot)
                        prev = st.slot_end - 16 * st.ndma
                        if prev > 0:
                            sem = self.slot_sems[key]
                            if waits.get(id(sem), (None, -1))[1] < prev:
                                waits[id(sem)] = (sem, prev)
                    for sem, val in waits.values():
                        e.wait_ge(sem, val)
                    out = st.emit(e)
                    if st.ndma:
                        insts = out if isinstance(out, (list, tuple)) else [out]
                        assert len(insts) == st.ndma, (st.idx, st.tag, len(insts))
                        sem = self.slot_sems[(st.eng, st.slot)]
                        for ins in insts:
                            ins.then_inc(sem, 16)
                    else:
                        assert out is not None, (st.idx, st.tag)
                        out.then_inc(self.sems[st.eng], 1)
            return run

        for eng in ENGINES:
            if per_eng[eng]:
                getattr(block, eng)(mk(per_eng[eng]))

    def frontier(self):
        last = {}
        for st in self.steps:
            last[(st.eng, st.slot if st.ndma else None)] = st
        return list(last.values())


def _prep_phase1_inputs(i, ins):
    h0 = 128 * i
    u = np.asarray(ins["L0_attn_u"], np.float32)
    log_dt = np.asarray(ins["L0_attn_log_dt"], np.float32)
    lwre = np.asarray(ins["L0_attn_log_w_real"], np.float32).reshape(N2)
    wim = np.asarray(ins["L0_attn_w_imag"], np.float32).reshape(N2)
    c_re = np.asarray(ins["L0_attn_C_real"], np.float32).reshape(H, N2)
    c_im = np.asarray(ins["L0_attn_C_imag"], np.float32).reshape(H, N2)
    D = np.asarray(ins["L0_attn_D"], np.float32).reshape(H)

    d = {}
    uc = u[:, :, h0:h0 + 128].reshape(B, NCHUNK, T, HL)
    d["u_conv"] = np.ascontiguousarray(uc.transpose(2, 3, 0, 1)).reshape(
        T, HL * BC)

    def repA(val_hn):
        a = np.broadcast_to(np.asarray(val_hn, np.float32)[:, None, :],
                            (HL, 2, N2)).reshape(HL * 64)
        return np.ascontiguousarray(np.broadcast_to(a[None, :], (T, HL * 64)))

    ld_h = np.broadcast_to(log_dt[h0:h0 + 128, None], (HL, N2))
    d["tA_logdt"] = repA(ld_h)
    d["tA_lwre"] = repA(np.broadcast_to(lwre[None, :], (HL, N2)))
    d["tA_wim"] = repA(np.broadcast_to(wim[None, :], (HL, N2)))

    def repB(val_hn):
        v = np.asarray(val_hn, np.float32).reshape(64, 2, N2)
        out = np.empty((2, 2, N2, 64, 129), np.float32)
        for hp in range(2):
            out[hp] = np.broadcast_to(v[:, hp, :].T[None, :, :, None],
                                      (2, N2, 64, 129))
        return np.ascontiguousarray(out.reshape(128, 64 * 129))

    d["tB_logdt"] = repB(ld_h)
    d["tB_lwre"] = repB(np.broadcast_to(lwre[None, :], (HL, N2)))
    d["tB_wim"] = repB(np.broadcast_to(wim[None, :], (HL, N2)))

    def colB(mat_hn):
        v = np.asarray(mat_hn, np.float32).reshape(64, 2, N2)
        out = np.empty((2, 2, N2, 64), np.float32)
        for hp in range(2):
            out[hp] = np.broadcast_to(v[:, hp, :].T[None], (2, N2, 64))
        return np.ascontiguousarray(out.reshape(128, 64))

    d["cB_re"] = colB(c_re[h0:h0 + 128])
    d["cB_im"] = colB(c_im[h0:h0 + 128])
    wcol = np.broadcast_to(wim[None, None, :], (2, 2, N2)).reshape(128, 1)
    d["wB_wim"] = np.ascontiguousarray(wcol)
    lcol = np.broadcast_to(lwre[None, None, :], (2, 2, N2)).reshape(128, 1)
    d["wB_lwre"] = np.ascontiguousarray(lcol)

    d["al_logdt_col"] = np.ascontiguousarray(log_dt[h0:h0 + 128, None])
    d["al_wim"] = np.ascontiguousarray(
        np.broadcast_to(wim[None, None, :], (HL, B, N2)).reshape(HL, 128))
    d["al_lwre"] = np.ascontiguousarray(
        np.broadcast_to(lwre[None, None, :], (HL, B, N2)).reshape(HL, 128))
    d["D_rep"] = np.ascontiguousarray(
        np.broadcast_to(D[None, h0:h0 + 128], (128, 128)))
    return d


def _col_layout(vec, nb):
    v = np.asarray(vec, np.float32).reshape(nb, 128)
    return np.ascontiguousarray(v.T)


def _prep_phase2_inputs(i, ins):
    b_c, half, k4 = i >> 1, i & 1, i % 4
    d = {}
    u = np.asarray(ins["L0_attn_u"], np.float32)
    d["u_tok"] = np.ascontiguousarray(
        u[b_c, half * 4096:(half + 1) * 4096, :].T)
    w_out = np.asarray(ins["L0_attn_W_out"], np.float32)
    d["w_outT"] = np.ascontiguousarray(w_out.T)
    d["b_out_col"] = _col_layout(ins["L0_attn_b_out"], 8)
    d["ln_g_col"] = _col_layout(ins["L0_ffn_ln_g"], 8)
    d["ln_b_col"] = _col_layout(ins["L0_ffn_ln_b"], 8)
    d["aw1"] = np.ascontiguousarray(np.asarray(ins["L0_ffn_aw1"], np.float32))
    d["aw2"] = np.ascontiguousarray(np.asarray(ins["L0_ffn_aw2"], np.float32))
    d["ab1_col"] = _col_layout(ins["L0_ffn_ab1"], 32)
    d["ab2_col"] = _col_layout(ins["L0_ffn_ab2"], 8)
    gw = np.asarray(ins["L0_ffn_gate_w"], np.float32).reshape(1024, 6)
    d["gw_cols"] = np.ascontiguousarray(gw.reshape(8, 128, 6).transpose(
        1, 0, 2).reshape(128, 48))
    gb = np.zeros((8, 1), np.float32)
    gb[0:6, 0] = np.asarray(ins["L0_ffn_gate_b"], np.float32).reshape(6)
    d["gb_col"] = gb
    ew1 = np.asarray(ins["L0_ffn_ew1"], np.float32).reshape(6144, 2048)
    d["ew1_shard"] = np.ascontiguousarray(ew1[i * 768:(i + 1) * 768])
    ew2 = np.asarray(ins["L0_ffn_ew2"], np.float32).reshape(12288, 1024)
    d["ew2_shard"] = np.ascontiguousarray(ew2[i * 1536:(i + 1) * 1536])
    eb1 = np.asarray(ins["L0_ffn_eb1"], np.float32).reshape(6, 16, 128)
    d["eb1_col"] = np.ascontiguousarray(
        eb1.transpose(2, 0, 1).reshape(128, 96))
    eb2 = np.zeros((8, 1024), np.float32)
    eb2[0:6] = np.asarray(ins["L0_ffn_eb2"], np.float32).reshape(6, 1024)
    d["eb2_nat"] = eb2
    gens = np.asarray(ins["L0_ffn_gens"], np.float32)
    ga = gens[k4].T if k4 in (0, 2) else gens[k4]
    d["gens_a"] = np.ascontiguousarray(ga)
    d["gens_at"] = np.ascontiguousarray(ga.T)
    gc = np.asarray(ins["L0_ffn_gc_w"], np.float32).reshape(2048)
    d["gc1_col"] = _col_layout(gc[0:1024], 8)
    d["gc2_col"] = _col_layout(gc[1024:2048], 8)
    d["gcb"] = np.asarray(ins["L0_ffn_gc_b"], np.float32).reshape(1, 1)
    return d


def prep_inputs(i, ins):
    d = _prep_phase1_inputs(i, ins)
    d.update(_prep_phase2_inputs(i, ins))
    return d


_FACT = [math.factorial(n) for n in range(13)]
_CEYE_COEF = [1.0, 0.5, 1.0 / 6, 1.0 / 24, 1.0 / 120, 1.0 / 720,
              1.0 / 5040, 1.0 / 40320, 1.0 / 362880, 1.0 / 3628800,
              1.0 / 39916800, 1.0 / 479001600]


def _sincos(ch, th, sc, oc, os, sl, tag, ki=None):
    assert ki is not None

    def reduce_into(v, out, extra):
        off = (extra + PI) / TWO_PI
        v.tensor_scalar(sc[sl], th[sl], 1.0 / TWO_PI, off, ALU.mult, ALU.add)
        v.tensor_copy(ki[sl], sc[sl])
        v.tensor_copy(sc[sl], ki[sl])
        v.tensor_single_scalar(out[sl], th[sl], extra + PI, ALU.add)
        v.scalar_tensor_tensor(out[sl], sc[sl], -C_HI, out[sl], ALU.mult, ALU.add)
        v.scalar_tensor_tensor(out[sl], sc[sl], -C_MID, out[sl], ALU.mult, ALU.add)
        v.scalar_tensor_tensor(out[sl], sc[sl], -C_LO, out[sl], ALU.mult, ALU.add)
        v.tensor_single_scalar(out[sl], out[sl], PI, ALU.subtract)
        v.tensor_single_scalar(sc[sl], out[sl], -PI, ALU.is_lt)
        v.scalar_tensor_tensor(out[sl], sc[sl], TWO_PI, out[sl], ALU.mult, ALU.add)
        v.tensor_single_scalar(out[sl], out[sl], 3.141592, ALU.min)
        return v.tensor_single_scalar(out[sl], out[sl], -3.141592, ALU.max)

    ch.add("vector", lambda v: reduce_into(v, os, 0.0), tag=tag + "_sred")
    ch.add("scalar", lambda s: s.activation(os[sl], os[sl], AF.Sin),
           tag=tag + "_sin")
    ch.add("vector", lambda v: reduce_into(v, oc, PI / 2), tag=tag + "_cred")
    ch.add("scalar", lambda s: s.activation(oc[sl], oc[sl], AF.Sin),
           tag=tag + "_cos")


def _rows(t, lo, hi, cols=None):
    ap = t.ap() if hasattr(t, "ap") else t
    if cols is None:
        return ap[lo:hi, :]
    return ap[lo:hi, cols]


def build_program(phase1_only=False, upto=None, variant=None):
    nc = bacc.Bacc("TRN2", target_bir_lowering=False,
                   detect_race_conditions=False)
    p = type("P", (), {})()

    def din(name, shape):
        setattr(p, name, nc.dram_tensor(name, list(shape), F32,
                                        kind="ExternalInput"))
    din("u_conv", (T, HL * BC))
    for nm in ("tA_logdt", "tA_lwre", "tA_wim"):
        din(nm, (128, 8192))
    for nm in ("tB_logdt", "tB_lwre", "tB_wim"):
        din(nm, (128, 8256))
    din("cB_re", (128, 64))
    din("cB_im", (128, 64))
    din("wB_wim", (128, 1))
    din("wB_lwre", (128, 1))
    din("al_logdt_col", (128, 1))
    din("al_wim", (128, 128))
    din("al_lwre", (128, 128))
    din("D_rep", (128, 128))

    din("u_tok", (1024, 4096))
    din("w_outT", (1024, 1024))
    din("b_out_col", (128, 8))
    din("ln_g_col", (128, 8))
    din("ln_b_col", (128, 8))
    din("aw1", (1024, 4096))
    din("aw2", (4096, 1024))
    din("ab1_col", (128, 32))
    din("ab2_col", (128, 8))
    din("gw_cols", (128, 48))
    din("gb_col", (8, 1))
    din("ew1_shard", (768, 2048))
    din("ew2_shard", (1536, 1024))
    din("eb1_col", (128, 96))
    din("eb2_nat", (8, 1024))
    din("gens_a", (1024, 1024))
    din("gens_at", (1024, 1024))
    din("gc1_col", (128, 8))
    din("gc2_col", (128, 8))
    din("gcb", (1, 1))

    p.eye_f32 = nc.inline_tensor(np.eye(128, dtype=np.float32), name="c_eye_f")
    p.eye_bf = nc.inline_tensor(np.eye(128, dtype=ml_dtypes.bfloat16),
                                name="c_eye_b")
    p.triu_f32 = nc.inline_tensor(
        np.ascontiguousarray(np.triu(np.ones((128, 128), np.float32))),
        name="c_triu")
    p.tcol = nc.inline_tensor(
        np.ascontiguousarray((127.0 - np.arange(128, dtype=np.float32))[:, None]),
        name="c_tcol")
    xr = np.broadcast_to(np.arange(129, dtype=np.float32)[None, None, :],
                         (128, 32, 129)).reshape(128, 32 * 129)
    p.xramp = nc.inline_tensor(np.ascontiguousarray(xr), name="c_xramp")
    ce = np.zeros((128, 128 * 12), np.float32)
    for j, cf in enumerate(_CEYE_COEF):
        ce[:, j * 128:(j + 1) * 128] = cf * np.eye(128, dtype=np.float32)
    p.ceye = nc.inline_tensor(ce.astype(ml_dtypes.bfloat16), name="c_ceye")
    es = np.zeros((6, 768), np.float32)
    for e in range(6):
        es[e, e * 128:(e + 1) * 128] = 1.0
    p.esel = nc.inline_tensor(es.astype(ml_dtypes.bfloat16), name="c_esel")
    p.ones_cb = nc.inline_tensor(
        np.ones((128, 1), ml_dtypes.bfloat16), name="c_onescb")
    p.ones_cf = nc.inline_tensor(
        np.ones((128, 1), np.float32), name="c_onescf")
    p.ones_rf = nc.inline_tensor(
        np.ones((1, 128), np.float32), name="c_onesrf")
    p.eps_t = nc.inline_tensor(np.full((8, 1), 1e-5, np.float32),
                               name="c_eps")

    p.z_hbm = nc.dram_tensor("z_hbm", [64, HL * BC], F32)
    p.p_hbm = nc.dram_tensor("p_hbm", [128, 8256], BF16)
    p.q_hbm = nc.dram_tensor("q_hbm", [128, 8256], BF16)
    p.cc_a2in = nc.dram_tensor("cc_a2in", [1024, 4096], BF16)
    p.cc_a2out = nc.dram_tensor("cc_a2out", [1024, 4096], BF16)
    p.x1b = nc.dram_tensor("x1b", [1024, 4096], BF16)
    p.xnb = nc.dram_tensor("xnb", [1024, 4096], BF16)
    p.abelb = nc.dram_tensor("abelb", [1024, 4096], BF16)
    p.w6_hbm = nc.dram_tensor("w6_hbm", [32, 4096], BF16)
    p.cc_e1in = nc.dram_tensor("cc_e1in", [768, 2048], BF16)
    p.cc_e1out = nc.dram_tensor("cc_e1out", [6144, 2048], BF16)
    p.cc_e2in = nc.dram_tensor("cc_e2in", [1536, 1024], BF16)
    p.cc_e2out = nc.dram_tensor("cc_e2out", [12288, 1024], BF16)
    p.gA_hbm = nc.dram_tensor("gA_hbm", [1024, 1024], BF16)
    p.gAt_hbm = nc.dram_tensor("gAt_hbm", [1024, 1024], BF16)
    for nm in ("exA2", "exA2t", "exA3", "exA4", "exA4t",
               "exS0", "exS1", "exS2", "exW", "exP1", "exP2", "exR"):
        setattr(p, nm, nc.dram_tensor(nm, [1024, 1024], BF16))
    p.cc_Ein = nc.dram_tensor("cc_Ein", [1024, 1024], BF16)
    p.cc_Eout = nc.dram_tensor("cc_Eout", [1024 * NCORE, 1024], BF16)
    p.out_ot = nc.dram_tensor("out_ot", [1024, 4096], F32,
                              kind="ExternalOutput")

    if phase1_only:
        p.dbg_g = nc.dram_tensor("dbg_g", [1024, 4096], BF16,
                                 kind="ExternalOutput")
    if upto is not None:
        p.dbg_f = nc.dram_tensor("dbg_f", [128, 32768], F32,
                                 kind="ExternalOutput")
        p.dbg_b = nc.dram_tensor("dbg_b", [128, 36992], BF16,
                                 kind="ExternalOutput")

    def _finish_dump(srcs):
        def ed(s):
            return [s.dma_start(out=d, in_=sr) for d, sr in srcs]
        std = ch.add("sync", ed, deps=ch.frontier(), ndma=len(srcs), tag="dump")
        ch.add("gpsimd", lambda gp: gp.engine_nop(), deps=[std], tag="fin")

    ch = Chain(nc)

    lstack = []

    def allocL(name, shape, dt):
        t = nc.alloc_sbuf_tensor(name, shape, dt, side="left")
        lstack.append(t)
        return t

    eye_f = allocL("eye_f", [128, 128], F32)
    triu_f = allocL("triu_f", [128, 128], F32)
    eye_b = allocL("eye_b", [128, 128], BF16)
    tcol = allocL("tcol", [128, 1], F32)
    drep = allocL("drep", [128, 128], F32)
    a_re = allocL("a_re", [128, 128], F32)
    a_im = allocL("a_im", [128, 128], F32)
    epsc = allocL("epsc", [8, 1], F32)
    stkA = contextlib.ExitStack()
    tabA = stkA.enter_context(nc.sbuf_tensor("tabA", [128, 8192], BF16))

    def e_consts(s):
        return [
            s.dma_start(out=eye_f[:], in_=p.eye_f32[:, :]),
            s.dma_start(out=triu_f[:], in_=p.triu_f32[:, :]),
            s.dma_start(out=eye_b[:], in_=p.eye_bf[:, :]),
            s.dma_start(out=tcol[:], in_=p.tcol[:, :]),
            s.dma_start(out=drep[:], in_=p.D_rep[:, :]),
            s.dma_start(out=epsc[:], in_=p.eps_t[:, :]),
        ]
    st_consts = ch.add("sync", e_consts, deps=[], ndma=6, tag="consts")

    with contextlib.ExitStack() as _ctx:
        v0 = _ctx.enter_context(nc.sbuf_tensor("v0", [128, 4224], F32))
        v1 = _ctx.enter_context(nc.sbuf_tensor("v1", [128, 4224], F32))
        v2 = _ctx.enter_context(nc.sbuf_tensor("v2", [128, 4224], F32))
        v3 = _ctx.enter_context(nc.sbuf_tensor("v3", [128, 4224], F32))
        v4 = _ctx.enter_context(nc.sbuf_tensor("v4", [128, 4224], F32))
        pk = _ctx.enter_context(nc.sbuf_tensor("pk", [128, 4128], BF16))
        xramp = _ctx.enter_context(nc.sbuf_tensor("xramp", [128, 4128], F32))
        ki = _ctx.enter_context(nc.sbuf_tensor("ki", [128, 4224], mybir.dt.int32))
        dbgt = _ctx.enter_context(nc.sbuf_tensor("dbgt", [128, 512], F32))
        cwre = _ctx.enter_context(nc.sbuf_tensor("cwre", [128, 64], F32))
        cwim = _ctx.enter_context(nc.sbuf_tensor("cwim", [128, 64], F32))
        cw0 = _ctx.enter_context(nc.sbuf_tensor("cw0", [128, 64], F32))
        cw1 = _ctx.enter_context(nc.sbuf_tensor("cw1", [128, 64], F32))
        cw2 = _ctx.enter_context(nc.sbuf_tensor("cw2", [128, 64], F32))
        cwa = _ctx.enter_context(nc.sbuf_tensor("cwa", [128, 32], F32))
        cwb = _ctx.enter_context(nc.sbuf_tensor("cwb", [128, 32], F32))
        dbg0 = _ctx.enter_context(nc.sbuf_tensor("dbg0", [128, 32], F32))
        cre_sb = _ctx.enter_context(nc.sbuf_tensor("cre_sb", [128, 64], F32))
        cim_sb = _ctx.enter_context(nc.sbuf_tensor("cim_sb", [128, 64], F32))
        wrecol = _ctx.enter_context(nc.sbuf_tensor("wrecol", [128, 1], F32))
        wimcol = _ctx.enter_context(nc.sbuf_tensor("wimcol", [128, 1], F32))
        alw = _ctx.enter_context(nc.sbuf_tensor("alw", [128, 128], F32))
        allw = _ctx.enter_context(nc.sbuf_tensor("allw", [128, 128], F32))
        aldt = _ctx.enter_context(nc.sbuf_tensor("aldt", [128, 1], F32))
        alm = _ctx.enter_context(nc.sbuf_tensor("alm", [128, 128], F32))
        blk0 = _ctx.enter_context(nc.Block())

        st_s0in = ch.add("sync", lambda s: [
            s.dma_start(out=xramp[:], in_=p.xramp[:, :]),
            s.dma_start(out=cre_sb[:], in_=p.cB_re[:, :]),
            s.dma_start(out=cim_sb[:], in_=p.cB_im[:, :]),
            s.dma_start(out=wrecol[:], in_=p.wB_lwre[:, :]),
            s.dma_start(out=wimcol[:], in_=p.wB_wim[:, :]),
            s.dma_start(out=alw[:], in_=p.al_wim[:, :]),
            s.dma_start(out=allw[:], in_=p.al_lwre[:, :]),
            s.dma_start(out=aldt[:], in_=p.al_logdt_col[:, :]),
        ], deps=[], ndma=8, tag="s0in")

        ch.add("scalar", lambda s: s.activation(wrecol[:], wrecol[:], AF.Exp),
               deps=[st_s0in], tag="wre_e")
        st_wren = ch.add("vector", lambda v: v.tensor_single_scalar(
            wrecol[:], wrecol[:], -1.0, ALU.mult), tag="wre_n")

        def e_al1(v):
            return v.tensor_single_scalar(alm[:], allw[:], aldt[:, 0:1], ALU.add)
        ch.add("vector", e_al1, deps=[st_s0in], tag="al1")
        ch.add("scalar", lambda s: s.activation(alm[:], alm[:], AF.Exp),
               tag="al2")
        ch.add("scalar", lambda s: s.activation(aldt[:], aldt[:], AF.Exp),
               tag="al3")
        def e_al4(v):
            v.tensor_single_scalar(alm[:], alm[:], float(T), ALU.mult)
            v.tensor_single_scalar(allw[:], alw[:], aldt[:, 0:1], ALU.mult)
            return v.tensor_single_scalar(allw[:], allw[:], float(T), ALU.mult)
        ch.add("vector", e_al4, tag="al4")
        ch.add("scalar", lambda s: s.activation(alm[:], alm[:], AF.Exp,
                                                scale=-1.0), tag="al5")
        _sincos(ch, allw, alw, a_re, a_im, (slice(None), slice(0, 128)), "al", ki=ki)
        def e_al6(v):
            v.tensor_tensor(a_re[:], a_re[:], alm[:], ALU.mult)
            return v.tensor_tensor(a_im[:], a_im[:], alm[:], ALU.mult)
        ch.add("vector", e_al6, tag="al6")

        st_half_done = None
        for hf in range(2):
            lo = hf * 4096
            sl = (slice(None), slice(0, 4096))
            st_in = ch.add("sync", (lambda s, lo=lo: [
                s.dma_start(out=v0[:, 0:4096], in_=p.tA_logdt[:, lo:lo + 4096]),
                s.dma_start(out=v1[:, 0:4096], in_=p.tA_lwre[:, lo:lo + 4096]),
                s.dma_start(out=v2[:, 0:4096], in_=p.tA_wim[:, lo:lo + 4096]),
            ]), deps=[st_consts, st_half_done], ndma=3, tag=f"tAin{hf}")

            def eA1(v, sl=sl):
                return v.tensor_tensor(v3[sl], v0[sl], v1[sl], ALU.add)
            ch.add("vector", eA1, deps=[st_in], tag=f"tA1.{hf}")
            ch.add("scalar", lambda s, sl=sl: s.activation(v3[sl], v3[sl], AF.Exp),
                   tag=f"tA2.{hf}")
            def eA3(v, sl=sl):
                return v.tensor_single_scalar(v3[sl], v3[sl], tcol[:, 0:1], ALU.mult)
            ch.add("vector", eA3, tag=f"tA3.{hf}")
            ch.add("scalar", lambda s, sl=sl: s.activation(v3[sl], v3[sl], AF.Exp,
                                                           scale=-1.0),
                   tag=f"tA4.{hf}")
            ch.add("scalar", lambda s, sl=sl: s.activation(v0[sl], v0[sl], AF.Exp),
                   tag=f"tA5.{hf}")
            def eA6(v, sl=sl):
                v.tensor_tensor(v2[sl], v0[sl], v2[sl], ALU.mult)
                return v.tensor_single_scalar(v2[sl], v2[sl], tcol[:, 0:1], ALU.mult)
            ch.add("vector", eA6, tag=f"tA6.{hf}")
            _sincos(ch, v2, v4, v0, v1, sl, f"tA7.{hf}", ki=ki)
            def eA8(v, lo=lo, sl=sl):
                v.tensor_single_scalar(v0[sl], v0[sl], 2.0, ALU.mult)
                i8 = v.tensor_single_scalar(v1[sl], v1[sl], -2.0, ALU.mult)
                ch.gap(v, i8, "vector")
                def sv(t, r):
                    return t.ap()[:, lo:lo + 4096].rearrange(
                        "p (h r n) -> p h r n", r=2, n=32)[:, :, r, :]
                def sw(t, r):
                    return t.ap()[:, 0:4096].rearrange(
                        "p (h r n) -> p h r n", r=2, n=32)[:, :, r, :]
                v.tensor_tensor(sv(tabA, 0), sw(v3, 0), sw(v0, 0), ALU.mult)
                return v.tensor_tensor(sv(tabA, 1), sw(v3, 1), sw(v1, 1),
                                       ALU.mult)
            st_half_done = ch.add("vector", eA8, tag=f"tA8.{hf}")

        st_prev_p = None
        for hf in range(2):
            lo = hf * 4128
            sl = (slice(None), slice(0, 4128))
            st_in = ch.add("sync", (lambda s, lo=lo: [
                s.dma_start(out=v0[:, 0:4128], in_=p.tB_logdt[:, lo:lo + 4128]),
                s.dma_start(out=v1[:, 0:4128], in_=p.tB_lwre[:, lo:lo + 4128]),
                s.dma_start(out=v2[:, 0:4128], in_=p.tB_wim[:, lo:lo + 4128]),
            ]), deps=[st_s0in, st_half_done if hf == 0 else st_prev_p],
                ndma=3, tag=f"tBin{hf}")

            def eB1(v):
                return v.tensor_tensor(v3[sl], v0[sl], v1[sl], ALU.add)
            ch.add("vector", eB1, deps=[st_in], tag=f"tB1.{hf}")
            ch.add("scalar", lambda s: s.activation(v3[sl], v3[sl], AF.Exp),
                   tag=f"tB2.{hf}")
            ch.add("scalar", lambda s: s.activation(v0[sl], v0[sl], AF.Exp),
                   tag=f"tB3.{hf}")
            def eB4(v):
                v.tensor_tensor(v2[sl], v0[sl], v2[sl], ALU.mult)
                return v.tensor_tensor(v2[sl], v2[sl], xramp[sl], ALU.mult)
            ch.add("vector", eB4, tag=f"tB4.{hf}")
            _sincos(ch, v2, v4, v0, v1, sl, f"tB5.{hf}", ki=ki)
            def eB6(v):
                return v.tensor_tensor(v2[sl], v3[sl], xramp[sl], ALU.mult)
            ch.add("vector", eB6, tag=f"tB6.{hf}")
            ch.add("scalar", lambda s: s.activation(v4[sl], v2[sl], AF.Exp),
                   tag=f"tB7.{hf}")
            st_radp = ch.add("scalar",
                             lambda s: s.activation(v2[sl], v2[sl], AF.Exp,
                                                    scale=-1.0),
                             tag=f"tB8.{hf}")
            def eQ(v):
                v.tensor_single_scalar(v4[sl], v4[sl], 2.0, ALU.mult)
                for r0, r1 in ((0, 32), (64, 96)):
                    v.tensor_tensor(pk.ap()[r0:r1, 0:4128], v4.ap()[r0:r1, 0:4128],
                                    v0.ap()[r0:r1, 0:4128], ALU.mult)
                last = None
                for r0, r1 in ((32, 64), (96, 128)):
                    last = v.tensor_tensor(pk.ap()[r0:r1, 0:4128],
                                           v4.ap()[r0:r1, 0:4128],
                                           v1.ap()[r0:r1, 0:4128], ALU.mult)
                return last
            st_q = ch.add("vector", eQ,
                          deps=[ch.steps[-1]] + ([st_prev_po] if hf else []),
                          tag=f"tBQ.{hf}")
            st_qo = ch.add("sync", (lambda s, lo=lo: [
                s.dma_start(out=p.q_hbm[:, lo:lo + 4128], in_=pk[:, 0:4128])]),
                deps=[st_q], ndma=1, tag=f"tBQo.{hf}")

            def eAB(v):
                v.tensor_tensor(v4[sl], v2[sl], v0[sl], ALU.mult)
                return v.tensor_tensor(v2[sl], v2[sl], v1[sl], ALU.mult)
            st_ab = ch.add("vector", eAB, deps=[st_qo, st_radp], tag=f"tBAB.{hf}")

            def eCw(v, hf=hf):
                cs = slice(hf * 32, hf * 32 + 32)
                zz = slice(0, 32)
                A1 = v4.ap()[:, 0:4128].rearrange("p (h x) -> p h x", x=129)[:, :, 1]
                B1 = v2.ap()[:, 0:4128].rearrange("p (h x) -> p h x", x=129)[:, :, 1]
                ch.gap(v, v.tensor_copy(cw0[:, zz], A1), 'vector')
                ch.gap(v, v.tensor_copy(cw1[:, zz], B1), 'vector')
                ch.gap(v, v.tensor_single_scalar(cwa[:, zz], cw0[:, zz], 0.0, ALU.mult), 'vector')
                ch.gap(v, v.tensor_single_scalar(cwb[:, zz], cwa[:, zz], wimcol[:, 0:1], ALU.add), 'vector')
                ch.gap(v, v.tensor_single_scalar(cwa[:, zz], cwa[:, zz], wrecol[:, 0:1], ALU.add), 'vector')
                return v.tensor_single_scalar(cw0[:, zz], cw0[:, zz], 1.0,
                                              ALU.subtract)
            ch.add("vector", eCw, deps=[st_ab, st_wren], sync_all=True,
                   tag=f"tBcw.{hf}")

            def eCw2(v, hf=hf):
                cs = slice(hf * 32, hf * 32 + 32)
                zz = slice(0, 32)
                ch.gap(v, v.tensor_tensor(cw2[:, zz], cwa[:, zz], cwa[:, zz], ALU.mult), 'vector')
                ch.gap(v, v.tensor_tensor(cwre[:, zz], cwb[:, zz], cwb[:, zz], ALU.mult), 'vector')
                ch.gap(v, v.tensor_tensor(cw2[:, zz], cw2[:, zz], cwre[:, zz], ALU.add), 'vector')
                ch.gap(v, v.reciprocal(cw2[:, zz], cw2[:, zz]), 'vector')
                ch.gap(v, v.tensor_tensor(cwre[:, zz], cre_sb[:, cs], cw0[:, zz], ALU.mult), 'vector')
                ch.gap(v, v.tensor_tensor(dbg0[:, zz], cim_sb[:, cs], cw1[:, zz], ALU.mult), 'vector')
                ch.gap(v, v.tensor_tensor(cwre[:, zz], cwre[:, zz], dbg0[:, zz], ALU.subtract), 'vector')
                ch.gap(v, v.tensor_tensor(cwim[:, zz], cre_sb[:, cs], cw1[:, zz], ALU.mult), 'vector')
                ch.gap(v, v.tensor_tensor(dbg0[:, zz], cim_sb[:, cs], cw0[:, zz], ALU.mult), 'vector')
                ch.gap(v, v.tensor_tensor(cwim[:, zz], cwim[:, zz], dbg0[:, zz], ALU.add), 'vector')
                ch.gap(v, v.tensor_tensor(cw0[:, zz], cwre[:, zz], cwa[:, zz], ALU.mult), 'vector')
                ch.gap(v, v.tensor_tensor(dbg0[:, zz], cwim[:, zz], cwb[:, zz], ALU.mult), 'vector')
                ch.gap(v, v.tensor_tensor(cw0[:, zz], cw0[:, zz], dbg0[:, zz], ALU.add), 'vector')
                ch.gap(v, v.tensor_tensor(cw0[:, zz], cw0[:, zz], cw2[:, zz], ALU.mult), 'vector')
                ch.gap(v, v.tensor_tensor(cw1[:, zz], cwim[:, zz], cwa[:, zz], ALU.mult), 'vector')
                ch.gap(v, v.tensor_tensor(dbg0[:, zz], cwre[:, zz], cwb[:, zz], ALU.mult), 'vector')
                ch.gap(v, v.tensor_tensor(cw1[:, zz], cw1[:, zz], dbg0[:, zz], ALU.subtract), 'vector')
                ch.gap(v, v.tensor_tensor(cw1[:, zz], cw1[:, zz], cw2[:, zz], ALU.mult), 'vector')
                ch.gap(v, v.tensor_copy(cwre[:, zz], cw0[:, zz]), 'vector')
                ch.gap(v, v.tensor_copy(cwim[:, zz], cw1[:, zz]), 'vector')
                ch.gap(v, v.tensor_single_scalar(cw0[:, zz], cwre[:, zz], -1.0, ALU.mult), 'vector')
                return v.tensor_single_scalar(cw1[:, zz], cwim[:, zz], -1.0,
                                              ALU.mult)
            ch.add("vector", eCw2, deps="prev", sync_all=True, tag=f"tBcw2.{hf}")

            def ePrep(v):
                d0 = v0.ap()[:, 0:4128].rearrange("p (b x) -> p b x", x=129)
                d1 = v1.ap()[:, 0:4128].rearrange("p (b x) -> p b x", x=129)
                v.tensor_copy(d0[:, :, 0], cwre.ap()[:, 0:32])
                i1 = v.tensor_copy(d1[:, :, 0], cwim.ap()[:, 0:32])
                ch.gap(v, i1, "vector")
                k = 1
                while k < 129:
                    m = min(k, 129 - k)
                    v.tensor_copy(d0[:, :, k:k + m], d0[:, :, 0:m])
                    last = v.tensor_copy(d1[:, :, k:k + m], d1[:, :, 0:m])
                    ch.gap(v, last, "vector")
                    k *= 2
                return v.tensor_copy(d0[:, :, 0], d0[:, :, 0])
            ch.add("vector", ePrep, deps="prev", sync_all=True, tag=f"tBPr.{hf}")

            def eP(v):
                last = None
                for r0, r1 in ((0, 32), (64, 96)):
                    rr = slice(r0, r1)
                    xs = slice(0, 4128)
                    v.tensor_tensor(v3.ap()[rr, xs], v0.ap()[rr, xs],
                                    v4.ap()[rr, xs], ALU.mult)
                    v.tensor_tensor(pk.ap()[rr, xs], v1.ap()[rr, xs],
                                    v2.ap()[rr, xs], ALU.mult)
                    v.tensor_tensor(pk.ap()[rr, xs], v3.ap()[rr, xs],
                                    pk.ap()[rr, xs], ALU.subtract)
                for r0, r1 in ((32, 64), (96, 128)):
                    rr = slice(r0, r1)
                    xs = slice(0, 4128)
                    v.tensor_tensor(v3.ap()[rr, xs], v0.ap()[rr, xs],
                                    v2.ap()[rr, xs], ALU.mult)
                    v.tensor_tensor(pk.ap()[rr, xs], v1.ap()[rr, xs],
                                    v4.ap()[rr, xs], ALU.mult)
                    last = v.tensor_tensor(pk.ap()[rr, xs], v3.ap()[rr, xs],
                                           pk.ap()[rr, xs], ALU.add)
                return last
            st_p = ch.add("vector", eP, deps="prev", sync_all=True,
                          tag=f"tBP.{hf}")
            st_prev_p = st_p
            st_prev_po = ch.add("sync", (lambda s, lo=lo: [
                s.dma_start(out=p.p_hbm[:, lo:lo + 4128], in_=pk[:, 0:4128])]),
                deps=[st_p], ndma=1, tag=f"tBPo.{hf}")

        if upto == "s0":
            _finish_dump([
                (p.dbg_b.ap()[:, 0:8192], tabA.ap()[:, :]),
                (p.dbg_b.ap()[:, 8192:16448], p.p_hbm.ap()[:, :]),
                (p.dbg_b.ap()[:, 16448:24704], p.q_hbm.ap()[:, :]),
                (p.dbg_f.ap()[:, 0:4128], v4.ap()[:, 0:4128]),
                (p.dbg_f.ap()[:, 4128:8256], v2.ap()[:, 0:4128]),
                (p.dbg_f.ap()[:, 8256:8288], cwre.ap()[:, 0:32]),
                (p.dbg_f.ap()[:, 8320:8352], cwim.ap()[:, 0:32]),
                (p.dbg_f.ap()[:, 8384:8448], cre_sb.ap()[:, :]),
                (p.dbg_f.ap()[:, 8448:12576], v0.ap()[:, 0:4128]),
                (p.dbg_f.ap()[:, 12576:16704], v1.ap()[:, 0:4128]),
                (p.dbg_f.ap()[:, 16704:20832], v3.ap()[:, 0:4128]),
            ])
            ch.build_section(blk0)
            return nc
        with nc.named_scope("s0"):
            ch.build_section(blk0)

    ch.section_barrier()
    st_tabA_done = ch.steps[-1]
    with contextlib.ExitStack() as _ctx:
        ubf = _ctx.enter_context(nc.sbuf_tensor("ubf", [128, 3 * 1024], BF16))
        uf32 = _ctx.enter_context(nc.sbuf_tensor("uf32", [128, 2048], F32))
        zstg = _ctx.enter_context(nc.sbuf_tensor("zstg", [64, 2 * 1024], F32))
        zp0 = _ctx.enter_context(nc.psum_tensor([64, 1024], F32))
        zp1 = _ctx.enter_context(nc.psum_tensor([64, 1024], F32))
        blk1 = _ctx.enter_context(nc.Block())

        zps = (zp0, zp1)
        st_u, st_mm, st_ev, st_zo = {}, {}, {}, {}
        for g in range(32):
            bsl = slice((g % 3) * 1024, (g % 3) * 1024 + 1024)
            st_uf = ch.add("sync", (lambda sy, g=g: [
                sy.dma_start(out=uf32[:, (g % 2) * 1024:(g % 2) * 1024 + 1024],
                             in_=p.u_conv[:, g * 1024:(g + 1) * 1024])]),
                deps=[st_u.get(g - 2)], ndma=1, tag=f"u1f.{g}")
            st_u[g] = ch.add("vector", (lambda v, g=g, bsl=bsl: v.tensor_copy(
                ubf[:, bsl], uf32[:, (g % 2) * 1024:(g % 2) * 1024 + 1024])),
                deps=[st_uf, st_ev.get(g - 3)], tag=f"u1.{g}")
            zp = zps[g % 2]
            def emm(t, g=g, zp=zp):
                last = None
                for hh in range(4):
                    h = 4 * g + hh
                    base = (g % 3) * 1024 + hh * 256
                    last = t.matmul(
                        zp[:, hh * 256:(hh + 1) * 256],
                        tabA.ap()[:, h * 64:(h + 1) * 64],
                        ubf.ap()[:, base:base + 256],
                        start=True, stop=True)
                return last
            st_mm[g] = ch.add("tensor", emm,
                              deps=[st_u[g], st_tabA_done if g == 0 else None,
                                    st_ev.get(g - 2)],
                              tag=f"z_mm.{g}")
            ssl = slice((g % 2) * 1024, (g % 2) * 1024 + 1024)
            st_ev[g] = ch.add("scalar", (lambda s, zp=zp, ssl=ssl:
                              s.copy(zstg[:, ssl], zp[:, 0:1024])),
                              deps=[st_mm[g], st_zo.get(g - 2)], tag=f"z_ev.{g}")
            st_zo[g] = ch.add("sync", (lambda s, g=g, ssl=ssl: [
                s.dma_start(out=p.z_hbm[:, g * 1024:(g + 1) * 1024],
                            in_=zstg[:, ssl])]),
                deps=[st_ev[g]], ndma=1, tag=f"z_o.{g}")
        if upto == "s1":
            _finish_dump([(p.dbg_f.ap()[0:64, :], p.z_hbm.ap()[:, :])])
            ch.build_section(blk1)
            return nc
        with nc.named_scope("s1"):
            ch.build_section(blk1)
    stkA.close()

    ch.section_barrier()
    stkS = contextlib.ExitStack()
    smov = stkS.enter_context(nc.sbuf_tensor("smov", [64, HL * BC], BF16))
    with contextlib.ExitStack() as _ctx:
        zall = _ctx.enter_context(nc.sbuf_tensor("zall", [128, 16384], F32))
        ewstg = _ctx.enter_context(nc.sbuf_tensor("ewstg", [128, 4096], F32))
        ewcst = _ctx.enter_context(nc.sbuf_tensor("ewcst", [128, 4096],
                                                  BF16))
        s0t = _ctx.enter_context(nc.sbuf_tensor("s0", [128, 256], F32))
        s1t = _ctx.enter_context(nc.sbuf_tensor("s1", [128, 256], F32))
        rt0 = _ctx.enter_context(nc.sbuf_tensor("rt0", [128, 128], F32))
        rt1 = _ctx.enter_context(nc.sbuf_tensor("rt1", [128, 128], F32))
        rt2 = _ctx.enter_context(nc.sbuf_tensor("rt2", [128, 128], F32))
        tp0 = _ctx.enter_context(nc.psum_tensor([64, 512], F32))
        tp1 = _ctx.enter_context(nc.psum_tensor([64, 512], F32))
        blk2 = _ctx.enter_context(nc.Block())

        zlast = st_zo[31]
        zdeps = [st_zo[g] for g in range(26, 32)]
        st_zrb = []
        for rq in range(4):
            def erb(s, rq=rq):
                dms = []
                for rr in range(16):
                    r = rq * 16 + rr
                    src = p.z_hbm.ap().rearrange(
                        "o (h b c) -> o h b c", h=128, b=4, c=64)[r]
                    dstap = zall.ap().rearrange("h (b r2 c) -> h b r2 c",
                                                b=4, r2=64)[:, :, r, :]
                    dms.append(s.dma_start(out=dstap, in_=src))
                return dms
            st_zrb.append(ch.add("sync", erb, deps=zdeps, ndma=16,
                                 tag=f"zrb.{rq}"))

        def ezero(v):
            ap = smov.ap().rearrange("r (h b c) -> r h b c", h=128, b=4)[
                :, :, :, 0]
            return v.memset(ap, 0.0)
        st_sz = ch.add("vector", ezero, deps=[], tag="smov_zero")

        def replane(t, r):
            return t.ap().rearrange("h (b r2 n) -> h b r2 n", b=4, r2=2)[
                :, :, r, :]
        def zplane(c, r):
            return zall.ap().rearrange(
                "h (b rr n c) -> h b rr n c", b=4, rr=2, n=32)[:, :, r, :, c]
        def aplane(t):
            return t.ap().rearrange("h (b n) -> h b n", b=4)

        st_vec, st_tr, st_cp = {}, {}, {}
        st_init = ch.add("vector", lambda v: v.memset(s1t[:], 0.0),
                         deps=[], tag="s_init")
        for c in range(NCHUNK):
            cur = (s0t, s1t)[c % 2]
            prv = (s0t, s1t)[1 - c % 2]
            def erec(v, c=c, cur=cur, prv=prv):
                pre, pim = replane(prv, 0), replane(prv, 1)
                cre, cim = replane(cur, 0), replane(cur, 1)
                ar, ai = aplane(a_re), aplane(a_im)
                t0, t1, t2 = (rt0.ap().rearrange("h (b n) -> h b n", b=4),
                              rt1.ap().rearrange("h (b n) -> h b n", b=4),
                              rt2.ap().rearrange("h (b n) -> h b n", b=4))
                i = v.tensor_tensor(t0, pre, ar, ALU.mult)
                i = v.tensor_tensor(t1, pim, ai, ALU.mult)
                ch.gap(v, i, "vector")
                i = v.tensor_tensor(t0, t0, t1, ALU.add)
                i = v.tensor_tensor(t1, pim, ar, ALU.mult)
                i = v.tensor_tensor(t2, pre, ai, ALU.mult)
                ch.gap(v, i, "vector")
                i = v.tensor_tensor(t1, t1, t2, ALU.subtract)
                ch.gap(v, i, "vector")
                v.tensor_tensor(cre, t0, zplane(c, 0), ALU.add)
                return v.tensor_tensor(cim, t1, zplane(c, 1), ALU.add)
            deps = [st_init if c == 0 else st_vec[c - 1]]
            if c == 0:
                deps += st_zrb
            if c >= 2:
                deps.append(st_tr.get(c - 2))
            st_vec[c] = ch.add("vector", erec, deps=deps, sync_all=True,
                               tag=f"rec.{c}")

            if c < NCHUNK - 1:
                tp = (tp0, tp1)[c % 2]
                def etr(t, cur=cur, tp=tp):
                    last = None
                    for b in range(4):
                        last = t.transpose(tp[:, b * 128:(b + 1) * 128],
                                           cur.ap()[:, b * 64:(b + 1) * 64],
                                           eye_f[:])
                    return last
                st_tr[c] = ch.add("tensor", etr,
                                  deps=[st_vec[c], st_cp.get(c - 2)],
                                  tag=f"tr.{c}")
                def ecp(s, c=c, tp=tp):
                    last = None
                    for b in range(4):
                        dst = smov.ap().rearrange("r (h b2 c2) -> r h b2 c2",
                                                  h=128, b2=4)[:, :, b, c + 1]
                        last = s.copy(dst, tp[:, b * 128:(b + 1) * 128])
                    return last
                st_cp[c] = ch.add("scalar", ecp, deps=[st_tr[c], st_sz],
                                  tag=f"cp.{c}")

        st_ewo = {}
        ewblocks = ([(p.ew1_shard, p.cc_e1in, rb, 2048) for rb in range(6)]
                    + [(p.ew2_shard, p.cc_e2in, rb, 1024) for rb in range(12)])
        for k, (esrc, edst, rb, w) in enumerate(ewblocks):
            ssl = slice((k % 2) * 2048, (k % 2) * 2048 + w)
            sti = ch.add("sync", (lambda s, esrc=esrc, rb=rb, ssl=ssl: [
                s.dma_start(out=ewstg[:, ssl],
                            in_=esrc.ap()[rb * 128:(rb + 1) * 128, :])]),
                deps=[st_ewo.get(k - 2)], ndma=1, tag=f"ewi.{k}")
            stc = ch.add("scalar", (lambda s, ssl=ssl: s.copy(
                ewcst[:, ssl], ewstg[:, ssl])),
                deps=[sti, st_ewo.get(k - 2)], tag=f"ewc.{k}")
            st_ewo[k] = ch.add("sync", (lambda s, edst=edst, rb=rb, ssl=ssl: [
                s.dma_start(out=edst.ap()[rb * 128:(rb + 1) * 128, :],
                            in_=ewcst[:, ssl])]), deps=[stc], ndma=1,
                tag=f"ewo.{k}")

        def eag1(gp):
            return gp.collective_compute(
                "AllGather", ALU.bypass, replica_groups=[list(range(NCORE))],
                ins=[p.cc_e1in.ap().opt()], outs=[p.cc_e1out.ap().opt()])
        ch.add("gpsimd", eag1, deps=[st_ewo[k] for k in range(6)],
               tag="ew1ag")

        def eag2(gp):
            return gp.collective_compute(
                "AllGather", ALU.bypass, replica_groups=[list(range(NCORE))],
                ins=[p.cc_e2in.ap().opt()], outs=[p.cc_e2out.ap().opt()])
        ch.add("gpsimd", eag2, deps=[st_ewo[k] for k in range(6, 18)],
               tag="ew2ag")

        if upto == "s2":
            _finish_dump([(p.dbg_b.ap()[0:64, 0:32768], smov.ap()[:, :])])
            ch.build_section(blk2)
            return nc
        with nc.named_scope("s2"):
            ch.build_section(blk2)

    ch.section_barrier()
    last_cp = st_cp[NCHUNK - 2]
    last_vec = st_vec[NCHUNK - 1]
    with contextlib.ExitStack() as _ctx:
        punp = _ctx.enter_context(nc.sbuf_tensor("punp", [64, 16512], BF16))
        qunp = _ctx.enter_context(nc.sbuf_tensor("qunp", [64, 16512], BF16))
        gsb = _ctx.enter_context(nc.sbuf_tensor("gsb", [128, 16384], BF16))
        ubf2 = _ctx.enter_context(nc.sbuf_tensor("ubf2", [128, 3 * 1024], BF16))
        uf32b = _ctx.enter_context(nc.sbuf_tensor("uf32b", [128, 2048], F32))
        ktb = _ctx.enter_context(nc.sbuf_tensor("kt", [128, 2 * 128], BF16))
        ked = _ctx.enter_context(nc.sbuf_tensor("ked", [128, 128], BF16))
        ktdbg = _ctx.enter_context(nc.sbuf_tensor("ktdbg", [128, 256], BF16))
        kp0 = _ctx.enter_context(nc.psum_tensor([128, 512], F32))
        kp1 = _ctx.enter_context(nc.psum_tensor([128, 512], F32))
        cv0 = _ctx.enter_context(nc.psum_tensor([128, 1024], F32))
        cv1 = _ctx.enter_context(nc.psum_tensor([128, 1024], F32))
        blk3 = _ctx.enter_context(nc.Block())

        st_pq = ch.add("sync", lambda s: [
            s.dma_start(
                out=punp.ap().rearrange("r (h2 x) -> r h2 x", x=129)[:, 0::2, :],
                in_=p.p_hbm.ap()[0:64, :].rearrange("r (h2 x) -> r h2 x", x=129)),
            s.dma_start(
                out=punp.ap().rearrange("r (h2 x) -> r h2 x", x=129)[:, 1::2, :],
                in_=p.p_hbm.ap()[64:128, :].rearrange("r (h2 x) -> r h2 x",
                                                      x=129)),
            s.dma_start(
                out=qunp.ap().rearrange("r (h2 x) -> r h2 x", x=129)[:, 0::2, :],
                in_=p.q_hbm.ap()[0:64, :].rearrange("r (h2 x) -> r h2 x", x=129)),
            s.dma_start(
                out=qunp.ap().rearrange("r (h2 x) -> r h2 x", x=129)[:, 1::2, :],
                in_=p.q_hbm.ap()[64:128, :].rearrange("r (h2 x) -> r h2 x",
                                                      x=129)),
        ], deps=ch.frontier(), ndma=4, tag="pq_in")

        kps = (kp0, kp1)
        cvs = (cv0, cv1)
        st_u2, st_kg, st_msk, st_cv, st_gv = {}, {}, {}, {}, {}
        glast_flush = []
        for g in range(32):
            bsl = slice((g % 3) * 1024, (g % 3) * 1024 + 1024)
            st_uf2 = ch.add("sync", (lambda sy, g=g: [
                sy.dma_start(out=uf32b[:, (g % 2) * 1024:(g % 2) * 1024 + 1024],
                             in_=p.u_conv[:, g * 1024:(g + 1) * 1024])]),
                deps=[st_u2.get(g - 2)], ndma=1, tag=f"u2f.{g}")
            st_u2[g] = ch.add("vector", (lambda v, g=g, bsl=bsl: v.tensor_copy(
                ubf2[:, bsl], uf32b[:, (g % 2) * 1024:(g % 2) * 1024 + 1024])),
                deps=[st_uf2, st_gv.get(g - 3)], tag=f"u2.{g}")
            kp = kps[g % 2]
            def ekg(t, g=g, kp=kp):
                last = None
                for hh in range(4):
                    h = 4 * g + hh
                    last = t.matmul(
                        kp[:, hh * 128:(hh + 1) * 128],
                        qunp.ap()[:, h * 129:h * 129 + 128],
                        punp.ap()[:, h * 129:h * 129 + 128],
                        start=True, stop=True)
                return last
            st_kg[g] = ch.add("tensor", ekg,
                              deps=[st_pq if g == 0 else None,
                                    st_msk.get(g - 2)],
                              tag=f"kg.{g}")
            cv = cvs[g % 2]
            prev_cv_consumer = st_gv.get(g - 2)
            substeps = []
            for hh in range(4):
                def emsk1(v, g=g, hh=hh, kp=kp):
                    h = 4 * g + hh
                    kslot = ktb.ap()[:, (hh % 2) * 128:(hh % 2) * 128 + 128]
                    v.tensor_tensor(kslot, kp[:, hh * 128:(hh + 1) * 128],
                                    triu_f[:], ALU.mult)
                    ch.gap(v, v.tensor_single_scalar(ked[:], eye_b[:],
                                                     drep[:, h:h + 1], ALU.mult),
                           "vector")
                    i = v.tensor_tensor(kslot, kslot, ked[:], ALU.add)
                    ch.gap(v, i, "vector")
                    return v.tensor_copy(ked[0:1, 0:1], kslot[0:1, 0:1])
                dmsk = [st_kg[g]]
                if substeps:
                    dmsk.append(substeps[-1])
                if hh >= 2:
                    dmsk.append(substeps[2 * (hh - 2) + 1])
                stm = ch.add("vector", emsk1, deps=dmsk, tag=f"msk.{g}.{hh}")
                if upto == "s3" and hh == 0 and g in (0, 8):
                    def estash(v, g=g):
                        return v.tensor_copy(
                            ktdbg.ap()[:, (0 if g == 0 else 128):(128 if g == 0 else 256)],
                            ktb.ap()[:, 0:128])
                    stm = ch.add("vector", estash, deps=[stm], sync_all=True,
                                 tag=f"ktstash.{g}")
                def ecv1(t, g=g, hh=hh, cv=cv):
                    h = 4 * g + hh
                    kslot = ktb.ap()[:, (hh % 2) * 128:(hh % 2) * 128 + 128]
                    csl = cv[:, (hh % 2) * 512:(hh % 2) * 512 + 256]
                    base = (g % 3) * 1024 + hh * 256
                    if variant == "noaout":
                        return t.matmul(csl, kslot,
                                        ubf2.ap()[:, base:base + 256],
                                        start=True, stop=True)
                    if variant == "uonly":
                        return t.matmul(csl, eye_b[:],
                                        ubf2.ap()[:, base:base + 256],
                                        start=True, stop=True)
                    t.matmul(csl, kslot, ubf2.ap()[:, base:base + 256],
                             start=True, stop=False)
                    return t.matmul(
                        csl, punp.ap()[:, h * 129 + 1:h * 129 + 129],
                        smov.ap()[:, h * 256:(h + 1) * 256],
                        start=False, stop=True)
                dcv = [stm, st_u2[g], last_cp if g == 0 else None,
                       last_vec if g == 0 else None]
                if hh >= 2:
                    dcv.append(st_gv[(g, hh - 2)])
                elif prev_cv_consumer is not None:
                    dcv.append(prev_cv_consumer)
                stc = ch.add("tensor", ecv1, deps=dcv, tag=f"cv.{g}.{hh}")
                def egv1(s, g=g, hh=hh, cv=cv):
                    h = 4 * g + hh
                    csl = cv[:, (hh % 2) * 512:(hh % 2) * 512 + 256]
                    if variant == "uonly":
                        dst0 = gsb.ap().rearrange("t (bc hl) -> t bc hl",
                                                  hl=64)[:, :, h % 64]
                        return s.activation(dst0, csl, AF.Copy)
                    dst = gsb.ap().rearrange("t (bc hl) -> t bc hl", hl=64)[
                        :, :, h % 64]
                    return s.activation(dst, csl, AF.Gelu)
                stg = ch.add("scalar", egv1, deps=[stc], tag=f"gv.{g}.{hh}")
                st_gv[(g, hh)] = stg
                substeps += [stm, stc]
                st_msk[g] = stm
            st_gv[g] = st_gv[(g, 3)]
            if g in (15, 31):
                half = 0 if g == 15 else 1
                def efl(s, half=half):
                    src = gsb.ap().rearrange("t (j c2 hl) -> t j c2 hl",
                                             j=8, c2=32)
                    dst = p.cc_a2in.ap().rearrange(
                        "(j t) (c2 hl) -> j t c2 hl", t=128, hl=128)[
                        :, :, :, half * 64:half * 64 + 64]
                    return [s.dma_start(out=dst[j], in_=src[:, j])
                            for j in range(8)]
                fdeps = [st_gv[(gg, hhh)] for gg in range(half * 16, g + 1)
                         for hhh in range(4)]
                glast_flush.append(ch.add("sync", efl, deps=fdeps[-8:] + [
                    st_gv[(half * 16, 0)]], ndma=8, tag=f"gflush.{half}"))
                glast_flush[-1].deps = [st_gv[(g, 3)]]

        if upto == "s3":
            def edpq(gp):
                return [gp.dma_start(out=p.dbg_f.ap()[0:64, 0:16512],
                                     in_=punp.ap()[:, :]),
                        gp.dma_start(out=p.dbg_f.ap()[64:128, 0:16512],
                                     in_=qunp.ap()[:, :]),
                        gp.dma_start(out=p.dbg_f.ap()[:, 20000:20256],
                                     in_=ktdbg.ap()[:, :])]
            ch.add("gpsimd", edpq, deps=ch.frontier(), ndma=3, tag="dpq")
            _finish_dump([(p.dbg_b.ap()[:, 0:32768], p.cc_gin.ap()[:, :])])
            ch.build_section(blk3)
            return nc

        def eag(gp):
            with aot_env.allow_ici_all_to_all():
                return gp.collective_compute(
                    "AllToAll", ALU.bypass,
                    replica_groups=[list(range(NCORE))],
                    ins=[p.cc_a2in.ap().opt()], outs=[p.cc_a2out.ap().opt()])
        st_ag = ch.add("gpsimd", eag, deps=glast_flush, tag="g_a2a")
        p.st_ag = st_ag

        if phase1_only:
            def edbg(s):
                return [s.dma_start(out=p.dbg_g[:, :], in_=p.cc_a2out[:, :])]
            st_dbg = ch.add("sync", edbg, deps=[st_ag], ndma=1, tag="dbg_g")
            ch.add("gpsimd", lambda gp: gp.engine_nop(), deps=[st_dbg],
                   tag="fin")
        with nc.named_scope("s3"):
            ch.build_section(blk3)
    stkS.close()

    if phase1_only:
        return nc

    ch.section_barrier()
    stkG = contextlib.ExitStack()
    g_mov = stkG.enter_context(nc.sbuf_tensor("g_mov", [128, 32768], BF16))
    graw = stkG.enter_context(nc.sbuf_tensor("graw", [128, 8192], BF16))
    wT_sb = stkG.enter_context(nc.sbuf_tensor("wT_sb", [128, 8192], BF16))
    wstg = stkG.enter_context(nc.sbuf_tensor("wstg", [128, 2048], F32))
    with contextlib.ExitStack() as _ctx:
        M0 = _ctx.enter_context(nc.sbuf_tensor("M0", [128, 8192], BF16))
        M1 = _ctx.enter_context(nc.sbuf_tensor("M1", [128, 8192], BF16))
        M2 = _ctx.enter_context(nc.sbuf_tensor("M2", [128, 8192], BF16))
        M3 = _ctx.enter_context(nc.sbuf_tensor("M3", [128, 8192], BF16))
        ostg = _ctx.enter_context(nc.sbuf_tensor("ostg", [128, 4 * 512], BF16))
        gstg = _ctx.enter_context(nc.sbuf_tensor("gstg", [128, 2 * 1024], F32))
        cchk = _ctx.enter_context(nc.sbuf_tensor("cchk", [128, 2 * 2048], BF16))
        schk = _ctx.enter_context(nc.sbuf_tensor("schk", [128, 2 * 512], BF16))
        ceye = _ctx.enter_context(nc.sbuf_tensor("ceye", [128, 1536], BF16))
        pp0 = _ctx.enter_context(nc.psum_tensor([128, 512], F32))
        pp1 = _ctx.enter_context(nc.psum_tensor([128, 512], F32))
        pp2 = _ctx.enter_context(nc.psum_tensor([128, 512], F32))
        gtp0 = _ctx.enter_context(nc.psum_tensor([128, 512], BF16))
        gtp1 = _ctx.enter_context(nc.psum_tensor([128, 512], BF16))
        blk4 = _ctx.enter_context(nc.Block())

        st_wc = {}
        for hb in range(8):
            wsl = slice((hb % 2) * 1024, (hb % 2) * 1024 + 1024)
            sti = ch.add("sync", (lambda s, hb=hb, wsl=wsl: [
                s.dma_start(out=wstg[:, wsl],
                            in_=p.w_outT.ap()[hb * 128:(hb + 1) * 128, :])]),
                deps=[st_wc.get(hb - 2)], ndma=1, tag=f"w_i{hb}")
            st_wc[hb] = ch.add("scalar", (lambda s, hb=hb, wsl=wsl:
                               s.copy(wT_sb[:, hb * 1024:(hb + 1) * 1024],
                                      wstg[:, wsl])),
                               deps=[sti], tag=f"w_c{hb}")
        p.st_wT = st_wc[7]

        gtps = (gtp0, gtp1)
        st_gr, st_gtr, st_gev = {}, {}, {}
        for isl in range(8):
            gsl = slice((isl % 2) * 4096, (isl % 2) * 4096 + 4096)
            st_gr[isl] = ch.add("sync", (lambda s, isl=isl, gsl=gsl: [
                s.dma_start(out=graw[:, gsl],
                            in_=p.cc_a2out.ap()[
                                isl * 128:(isl + 1) * 128, 0:4096])]),
                deps=[p.st_ag, st_gev.get((isl - 2, 7))], ndma=1,
                tag=f"g_i{isl}")
            for g4 in range(8):
                gt = gtps[g4 % 2]

                def etr(t, isl=isl, g4=g4, gt=gt, gsl=gsl):
                    last = None
                    for j in range(4):
                        c2 = g4 * 4 + j
                        last = t.transpose(
                            gt[:, j * 128:(j + 1) * 128],
                            graw.ap()[:, gsl.start + c2 * 128:
                                      gsl.start + c2 * 128 + 128],
                            eye_b[:])
                    return last
                st_gtr[(isl, g4)] = ch.add(
                    "tensor", etr,
                    deps=[st_gr[isl], st_gev.get((isl, g4 - 2)),
                          st_gev.get((isl - 1, g4 + 6))],
                    tag=f"g_t{isl}.{g4}")
                st_gev[(isl, g4)] = ch.add(
                    "scalar", (lambda s, isl=isl, g4=g4, gt=gt:
                               s.copy(g_mov[:, isl * 4096 + g4 * 512:
                                            isl * 4096 + g4 * 512 + 512],
                                      gt[:, :])),
                    deps=[st_gtr[(isl, g4)]], tag=f"g_e{isl}.{g4}")
        p.g_mov_done = st_gev[(7, 7)]

        st_ce = ch.add("sync", lambda s: [
            s.dma_start(out=ceye[:], in_=p.ceye[:, :])], deps=[], ndma=1,
            tag="ceye")

        st_gc = {}
        for ori, (gsrc, gdst, mb) in enumerate(
                ((p.gens_a, p.gA_hbm, M0), (p.gens_at, p.gAt_hbm, M1))):
            for cb in range(8):
                sl = slice((cb % 2) * 1024, (cb % 2) * 1024 + 1024)
                sti = ch.add("sync", (lambda s, gsrc=gsrc, cb=cb, sl=sl: [
                    s.dma_start(out=gstg[:, sl],
                                in_=gsrc.ap()[cb * 128:(cb + 1) * 128, :])]),
                    deps=[st_gc.get((ori, cb - 2))], ndma=1,
                    tag=f"gin.{ori}.{cb}")
                stc = ch.add("scalar", (lambda s, mb=mb, cb=cb, sl=sl:
                             s.copy(mb[:, cb * 1024:(cb + 1) * 1024],
                                    gstg[:, sl])),
                    deps=[sti], tag=f"gcst.{ori}.{cb}")
                st_gc[(ori, cb)] = ch.add("sync", (
                    lambda s, gdst=gdst, mb=mb, cb=cb: [
                        s.dma_start(out=gdst.ap()[cb * 128:(cb + 1) * 128, :],
                                    in_=mb[:, cb * 1024:(cb + 1) * 1024])]),
                    deps=[stc], ndma=1, tag=f"gout.{ori}.{cb}")
        st_gens = [st_gc[(0, 7)], st_gc[(1, 7)], st_gc[(0, 6)], st_gc[(1, 6)]]

        def mat_load(buf, hap, deps, tag):
            if callable(getattr(hap, "ap", None)):
                hap = hap.ap()
            return ch.add("sync", (lambda s, buf=buf, hap=hap: [
                s.dma_start(out=buf.ap().rearrange("q (dk n) -> q dk n",
                                                   n=1024),
                            in_=hap.rearrange("(dk q) n -> q dk n", q=128))]),
                deps=list(deps), ndma=1, tag=tag)

        def rowmaj(t):
            return t.ap().rearrange("(dk q) n -> q dk n", q=128)

        def dbg3(lo):
            return p.dbg_b.ap()[:, lo:lo + 8192].rearrange(
                "q (dk n) -> q dk n", n=1024)

        prev_ev = []

        def emit_product(xb, yb, out_buf=None, out_hbm=None, addh=None,
                         deps=(), tag=""):
            entry_prev = list(prev_ev)
            last_mm, st_ev, st_od = {}, {}, {}
            for ob in range(8):
                for nh in range(2):
                    gi = 2 * ob + nh
                    pp = (pp0, pp1)[gi % 2]
                    sta = None
                    asl = slice(nh * 512, nh * 512 + 512)
                    if addh is not None:
                        sta = ch.add("sync", (
                            lambda s, ob=ob, nh=nh, asl=asl: [
                                s.dma_start(
                                    out=schk[:, asl],
                                    in_=addh.ap()[ob * 128:(ob + 1) * 128,
                                                  nh * 512:(nh + 1) * 512])]),
                            deps=(list(deps) + [last_mm.get(2 * (ob - 1) + nh)]
                                  + (entry_prev if ob < 1 else [])),
                            ndma=1, tag=f"{tag}_a{ob}.{nh}")

                    def emm(t, ob=ob, nh=nh, pp=pp, asl=asl,
                            has_add=(addh is not None)):
                        last = None
                        for dk in range(8):
                            last = t.matmul(
                                pp[:, :],
                                xb.ap()[:, dk * 1024 + ob * 128:
                                        dk * 1024 + ob * 128 + 128],
                                yb.ap()[:, dk * 1024 + nh * 512:
                                        dk * 1024 + nh * 512 + 512],
                                start=(dk == 0),
                                stop=(dk == 7 and not has_add))
                        if has_add:
                            last = t.matmul(pp[:, :], ceye.ap()[:, 0:128],
                                            schk.ap()[:, asl],
                                            start=False, stop=True)
                        return last
                    mdeps = [last_mm.get(gi - 2), st_ev.get(gi - 2)]
                    if gi < 2:
                        mdeps += entry_prev + list(deps)
                    if sta is not None:
                        mdeps.append(sta)
                    stm = ch.add("tensor", emm, deps=mdeps,
                                 tag=f"{tag}_mm{ob}.{nh}")
                    last_mm[gi] = stm
                    osl = slice((gi % 4) * 512, (gi % 4) * 512 + 512)
                    if out_buf is not None:
                        ste = ch.add("scalar", (
                            lambda s, pp=pp, ob=ob, nh=nh: s.copy(
                                out_buf[:, ob * 1024 + nh * 512:
                                        ob * 1024 + nh * 512 + 512],
                                pp[:, :])),
                            deps=[stm], tag=f"{tag}_ev{ob}.{nh}")
                        st_ev[gi] = ste
                        if out_hbm is not None:
                            st_od[gi] = ch.add("sync", (
                                lambda s, ob=ob, nh=nh: [
                                    s.dma_start(
                                        out=out_hbm.ap()[
                                            ob * 128:(ob + 1) * 128,
                                            nh * 512:(nh + 1) * 512],
                                        in_=out_buf[:, ob * 1024 + nh * 512:
                                                    ob * 1024 + nh * 512
                                                    + 512])]),
                                deps=[ste], ndma=1, tag=f"{tag}_o{ob}.{nh}")
                    else:
                        ste = ch.add("scalar", (lambda s, pp=pp, osl=osl:
                                     s.copy(ostg[:, osl], pp[:, :])),
                                     deps=[stm, st_od.get(gi - 4)],
                                     tag=f"{tag}_ev{ob}.{nh}")
                        st_ev[gi] = ste
                        st_od[gi] = ch.add("sync", (
                            lambda s, ob=ob, nh=nh, osl=osl: [
                                s.dma_start(out=out_hbm.ap()[
                                    ob * 128:(ob + 1) * 128,
                                    nh * 512:(nh + 1) * 512],
                                    in_=ostg[:, osl])]),
                            deps=[ste], ndma=1, tag=f"{tag}_o{ob}.{nh}")
            prev_ev[:] = [st_ev[14], st_ev[15]]
            return ([st_ev[j] for j in range(16)]
                    + [st_od[j] for j in range(16) if j in st_od])

        dA2 = emit_product(M1, M0, out_buf=M2, out_hbm=p.exA2,
                           deps=st_gens, tag="pA2")
        dA2t = emit_product(M0, M1, out_buf=M3, deps=st_gens, tag="pA2t")
        dA3 = emit_product(M3, M0, out_hbm=p.exA3, deps=dA2t, tag="pA3")
        dA4 = emit_product(M3, M2, out_hbm=p.exA4, deps=dA2t + dA2,
                           tag="pA4")
        dA4t = emit_product(M2, M3, out_buf=M0, deps=dA2 + dA2t + dA3,
                            tag="pA4t")

        combo_deps = dA2 + dA3 + dA4 + dA4t + st_gens
        st_cmb = {}
        srcs_chunks = (p.gA_hbm, p.exA2, p.exA3, p.exA4)
        for rb in range(8):
            for nh in range(2):
                ci = rb * 2 + nh
                csl = slice((ci % 2) * 2048, (ci % 2) * 2048 + 2048)
                st_ci = ch.add("sync", (lambda s, rb=rb, nh=nh, csl=csl: [
                    s.dma_start(out=cchk[:, csl.start + j * 512:
                                         csl.start + j * 512 + 512],
                                in_=srcs_chunks[j].ap()[
                                    rb * 128:(rb + 1) * 128,
                                    nh * 512:(nh + 1) * 512])
                    for j in range(4)]),
                    deps=combo_deps + [st_cmb.get(("m", ci - 2))], ndma=4,
                    tag=f"cmb_i{ci}")
                diag = (rb // 4 == nh)
                doff = (rb % 4) * 128

                def ecmm(t, csl=csl, diag=diag, doff=doff):
                    def A(j):
                        return cchk.ap()[:, csl.start + j * 512:
                                         csl.start + j * 512 + 512]

                    def ident(j):
                        return ceye.ap()[:, j * 128:(j + 1) * 128]
                    last = None
                    for pp, terms, icf in (
                            (pp0, ((0, 0), (1, 1), (2, 2)), 0),
                            (pp1, ((4, 0), (5, 1), (6, 2)), 3),
                            (pp2, ((8, 0), (9, 1), (10, 2), (11, 3)), 7)):
                        for j, (cf, aj) in enumerate(terms):
                            last = t.matmul(pp[:, :], ident(cf), A(aj),
                                            start=(j == 0),
                                            stop=(j == len(terms) - 1
                                                  and not diag))
                        if diag:
                            last = t.matmul(pp[:, doff:doff + 128],
                                            ident(icf), eye_b[:],
                                            start=False, stop=True)
                    return last
                mdeps = [st_ci, st_ce, st_cmb.get(("e", ci - 1))]
                if ci < 1:
                    mdeps += prev_ev
                stm = ch.add("tensor", ecmm, deps=mdeps, tag=f"cmb_m{ci}")
                st_cmb[("m", ci)] = stm

                def ecev(s, ci=ci):
                    s.copy(ostg[:, 0:512], pp0[:, :])
                    s.copy(ostg[:, 512:1024], pp1[:, :])
                    return s.copy(ostg[:, 1024:1536], pp2[:, :])
                ste = ch.add("scalar", ecev,
                             deps=[stm, st_cmb.get(("o", ci - 1))],
                             tag=f"cmb_e{ci}")
                st_cmb[("e", ci)] = ste
                sto = ch.add("sync", (lambda s, rb=rb, nh=nh: [
                    s.dma_start(out=dst.ap()[rb * 128:(rb + 1) * 128,
                                             nh * 512:(nh + 1) * 512],
                                in_=ostg[:, j * 512:(j + 1) * 512])
                    for j, dst in enumerate((p.exS0, p.exS1, p.exS2))]),
                    deps=[ste], ndma=3, tag=f"cmb_o{ci}")
                st_cmb[("o", ci)] = sto
        prev_ev[:] = [st_cmb[("e", 15)]]
        combo_done = [st_cmb[("o", ci)] for ci in range(16)]

        st_ldS2 = mat_load(M1, p.exS2, deps=combo_done, tag="ldS2")
        dW = emit_product(M0, M1, out_buf=M2, addh=p.exS1,
                          deps=combo_done + dA4t + [st_ldS2], tag="pW")
        dU = emit_product(M0, M2, out_hbm=p.cc_Ein, addh=p.exS0,
                          deps=dW + combo_done, tag="pU")

        def eeag(gp):
            return gp.collective_compute(
                "AllGather", ALU.bypass, replica_groups=[list(range(NCORE))],
                ins=[p.cc_Ein.ap().opt()], outs=[p.cc_Eout.ap().opt()])
        st_eag = ch.add("gpsimd", eeag, deps=dU + [p.st_ag], tag="e_ag")

        st_lE1 = mat_load(M1, p.cc_Eout.ap()[1024:2048, :],
                          deps=[st_eag] + dU, tag="lE1")
        st_lE0 = mat_load(M2, p.cc_Eout.ap()[0:1024, :],
                          deps=[st_eag] + dU, tag="lE0")
        dP1 = emit_product(M1, M2, out_buf=M3,
                           deps=[st_lE1, st_lE0], tag="pP1")
        st_lE2 = mat_load(M0, p.cc_Eout.ap()[2048:3072, :],
                          deps=[st_eag] + dU, tag="lE2")
        st_lE3 = mat_load(M1, p.cc_Eout.ap()[3072:4096, :],
                          deps=dP1, tag="lE3")
        dP2 = emit_product(M0, M1, out_buf=M2,
                           deps=[st_lE2, st_lE3] + dP1, tag="pP2")
        dR = emit_product(M3, M2, out_hbm=p.exR, deps=dP2, tag="pR")

        if upto == "s4":
            _finish_dump([
                (dbg3(0), rowmaj(p.exR)),
                (dbg3(8192),
                 p.cc_Eout.ap()[0:1024, :].rearrange("(dk q) n -> q dk n",
                                                     q=128)),
                (dbg3(16384), rowmaj(p.exA2)),
                (dbg3(24576), rowmaj(p.exS1)),
            ])
            ch.build_section(blk4)
            return nc
        with nc.named_scope("s4"):
            ch.build_section(blk4)

    ch.section_barrier()
    with contextlib.ExitStack() as _ctx:
        u_t = _ctx.enter_context(nc.sbuf_tensor("u_t", [128, 1024], F32))
        v_f = _ctx.enter_context(nc.sbuf_tensor("v_f", [128, 4096], F32))
        x1f = _ctx.enter_context(nc.sbuf_tensor("x1f", [128, 4096], F32))
        vb = _ctx.enter_context(nc.sbuf_tensor("vb", [128, 1024], BF16))
        sqt = _ctx.enter_context(nc.sbuf_tensor("sqt", [128, 1024], BF16))
        x1bf = _ctx.enter_context(nc.sbuf_tensor("x1bf", [128, 1024], BF16))
        xnf = _ctx.enter_context(nc.sbuf_tensor("xnf", [128, 1024], F32))
        xnbf = _ctx.enter_context(nc.sbuf_tensor("xnbf", [128, 1024], BF16))
        gw_sb = _ctx.enter_context(nc.sbuf_tensor("gw_sb", [128, 48], F32))
        rows1 = _ctx.enter_context(nc.sbuf_tensor("rows1", [1, 2048], F32))
        rows2 = _ctx.enter_context(nc.sbuf_tensor("rows2", [1, 2048], F32))
        rrow1 = _ctx.enter_context(nc.sbuf_tensor("rrow1", [1, 512], F32))
        rrow2 = _ctx.enter_context(nc.sbuf_tensor("rrow2", [1, 512], F32))
        lg_sb = _ctx.enter_context(nc.sbuf_tensor("lg_sb", [32, 512], F32))
        lgt = _ctx.enter_context(nc.sbuf_tensor("lgt", [128, 8], F32))
        mxt = _ctx.enter_context(nc.sbuf_tensor("mxt", [128, 8], F32))
        negm = _ctx.enter_context(nc.sbuf_tensor("negm", [128, 8], F32))
        ex6 = _ctx.enter_context(nc.sbuf_tensor("ex6", [128, 8], F32))
        exd = _ctx.enter_context(nc.sbuf_tensor("exd", [128, 8], F32))
        msk6 = _ctx.enter_context(nc.sbuf_tensor("msk6", [128, 8], F32))
        dn8 = _ctx.enter_context(nc.sbuf_tensor("dn8", [128, 8], F32))
        w6f = _ctx.enter_context(nc.sbuf_tensor("w6f", [128, 32], F32))
        w6row = _ctx.enter_context(nc.sbuf_tensor("w6row", [32, 4096], BF16))
        boutc = _ctx.enter_context(nc.sbuf_tensor("boutc", [128, 8], F32))
        lngc = _ctx.enter_context(nc.sbuf_tensor("lngc", [128, 8], F32))
        lnbc = _ctx.enter_context(nc.sbuf_tensor("lnbc", [128, 8], F32))
        gb_sb = _ctx.enter_context(nc.sbuf_tensor("gb_sb", [8, 1], F32))
        onesrf = _ctx.enter_context(nc.sbuf_tensor("onesrf", [1, 128], F32))
        onescb = _ctx.enter_context(nc.sbuf_tensor("onescb", [128, 1], BF16))
        at0 = _ctx.enter_context(nc.psum_tensor([128, 512], F32))
        at1 = _ctx.enter_context(nc.psum_tensor([128, 512], F32))
        murepP = _ctx.enter_context(nc.psum_tensor([128, 512], F32))
        rrepP = _ctx.enter_context(nc.psum_tensor([128, 512], F32))
        sS = _ctx.enter_context(nc.psum_tensor([1, 512], F32))
        sQ = _ctx.enter_context(nc.psum_tensor([1, 512], F32))
        lgP = _ctx.enter_context(nc.psum_tensor([8, 512], F32))
        tpk = _ctx.enter_context(nc.psum_tensor([128, 160], F32))
        blk5 = _ctx.enter_context(nc.Block())

        ats = (at0, at1)

        def prevref(d, tg, ob):
            if ob >= 2:
                return d.get((tg, ob - 2))
            return d.get((tg - 1, ob + 6))

        st_c5 = ch.add("sync", lambda s: [
            s.dma_start(out=boutc[:], in_=p.b_out_col[:, :]),
            s.dma_start(out=lngc[:], in_=p.ln_g_col[:, :]),
            s.dma_start(out=lnbc[:], in_=p.ln_b_col[:, :]),
            s.dma_start(out=gw_sb[:], in_=p.gw_cols[:, :]),
            s.dma_start(out=gb_sb[:], in_=p.gb_col[:, :]),
            s.dma_start(out=onesrf[:], in_=p.ones_rf[:, :]),
            s.dma_start(out=onescb[:], in_=p.ones_cb[:, :]),
        ], deps=[], ndma=7, tag="s5c")

        def einit(v):
            v.memset(lg_sb.ap()[0:32, :], 0.0)
            v.memset(w6f.ap()[:, 6:32], 0.0)
            return v.memset(lgt[:], -1.0e30)
        st_init5 = ch.add("vector", einit, deps=[], tag="s5init")

        st_wT = p.st_wT
        g_mov_done = p.g_mov_done

        st_at, st_v, st_vb, st_sq, st_st1 = {}, {}, {}, {}, {}
        st_x1, st_st2, st_xn, st_lgmm = {}, {}, {}, {}
        st_x1b_d, st_xnb_d, st_ut = {}, {}, {}
        st_rm1 = st_rm2 = st_bc1 = st_bc2 = None
        prev_rm2 = prev_xn_last = prev_lgev = None
        last_w6ev = None
        for tg in range(8):
            for ob in range(8):
                usl = slice((ob % 2) * 512, (ob % 2) * 512 + 512)
                st_ut[(tg, ob)] = ch.add("sync", (
                    lambda s, tg=tg, ob=ob, usl=usl: [
                        s.dma_start(out=u_t[:, usl],
                                    in_=p.u_tok.ap()[
                                        ob * 128:(ob + 1) * 128,
                                        tg * 512:(tg + 1) * 512])]),
                    deps=[prevref(st_v, tg, ob)],
                    ndma=1, tag=f"u5.{tg}.{ob}")
                at = ats[ob % 2]

                def eat(t, tg=tg, ob=ob, at=at):
                    last = None
                    for hb in range(8):
                        last = t.matmul(
                            at[:, :],
                            wT_sb.ap()[:, hb * 1024 + ob * 128:
                                       hb * 1024 + ob * 128 + 128],
                            g_mov.ap()[:, hb * 4096 + tg * 512:
                                       hb * 4096 + tg * 512 + 512],
                            start=(hb == 0), stop=(hb == 7))
                    return last
                adeps = [st_wT, g_mov_done, prevref(st_v, tg, ob)]
                st_at[(tg, ob)] = ch.add("tensor", eat, deps=adeps,
                                         tag=f"at.{tg}.{ob}")

                def ev(v, tg=tg, ob=ob, at=at, usl=usl):
                    return v.scalar_tensor_tensor(
                        v_f[:, ob * 512:(ob + 1) * 512], at[:, :],
                        boutc[:, ob:ob + 1], u_t[:, usl],
                        ALU.add, ALU.add)
                st_v[(tg, ob)] = ch.add(
                    "vector", ev, deps=[st_at[(tg, ob)], st_ut[(tg, ob)]],
                    tag=f"v.{tg}.{ob}")
                st_vb[(tg, ob)] = ch.add(
                    "vector", (lambda v, tg=tg, ob=ob, usl=usl: v.tensor_copy(
                        vb[:, usl], v_f[:, ob * 512:(ob + 1) * 512])),
                    deps=[st_v[(tg, ob)], prevref(st_st1, tg, ob)],
                    tag=f"vb.{tg}.{ob}")
                st_sq[(tg, ob)] = ch.add(
                    "scalar", (lambda s, tg=tg, ob=ob, usl=usl: s.activation(
                        sqt[:, usl], v_f[:, ob * 512:(ob + 1) * 512],
                        AF.Square)),
                    deps=[st_v[(tg, ob)], prevref(st_st1, tg, ob),
                          prevref(st_st2, tg, ob)],
                    tag=f"sq.{tg}.{ob}")

                def est1(t, tg=tg, ob=ob, usl=usl):
                    t.matmul(sS[:, :], onescb.ap()[:, 0:1], vb.ap()[:, usl],
                             start=(ob == 0), stop=(ob == 7))
                    return t.matmul(sQ[:, :], onescb.ap()[:, 0:1],
                                    sqt.ap()[:, usl],
                                    start=(ob == 0), stop=(ob == 7))
                sdeps = [st_vb[(tg, ob)], st_sq[(tg, ob)], st_c5]
                if ob == 0:
                    sdeps.append(prev_rm2)
                st_st1[(tg, ob)] = ch.add("tensor", est1, deps=sdeps,
                                          tag=f"st1.{tg}.{ob}")

            def erm(v, rows=rows1):
                r = rows.ap()
                i = v.tensor_single_scalar(r[0:1, 0:512], sS[0:1, :],
                                           1.0 / 1024, ALU.mult)
                i = v.tensor_single_scalar(r[0:1, 512:1024], sQ[0:1, :],
                                           1.0 / 1024, ALU.mult)
                ch.gap(v, i, "vector")
                i = v.tensor_tensor(r[0:1, 1024:1536], r[0:1, 0:512],
                                    r[0:1, 0:512], ALU.mult)
                ch.gap(v, i, "vector")
                return v.tensor_tensor(r[0:1, 1024:1536], r[0:1, 512:1024],
                                       r[0:1, 1024:1536], ALU.subtract)
            st_rm1 = ch.add("vector", erm, deps=[st_st1[(tg, 7)]],
                            sync_all=True, tag=f"rm1.{tg}")
            st_sd1 = ch.add("scalar", (lambda s, rows=rows1: s.activation(
                rows.ap()[0:1, 1536:2048], rows.ap()[0:1, 1024:1536],
                AF.Sqrt, bias=epsc.ap()[0:1, 0:1])),
                deps=[st_rm1], tag=f"sd1.{tg}")
            st_rc1 = ch.add("vector", (lambda v, rows=rows1: v.reciprocal(
                rrow1.ap()[0:1, :], rows.ap()[0:1, 1536:2048])),
                deps=[st_sd1], sync_all=True, tag=f"rc1.{tg}")

            def ebc(t, rows=rows1):
                t.matmul(murepP[:, :], onesrf.ap()[0:1, :],
                         rows.ap()[0:1, 0:512], start=True, stop=True)
                return t.matmul(rrepP[:, :], onesrf.ap()[0:1, :],
                                rrow1.ap()[0:1, :], start=True, stop=True)
            st_bc1 = ch.add("tensor", ebc,
                            deps=[st_rc1, st_rm1, prev_xn_last],
                            tag=f"bc1.{tg}")

            for ob in range(8):
                usl = slice((ob % 2) * 512, (ob % 2) * 512 + 512)

                def ex1(v, tg=tg, ob=ob, usl=usl):
                    vs = v_f[:, ob * 512:(ob + 1) * 512]
                    xs = x1f[:, ob * 512:(ob + 1) * 512]
                    i = v.tensor_tensor(xs, vs, murepP[:, :], ALU.subtract)
                    ch.gap(v, i, "vector")
                    i = v.tensor_tensor(xs, xs, rrepP[:, :], ALU.mult)
                    ch.gap(v, i, "vector")
                    return v.tensor_copy(x1bf[:, usl], xs)
                st_x1[(tg, ob)] = ch.add(
                    "vector", ex1,
                    deps=[st_bc1, st_st2.get((tg, ob - 2)),
                          prevref(st_x1b_d, tg, ob)],
                    sync_all=True, tag=f"x1.{tg}.{ob}")
                sq2deps = [st_x1[(tg, ob)], st_st2.get((tg, ob - 2))]
                if ob < 2:
                    sq2deps.append(st_st1.get((tg, ob + 6)))
                st_sq2 = ch.add(
                    "scalar", (lambda s, tg=tg, ob=ob, usl=usl: s.activation(
                        sqt[:, usl], x1f[:, ob * 512:(ob + 1) * 512],
                        AF.Square)),
                    deps=sq2deps, tag=f"sq2.{tg}.{ob}")

                def est2(t, tg=tg, ob=ob, usl=usl):
                    t.matmul(sS[:, :], onescb.ap()[:, 0:1], x1bf.ap()[:, usl],
                             start=(ob == 0), stop=(ob == 7))
                    return t.matmul(sQ[:, :], onescb.ap()[:, 0:1],
                                    sqt.ap()[:, usl],
                                    start=(ob == 0), stop=(ob == 7))
                s2deps = [st_x1[(tg, ob)], st_sq2]
                if ob == 0:
                    s2deps.append(st_rm1)
                st_st2[(tg, ob)] = ch.add("tensor", est2, deps=s2deps,
                                          tag=f"st2.{tg}.{ob}")
                st_x1b_d[(tg, ob)] = ch.add("sync", (
                    lambda s, tg=tg, ob=ob, usl=usl: [
                        s.dma_start(out=p.x1b.ap()[ob * 128:(ob + 1) * 128,
                                                   tg * 512:(tg + 1) * 512],
                                    in_=x1bf[:, usl])]),
                    deps=[st_x1[(tg, ob)]], ndma=1, tag=f"x1d.{tg}.{ob}")

            def erm2(v, rows=rows2):
                r = rows.ap()
                i = v.tensor_single_scalar(r[0:1, 0:512], sS[0:1, :],
                                           1.0 / 1024, ALU.mult)
                i = v.tensor_single_scalar(r[0:1, 512:1024], sQ[0:1, :],
                                           1.0 / 1024, ALU.mult)
                ch.gap(v, i, "vector")
                i = v.tensor_tensor(r[0:1, 1024:1536], r[0:1, 0:512],
                                    r[0:1, 0:512], ALU.mult)
                ch.gap(v, i, "vector")
                return v.tensor_tensor(r[0:1, 1024:1536], r[0:1, 512:1024],
                                       r[0:1, 1024:1536], ALU.subtract)
            st_rm2 = ch.add("vector", erm2, deps=[st_st2[(tg, 7)]],
                            sync_all=True, tag=f"rm2.{tg}")
            st_sd2 = ch.add("scalar", (lambda s, rows=rows2: s.activation(
                rows.ap()[0:1, 1536:2048], rows.ap()[0:1, 1024:1536],
                AF.Sqrt, bias=epsc.ap()[0:1, 0:1])),
                deps=[st_rm2], tag=f"sd2.{tg}")
            st_rc2 = ch.add("vector", (lambda v, rows=rows2: v.reciprocal(
                rrow2.ap()[0:1, :], rows.ap()[0:1, 1536:2048])),
                deps=[st_sd2], sync_all=True, tag=f"rc2.{tg}")

            def ebc2(t, rows=rows2):
                t.matmul(murepP[:, :], onesrf.ap()[0:1, :],
                         rows.ap()[0:1, 0:512], start=True, stop=True)
                return t.matmul(rrepP[:, :], onesrf.ap()[0:1, :],
                                rrow2.ap()[0:1, :], start=True, stop=True)
            st_bc2 = ch.add("tensor", ebc2,
                            deps=[st_rc2, st_rm2, st_x1[(tg, 7)]],
                            tag=f"bc2.{tg}")

            for ob in range(8):
                usl = slice((ob % 2) * 512, (ob % 2) * 512 + 512)

                def exn(v, tg=tg, ob=ob, usl=usl):
                    xs = x1f[:, ob * 512:(ob + 1) * 512]
                    i = v.tensor_tensor(xnf[:, usl], xs, murepP[:, :],
                                        ALU.subtract)
                    ch.gap(v, i, "vector")
                    i = v.tensor_tensor(xnf[:, usl], xnf[:, usl], rrepP[:, :],
                                        ALU.mult)
                    ch.gap(v, i, "vector")
                    i = v.tensor_scalar(xnf[:, usl], xnf[:, usl],
                                        lngc[:, ob:ob + 1], lnbc[:, ob:ob + 1],
                                        ALU.mult, ALU.add)
                    ch.gap(v, i, "vector")
                    return v.tensor_copy(xnbf[:, usl], xnf[:, usl])
                st_xn[(tg, ob)] = ch.add(
                    "vector", exn,
                    deps=[st_bc2, prevref(st_lgmm, tg, ob),
                          prevref(st_xnb_d, tg, ob)],
                    sync_all=True, tag=f"xn.{tg}.{ob}")

                def elg(t, tg=tg, ob=ob, usl=usl):
                    return t.matmul(lgP[0:6, :], gw_sb.ap()[:, ob * 6:
                                                            ob * 6 + 6],
                                    xnf.ap()[:, usl],
                                    start=(ob == 0), stop=(ob == 7))
                lgdeps = [st_xn[(tg, ob)]]
                if ob == 0:
                    lgdeps.append(prev_lgev)
                st_lgmm[(tg, ob)] = ch.add("tensor", elg, deps=lgdeps,
                                           tag=f"lg.{tg}.{ob}")
                st_xnb_d[(tg, ob)] = ch.add("sync", (
                    lambda s, tg=tg, ob=ob, usl=usl: [
                        s.dma_start(out=p.xnb.ap()[ob * 128:(ob + 1) * 128,
                                                   tg * 512:(tg + 1) * 512],
                                    in_=xnbf[:, usl])]),
                    deps=[st_xn[(tg, ob)]], ndma=1, tag=f"xnd.{tg}.{ob}")
            prev_xn_last = st_xn[(tg, 7)]
            prev_rm2 = st_rm2

            st_lgev = ch.add("vector", (lambda v: v.tensor_single_scalar(
                lg_sb.ap()[0:6, :], lgP[0:6, :],
                gb_sb.ap()[0:6, 0:1], ALU.add)),
                deps=[st_lgmm[(tg, 7)], st_c5, st_init5, last_w6ev],
                tag=f"lgev.{tg}")
            prev_lgev = st_lgev
            for tt in range(4):
                def etp(t, tt=tt):
                    return t.transpose(tpk[:, 0:32],
                                       lg_sb.ap()[0:32, tt * 128:
                                                  tt * 128 + 128],
                                       eye_f.ap()[0:32, 0:32])
                st_tp = ch.add("tensor", etp,
                               deps=[st_lgev, last_w6ev], tag=f"tp.{tg}.{tt}")

                def etk1(v):
                    i = v.tensor_copy(lgt[:, 0:6], tpk[:, 0:6])
                    ch.gap(v, i, "vector")
                    i = v.max(mxt[:], lgt[:])
                    ch.gap(v, i, "vector")
                    i = v.tensor_single_scalar(negm[:], mxt[:], -1.0,
                                               ALU.mult)
                    ch.gap(v, i, "vector")
                    return v.tensor_single_scalar(msk6[:, 0:6], lgt[:, 0:6],
                                                  mxt[:, 1:2], ALU.is_ge)
                st_tk1 = ch.add("vector", etk1, deps=[st_tp, st_init5],
                                sync_all=True, tag=f"tk1.{tg}.{tt}")

                def etk2(s):
                    s.activation(ex6[:, 0:6], lgt[:, 0:6], AF.Exp,
                                 bias=negm[:, 0:1])
                    return s.activation(exd[:], mxt[:], AF.Exp,
                                        bias=negm[:, 0:1])
                st_tk2 = ch.add("scalar", etk2, deps=[st_tk1],
                                tag=f"tk2.{tg}.{tt}")

                def etk3(v):
                    i = v.tensor_tensor(w6f[:, 0:6], msk6[:, 0:6],
                                        ex6[:, 0:6], ALU.mult)
                    i = v.tensor_single_scalar(dn8[:], exd[:], 1.0, ALU.add)
                    ch.gap(v, i, "vector")
                    i = v.reciprocal(dn8[:], dn8[:])
                    ch.gap(v, i, "vector")
                    return v.tensor_single_scalar(w6f[:, 0:6],
                                                  w6f[:, 0:6],
                                                  dn8[:, 1:2], ALU.mult)
                st_tk3 = ch.add("vector", etk3, deps=[st_tk2],
                                sync_all=True, tag=f"tk3.{tg}.{tt}")

                def etp2(t):
                    return t.transpose(tpk[0:32, 32:160], w6f[:], eye_f[:])
                st_tp2 = ch.add("tensor", etp2, deps=[st_tk3],
                                tag=f"tp2.{tg}.{tt}")

                def ew6(v, tg=tg, tt=tt):
                    return v.tensor_copy(
                        w6row.ap()[0:6, (tg * 4 + tt) * 128:
                                   (tg * 4 + tt) * 128 + 128],
                        tpk[0:6, 32:160])
                last_w6ev = ch.add("vector", ew6, deps=[st_tp2],
                                   tag=f"w6e.{tg}.{tt}")

        st_w6d = ch.add("sync", (lambda s: [
            s.dma_start(out=p.w6_hbm[:, :], in_=w6row[:])]),
            deps=[last_w6ev], ndma=1, tag="w6d")

        if upto == "s5":
            _finish_dump([
                (p.dbg_b.ap()[:, 0:32768].rearrange("q (dk n) -> q dk n",
                                                    n=4096),
                 p.xnb.ap().rearrange("(dk q) n -> q dk n", q=128)),
                (p.dbg_b.ap()[0:32, 32768:36864], p.w6_hbm.ap()[:, :]),
            ])
            ch.add("gpsimd", (lambda gp: [gp.dma_start(
                out=p.dbg_f.ap().rearrange("q (dk n) -> q dk n", n=4096),
                in_=p.x1b.ap().rearrange("(dk q) n -> q dk n", q=128))]),
                deps=ch.frontier(), ndma=1, tag="x1dump")
            ch.build_section(blk5)
            return nc
        with nc.named_scope("s5"):
            ch.build_section(blk5)
    stkG.close()

    ch.section_barrier()
    with contextlib.ExitStack() as _ctx:
        aw1_sb = _ctx.enter_context(nc.sbuf_tensor("aw1_sb", [128, 32768],
                                                   BF16))
        aw2_sb = _ctx.enter_context(nc.sbuf_tensor("aw2_sb", [128, 32768],
                                                   BF16))
        awstg = _ctx.enter_context(nc.sbuf_tensor("awstg", [128, 6144], F32))
        xn_tg = _ctx.enter_context(nc.sbuf_tensor("xn_tg", [128, 4096], BF16))
        a1_sb = _ctx.enter_context(nc.sbuf_tensor("a1_sb", [128, 16384],
                                                  BF16))
        abst = _ctx.enter_context(nc.sbuf_tensor("abst", [128, 1024], BF16))
        ab1c = _ctx.enter_context(nc.sbuf_tensor("ab1c", [128, 32], F32))
        ab2c = _ctx.enter_context(nc.sbuf_tensor("ab2c", [128, 8], F32))
        pr0 = _ctx.enter_context(nc.psum_tensor([128, 512], F32))
        pr1 = _ctx.enter_context(nc.psum_tensor([128, 512], F32))
        cn0 = _ctx.enter_context(nc.psum_tensor([128, 512], F32))
        cn1 = _ctx.enter_context(nc.psum_tensor([128, 512], F32))
        blk6 = _ctx.enter_context(nc.Block())

        st_c6 = ch.add("sync", lambda s: [
            s.dma_start(out=ab1c[:], in_=p.ab1_col[:, :]),
            s.dma_start(out=ab2c[:], in_=p.ab2_col[:, :]),
        ], deps=[], ndma=2, tag="s6c")

        st_w1c, st_w2c = {}, {}
        for ck in range(16):
            dblk, hh = ck >> 1, ck & 1
            asl = slice((ck % 2) * 2048, (ck % 2) * 2048 + 2048)
            sti = ch.add("sync", (lambda s, dblk=dblk, hh=hh, asl=asl: [
                s.dma_start(out=awstg[:, asl],
                            in_=p.aw1.ap()[dblk * 128:(dblk + 1) * 128,
                                           hh * 2048:(hh + 1) * 2048])]),
                deps=[st_w1c.get(ck - 2)], ndma=1, tag=f"aw1i.{ck}")
            st_w1c[ck] = ch.add("scalar", (
                lambda s, dblk=dblk, hh=hh, asl=asl: s.copy(
                    aw1_sb[:, dblk * 4096 + hh * 2048:
                           dblk * 4096 + hh * 2048 + 2048],
                    awstg[:, asl])),
                deps=[sti], tag=f"aw1c.{ck}")
        for fb in range(32):
            asl = slice(4096 + (fb % 2) * 1024, 4096 + (fb % 2) * 1024 + 1024)
            sti = ch.add("gpsimd", (lambda s, fb=fb, asl=asl: [
                s.dma_start(out=awstg[:, asl],
                            in_=p.aw2.ap()[fb * 128:(fb + 1) * 128, :])]),
                deps=[st_w2c.get(fb - 2)],
                ndma=1, tag=f"aw2i.{fb}")
            st_w2c[fb] = ch.add("vector", (lambda v, fb=fb, asl=asl:
                                v.tensor_copy(
                                    aw2_sb[:, fb * 1024:(fb + 1) * 1024],
                                    awstg[:, asl])),
                                deps=[sti], tag=f"aw2c.{fb}")
        aw1_done, aw2_done = st_w1c[15], st_w2c[31]

        st_xn6, st_pr, st_prev, st_cn, st_cev, st_abd = {}, {}, {}, {}, {}, {}
        for tg in range(8):
            st_xn6[tg] = ch.add("sync", (lambda s, tg=tg: [
                s.dma_start(out=xn_tg[:, dblk * 512:(dblk + 1) * 512],
                            in_=p.xnb.ap()[dblk * 128:(dblk + 1) * 128,
                                           tg * 512:(tg + 1) * 512])
                for dblk in range(8)]),
                deps=[st_pr.get((tg - 1, 31))], ndma=8, tag=f"xn6.{tg}")
            for fb in range(32):
                pr = (pr0, pr1)[fb % 2]

                def epr(t, tg=tg, fb=fb, pr=pr):
                    last = None
                    for dblk in range(8):
                        last = t.matmul(
                            pr[:, :],
                            aw1_sb.ap()[:, dblk * 4096 + fb * 128:
                                        dblk * 4096 + fb * 128 + 128],
                            xn_tg.ap()[:, dblk * 512:(dblk + 1) * 512],
                            start=(dblk == 0), stop=(dblk == 7))
                    return last
                prdeps = [st_xn6[tg], st_prev.get((tg, fb - 2))]
                if fb == 0:
                    prdeps += [aw1_done, st_cn.get((tg - 1, 7))]
                if tg == 0 and fb < 2:
                    prdeps.append(st_prev.get((tg - 1, 30 + fb)))
                st_pr[(tg, fb)] = ch.add("tensor", epr, deps=prdeps,
                                         tag=f"pr.{tg}.{fb}")
                st_prev[(tg, fb)] = ch.add("scalar", (
                    lambda s, tg=tg, fb=fb, pr=pr: s.activation(
                        a1_sb[:, fb * 512:(fb + 1) * 512], pr[:, :],
                        AF.Gelu, bias=ab1c.ap()[:, fb:fb + 1])),
                    deps=[st_pr[(tg, fb)], st_c6], tag=f"prev.{tg}.{fb}")
            for ob in range(8):
                cn = (cn0, cn1)[ob % 2]

                def ecn(t, tg=tg, ob=ob, cn=cn):
                    last = None
                    for fb in range(32):
                        last = t.matmul(
                            cn[:, :],
                            aw2_sb.ap()[:, fb * 1024 + ob * 128:
                                        fb * 1024 + ob * 128 + 128],
                            a1_sb.ap()[:, fb * 512:(fb + 1) * 512],
                            start=(fb == 0), stop=(fb == 31))
                    return last
                cndeps = [st_prev[(tg, 31)], st_cev.get((tg, ob - 2))]
                if ob == 0:
                    cndeps.append(aw2_done)
                st_cn[(tg, ob)] = ch.add("tensor", ecn, deps=cndeps,
                                         tag=f"cn.{tg}.{ob}")
                bsl = slice((ob % 2) * 512, (ob % 2) * 512 + 512)
                st_cev[(tg, ob)] = ch.add("vector", (
                    lambda v, ob=ob, cn=cn, bsl=bsl: v.tensor_single_scalar(
                        abst[:, bsl], cn[:, :], ab2c.ap()[:, ob:ob + 1],
                        ALU.add)),
                    deps=[st_cn[(tg, ob)], st_c6, st_abd.get((tg, ob - 2))],
                    tag=f"cev.{tg}.{ob}")
                st_abd[(tg, ob)] = ch.add("sync", (
                    lambda s, tg=tg, ob=ob, bsl=bsl: [
                        s.dma_start(out=p.abelb.ap()[ob * 128:(ob + 1) * 128,
                                                     tg * 512:(tg + 1) * 512],
                                    in_=abst[:, bsl])]),
                    deps=[st_cev[(tg, ob)]], ndma=1, tag=f"abd.{tg}.{ob}")

        if upto == "s6":
            _finish_dump([
                (p.dbg_b.ap()[:, 0:32768].rearrange("q (dk n) -> q dk n",
                                                    n=4096),
                 p.abelb.ap().rearrange("(dk q) n -> q dk n", q=128)),
            ])
            ch.build_section(blk6)
            return nc
        with nc.named_scope("s6"):
            ch.build_section(blk6)

    ch.section_barrier()
    with contextlib.ExitStack() as _ctx:
        ewc1 = _ctx.enter_context(nc.sbuf_tensor("ewc1", [128, 16384], BF16))
        ewc2 = _ctx.enter_context(nc.sbuf_tensor("ewc2", [128, 16384], BF16))
        xn_sp = _ctx.enter_context(nc.sbuf_tensor("xn_sp", [128, 8192], BF16))
        u16 = _ctx.enter_context(nc.sbuf_tensor("u16", [128, 8192], BF16))
        nb_acc = _ctx.enter_context(nc.sbuf_tensor("nb_acc", [128, 8192],
                                                   F32))
        w6s = _ctx.enter_context(nc.sbuf_tensor("w6s", [32, 4096], BF16))
        esel_sb = _ctx.enter_context(nc.sbuf_tensor("esel_sb", [8, 768],
                                                    BF16))
        eb2_sb = _ctx.enter_context(nc.sbuf_tensor("eb2_sb", [8, 1024], BF16))
        eb1c = _ctx.enter_context(nc.sbuf_tensor("eb1c", [128, 96], F32))
        wrep_f = _ctx.enter_context(nc.sbuf_tensor("wrep_f", [128, 1024],
                                                   F32))
        nbr_b = _ctx.enter_context(nc.sbuf_tensor("nbr_b", [128, 4096],
                                                  BF16))
        abel_t = _ctx.enter_context(nc.sbuf_tensor("abel_t", [128, 4096],
                                                   BF16))
        x1_t = _ctx.enter_context(nc.sbuf_tensor("x1_t", [128, 1024], BF16))
        nbb = _ctx.enter_context(nc.sbuf_tensor("nbb", [128, 4096], BF16))
        z_f = _ctx.enter_context(nc.sbuf_tensor("z_f", [128, 4096], F32))
        sqf = _ctx.enter_context(nc.sbuf_tensor("sqf", [128, 512], F32))
        gcstg = _ctx.enter_context(nc.sbuf_tensor("gcstg", [128, 16], F32))
        gc1b = _ctx.enter_context(nc.sbuf_tensor("gc1b", [128, 8], BF16))
        gc2b = _ctx.enter_context(nc.sbuf_tensor("gc2b", [128, 8], BF16))
        gcbf = _ctx.enter_context(nc.sbuf_tensor("gcbf", [1, 1], F32))
        growf = _ctx.enter_context(nc.sbuf_tensor("growf", [1, 512], F32))
        onesrf2 = _ctx.enter_context(nc.sbuf_tensor("onesrf2", [1, 128], F32))
        onescf = _ctx.enter_context(nc.sbuf_tensor("onescf", [128, 1], F32))
        rowsF = _ctx.enter_context(nc.sbuf_tensor("rowsF", [1, 2048], F32))
        rrowF = _ctx.enter_context(nc.sbuf_tensor("rrowF", [1, 512], F32))
        ot = _ctx.enter_context(nc.sbuf_tensor("ot", [128, 1024], F32))
        mp0 = _ctx.enter_context(nc.psum_tensor([128, 512], F32))
        mp1 = _ctx.enter_context(nc.psum_tensor([128, 512], F32))
        me0 = _ctx.enter_context(nc.psum_tensor([128, 512], F32))
        me1 = _ctx.enter_context(nc.psum_tensor([128, 512], F32))
        wrepP = _ctx.enter_context(nc.psum_tensor([128, 512], F32))
        gateP = _ctx.enter_context(nc.psum_tensor([8, 512], F32))
        sF1 = _ctx.enter_context(nc.psum_tensor([1, 512], F32))
        sF2 = _ctx.enter_context(nc.psum_tensor([1, 512], F32))
        blk7 = _ctx.enter_context(nc.Block())

        mps = (mp0, mp1)
        mes = (me0, me1)

        st_c7 = ch.add("sync", lambda s: [
            s.dma_start(out=w6s[:], in_=p.w6_hbm[:, :]),
            s.dma_start(out=esel_sb.ap()[0:6, :], in_=p.esel[:, :]),
            s.dma_start(out=z_f.ap()[0:8, 0:1024], in_=p.eb2_nat[:, :]),
            s.dma_start(out=eb1c[:], in_=p.eb1_col[:, :]),
            s.dma_start(out=gcstg[:, 0:8], in_=p.gc1_col[:, :]),
            s.dma_start(out=gcstg[:, 8:16], in_=p.gc2_col[:, :]),
            s.dma_start(out=gcbf[:], in_=p.gcb[:, :]),
            s.dma_start(out=onesrf2[:], in_=p.ones_rf[:, :]),
            s.dma_start(out=onescf[:], in_=p.ones_cf[:, :]),
        ], deps=[], ndma=9, tag="s7c")

        def ec7(v):
            i = v.tensor_copy(eb2_sb[:], z_f.ap()[0:8, 0:1024])
            i = v.tensor_copy(gc1b[:], gcstg[:, 0:8])
            return v.tensor_copy(gc2b[:], gcstg[:, 8:16])
        st_cc7 = ch.add("vector", ec7, deps=[st_c7], tag="s7cc")

        st_e1, st_e2, st_mp, st_pev, st_cm, st_acc = {}, {}, {}, {}, {}, {}
        st_wrp, st_wev = {}, {}
        st_xn7, st_rld, st_rot, st_mix, st_sqF, st_stF = {}, {}, {}, {}, {}, {}
        st_apl, st_od, st_x1t, st_abt, st_nbb = {}, {}, {}, {}, {}
        st_grep, st_bcF, st_rev = {}, {}, {}
        out_dmas = []
        for s in range(4):
            st_xn7[s] = ch.add("sync", (lambda sy, s=s: [
                sy.dma_start(out=xn_sp[:, dblk * 1024:(dblk + 1) * 1024],
                             in_=p.xnb.ap()[dblk * 128:(dblk + 1) * 128,
                                            s * 1024:(s + 1) * 1024])
                for dblk in range(8)]),
                deps=[st_mp.get((s - 1, 5, 1, 15))], ndma=8, tag=f"xn7.{s}")
            for e in range(6):
                e1deps = [st_mp.get((s, e - 1, 1, 15)),
                          st_mp.get((s - 1, 5, 1, 15)) if e == 0 else None]
                st_e1a = ch.add("sync", (lambda sy, e=e: [
                    sy.dma_start(out=ewc1[:, dblk * 2048:(dblk + 1) * 2048],
                                 in_=p.cc_e1out.ap()[
                                     e * 1024 + dblk * 128:
                                     e * 1024 + dblk * 128 + 128, :])
                    for dblk in range(4)]),
                    deps=e1deps, ndma=4, tag=f"ew1a.{s}.{e}")
                st_e1b = ch.add("scalar", (lambda sy, e=e: [
                    sy.dma_start(out=ewc1[:, dblk * 2048:(dblk + 1) * 2048],
                                 in_=p.cc_e1out.ap()[
                                     e * 1024 + dblk * 128:
                                     e * 1024 + dblk * 128 + 128, :])
                    for dblk in range(4, 8)]),
                    deps=e1deps, ndma=4, tag=f"ew1b.{s}.{e}")
                st_e1[(s, e)] = st_e1a
                e2deps = [st_cm.get((s, e - 1, 1, 7)),
                          st_cm.get((s - 1, 5, 1, 7)) if e == 0 else None]
                st_e2a = ch.add("gpsimd", (lambda sy, e=e: [
                    sy.dma_start(out=ewc2[:, fb * 1024:(fb + 1) * 1024],
                                 in_=p.cc_e2out.ap()[
                                     e * 2048 + fb * 128:
                                     e * 2048 + fb * 128 + 128, :])
                    for fb in range(8)]),
                    deps=e2deps, ndma=8, tag=f"ew2a.{s}.{e}")
                st_e2b = ch.add("scalar", (lambda sy, e=e: [
                    sy.dma_start(out=ewc2[:, fb * 1024:(fb + 1) * 1024],
                                 in_=p.cc_e2out.ap()[
                                     e * 2048 + fb * 128:
                                     e * 2048 + fb * 128 + 128, :])
                    for fb in range(8, 16)]),
                    deps=e2deps, ndma=8, tag=f"ew2b.{s}.{e}")
                st_e2[(s, e)] = st_e2a
                for sub2 in range(2):
                    tsl = slice(s * 1024 + sub2 * 512,
                                s * 1024 + sub2 * 512 + 512)
                    wsl = slice(sub2 * 512, sub2 * 512 + 512)

                    def ewr(t, e=e, tsl=tsl):
                        return t.matmul(wrepP[:, :],
                                        esel_sb.ap()[0:6, e * 128:
                                                     e * 128 + 128],
                                        w6s.ap()[0:6, tsl],
                                        start=True, stop=True)
                    if sub2 == 1:
                        prev_wev = st_wev.get((s, e, 0))
                    elif e > 0:
                        prev_wev = st_wev.get((s, e - 1, 1))
                    else:
                        prev_wev = st_wev.get((s - 1, 5, 1))
                    wdeps = [st_c7, prev_wev]
                    if e == 0 and sub2 == 0:
                        wdeps.append(st_mix.get((s - 1, 1, 7)))
                    st_wrp[(s, e, sub2)] = ch.add("tensor", ewr, deps=wdeps,
                                                  tag=f"wr.{s}.{e}.{sub2}")
                    st_wev[(s, e, sub2)] = ch.add("scalar", (
                        lambda sc, wsl=wsl: sc.copy(wrep_f[:, wsl],
                                                    wrepP[:, :])),
                        deps=[st_wrp[(s, e, sub2)],
                              st_acc.get((s, e - 1, sub2, 7)),
                              st_acc.get((s - 1, 5, sub2, 7))
                              if e == 0 else None],
                        tag=f"we.{s}.{e}.{sub2}")

                    for fb in range(16):
                        mp = mps[fb % 2]

                        def epr7(t, e=e, fb=fb, mp=mp, sub2=sub2, s=s):
                            last = None
                            for dblk in range(8):
                                last = t.matmul(
                                    mp[:, :],
                                    ewc1.ap()[:, dblk * 2048 + fb * 128:
                                              dblk * 2048 + fb * 128 + 128],
                                    xn_sp.ap()[:, dblk * 1024 + sub2 * 512:
                                               dblk * 1024 + sub2 * 512
                                               + 512],
                                    start=(dblk == 0), stop=(dblk == 7))
                            return last
                        pdeps = [st_e1a, st_e1b, st_xn7[s],
                                 st_pev.get((s, e, sub2, fb // 2 - 1))]
                        if fb < 2:
                            pdeps += [st_pev.get((s, e, sub2 - 1, 7)),
                                      st_pev.get((s, e - 1, 1, 7)),
                                      st_pev.get((s - 1, 5, 1, 7))
                                      if e == 0 else None,
                                      st_rev.get((s - 1, 1, 6))
                                      if e == 0 else None,
                                      st_rev.get((s - 1, 1, 7))
                                      if e == 0 else None]
                        st_mp[(s, e, sub2, fb)] = ch.add(
                            "tensor", epr7, deps=pdeps,
                            tag=f"mp.{s}.{e}.{sub2}.{fb}")
                        if fb % 2 == 1:
                            fp = fb // 2

                            def epe7(sc, e=e, fp=fp):
                                i = None
                                for j in range(2):
                                    fbj = 2 * fp + j
                                    i = sc.activation(
                                        u16[:, fbj * 512:(fbj + 1) * 512],
                                        mps[j][:, :], AF.Gelu,
                                        bias=eb1c.ap()[:, e * 16 + fbj:
                                                       e * 16 + fbj + 1])
                                return i
                            st_pev[(s, e, sub2, fp)] = ch.add(
                                "scalar", epe7,
                                deps=[st_mp[(s, e, sub2, fb)], st_c7],
                                tag=f"pe.{s}.{e}.{sub2}.{fp}")

                    for ob in range(8):
                        me = mes[ob % 2]

                        def ecn7(t, e=e, ob=ob, me=me, tsl=tsl):
                            first = True
                            if e == 0:
                                t.matmul(me[:, :],
                                         eb2_sb.ap()[0:6, ob * 128:
                                                     ob * 128 + 128],
                                         w6s.ap()[0:6, tsl],
                                         start=True, stop=False)
                                first = False
                            last = None
                            for fb in range(16):
                                last = t.matmul(
                                    me[:, :],
                                    ewc2.ap()[:, fb * 1024 + ob * 128:
                                              fb * 1024 + ob * 128 + 128],
                                    u16.ap()[:, fb * 512:(fb + 1) * 512],
                                    start=first and fb == 0, stop=(fb == 15))
                            return last
                        cdeps = [st_pev[(s, e, sub2, 7)], st_e2a, st_e2b,
                                 st_acc.get((s, e, sub2, ob - 2))]
                        if ob < 2:
                            cdeps += [st_acc.get((s, e, sub2 - 1, 6 + ob)),
                                      st_acc.get((s, e - 1, 1, 6 + ob)),
                                      st_apl.get((s - 1, 1, 6 + ob))
                                      if e == 0 else None]
                        st_cm[(s, e, sub2, ob)] = ch.add(
                            "tensor", ecn7, deps=cdeps,
                            tag=f"cm.{s}.{e}.{sub2}.{ob}")
                        asl = slice(ob * 1024 + sub2 * 512,
                                    ob * 1024 + sub2 * 512 + 512)

                        def eacc(v, e=e, me=me, asl=asl, wsl=wsl):
                            if e == 0:
                                return v.tensor_tensor(
                                    nb_acc[:, asl], me[:, :], wrep_f[:, wsl],
                                    ALU.mult)
                            i = v.tensor_tensor(z_f[:, 0:512], me[:, :],
                                                wrep_f[:, wsl], ALU.mult)
                            ch.gap(v, i, "vector")
                            return v.tensor_tensor(nb_acc[:, asl],
                                                   nb_acc[:, asl],
                                                   z_f[:, 0:512], ALU.add)
                        st_acc[(s, e, sub2, ob)] = ch.add(
                            "vector", eacc,
                            deps=[st_cm[(s, e, sub2, ob)],
                                  st_wev[(s, e, sub2)]],
                            sync_all=True, tag=f"ac.{s}.{e}.{sub2}.{ob}")

            st_rld[s] = ch.add("sync", (lambda sy: [
                sy.dma_start(out=u16.ap().rearrange("q (dk o) -> q dk o",
                                                    o=1024),
                             in_=p.exR.ap().rearrange("(dk q) o -> q dk o",
                                                      q=128))]),
                deps=[st_cm[(s, 5, 1, 7)]], ndma=1, tag=f"rld.{s}")
            for sub2 in range(2):
                tb = s * 1024 + sub2 * 512
                wsl = slice(sub2 * 512, sub2 * 512 + 512)
                for hq in range(2):
                    def enbb(v, sub2=sub2, hq=hq):
                        last = None
                        for dblk in range(4 * hq, 4 * hq + 4):
                            last = v.tensor_copy(
                                nbb[:, dblk * 512:(dblk + 1) * 512],
                                nb_acc[:, dblk * 1024 + sub2 * 512:
                                       dblk * 1024 + sub2 * 512 + 512])
                        return last
                    st_nbb[(s, sub2, hq)] = ch.add(
                        "vector", enbb,
                        deps=[st_acc[(s, 5, sub2, 7)],
                              st_rot.get((s - 1, 1, 7)) if sub2 == 0
                              else st_rot.get((s, 0, 7))],
                        tag=f"nbb.{s}.{sub2}.{hq}")
                st_abt[(s, sub2)] = ch.add("sync", (
                    lambda sy, s=s, sub2=sub2, tb=tb: [
                        sy.dma_start(out=abel_t[:, ob * 512:(ob + 1) * 512],
                                     in_=p.abelb.ap()[
                                         ob * 128:(ob + 1) * 128,
                                         tb:tb + 512])
                        for ob in range(8)]),
                    deps=[st_mix.get((s, sub2 - 1, 7)),
                          st_mix.get((s - 1, 1, 7)) if sub2 == 0 else None,
                          st_grep.get((s, sub2 - 1)),
                          st_grep.get((s - 1, 1)) if sub2 == 0 else None],
                    ndma=8, tag=f"abt.{s}.{sub2}")

                for ob in range(8):
                    mp = mps[ob % 2]

                    def erot(t, ob=ob, mp=mp):
                        last = None
                        for dblk in range(8):
                            last = t.matmul(
                                mp[:, :],
                                u16.ap()[:, dblk * 1024 + ob * 128:
                                         dblk * 1024 + ob * 128 + 128],
                                nbb.ap()[:, dblk * 512:(dblk + 1) * 512],
                                start=(dblk == 0), stop=(dblk == 7))
                        return last
                    st_rot[(s, sub2, ob)] = ch.add(
                        "tensor", erot,
                        deps=[st_rld[s], st_nbb[(s, sub2, 0)],
                              st_nbb[(s, sub2, 1)],
                              st_rev.get((s, sub2, ob - 2))],
                        tag=f"rt.{s}.{sub2}.{ob}")
                    st_rev[(s, sub2, ob)] = ch.add("scalar", (
                        lambda sc, ob=ob, mp=mp: sc.copy(
                            nbr_b[:, ob * 512:(ob + 1) * 512], mp[:, :])),
                        deps=[st_rot[(s, sub2, ob)],
                              st_mix.get((s, sub2 - 1, ob)),
                              st_mix.get((s - 1, 1, ob))
                              if sub2 == 0 else None],
                        tag=f"rv.{s}.{sub2}.{ob}")

                def egt(t, sub2=sub2):
                    for ob in range(8):
                        t.matmul(gateP[0:1, :], gc1b.ap()[:, ob:ob + 1],
                                 abel_t.ap()[:, ob * 512:(ob + 1) * 512],
                                 start=(ob == 0), stop=False)
                    last = None
                    for ob in range(8):
                        last = t.matmul(
                            gateP[0:1, :], gc2b.ap()[:, ob:ob + 1],
                            nbr_b.ap()[:, ob * 512:(ob + 1) * 512],
                            start=False, stop=(ob == 7))
                    return last
                st_gt = ch.add("tensor", egt,
                               deps=[st_abt[(s, sub2)],
                                     st_rev[(s, sub2, 7)], st_cc7,
                                     st_grep.get((s, sub2 - 1)),
                                     st_grep.get((s - 1, 1))
                                     if sub2 == 0 else None],
                               tag=f"gt.{s}.{sub2}")
                st_sg = ch.add("scalar", (lambda sc: sc.activation(
                    growf[0:1, :], gateP[0:1, :], AF.Sigmoid,
                    bias=gcbf.ap()[0:1, 0:1])),
                    deps=[st_gt, st_c7], tag=f"sg.{s}.{sub2}")
                st_grep[(s, sub2)] = ch.add("tensor", (lambda t: t.matmul(
                    wrepP[:, :], onesrf2.ap()[0:1, :], growf[0:1, :],
                    start=True, stop=True)),
                    deps=[st_sg, st_wev[(s, 5, sub2)],
                          st_acc[(s, 5, sub2, 7)]],
                    tag=f"gr.{s}.{sub2}")

                for ob in range(8):
                    xsl = slice((ob % 2) * 512, (ob % 2) * 512 + 512)
                    st_x1t[(s, sub2, ob)] = ch.add("sync", (
                        lambda sy, s=s, sub2=sub2, ob=ob, xsl=xsl, tb=tb: [
                            sy.dma_start(out=x1_t[:, xsl],
                                         in_=p.x1b.ap()[
                                             ob * 128:(ob + 1) * 128,
                                             tb:tb + 512])]),
                        deps=[st_mix.get((s, sub2, ob - 2)),
                              st_mix.get((s, sub2 - 1, 6 + ob))
                              if ob < 2 else None,
                              st_mix.get((s - 1, 1, 6 + ob))
                              if ob < 2 and sub2 == 0 else None],
                        ndma=1, tag=f"x1t.{s}.{sub2}.{ob}")

                    def emix(v, ob=ob, xsl=xsl):
                        zsl = slice(ob * 512, ob * 512 + 512)
                        absl = slice(ob * 512, ob * 512 + 512)
                        i = v.tensor_tensor(z_f[:, zsl], nbr_b[:, absl],
                                            abel_t[:, absl], ALU.subtract)
                        ch.gap(v, i, "vector")
                        i = v.tensor_tensor(z_f[:, zsl], z_f[:, zsl],
                                            wrepP[:, :], ALU.mult)
                        ch.gap(v, i, "vector")
                        i = v.tensor_tensor(z_f[:, zsl], z_f[:, zsl],
                                            abel_t[:, absl], ALU.add)
                        ch.gap(v, i, "vector")
                        return v.scalar_tensor_tensor(
                            z_f[:, zsl], x1_t[:, xsl], 2.0, z_f[:, zsl],
                            ALU.mult, ALU.add)
                    st_mix[(s, sub2, ob)] = ch.add(
                        "vector", emix,
                        deps=[st_rev[(s, sub2, ob)], st_grep[(s, sub2)],
                              st_x1t[(s, sub2, ob)], st_abt[(s, sub2)],
                              st_stF.get((s, sub2 - 1, ob)),
                              st_stF.get((s - 1, 1, ob)) if sub2 == 0
                              else None,
                              st_apl.get((s, sub2 - 1, ob)),
                              st_apl.get((s - 1, 1, ob)) if sub2 == 0
                              else None],
                        sync_all=True, tag=f"mx.{s}.{sub2}.{ob}")
                    st_sqF[(s, sub2, ob)] = ch.add("scalar", (
                        lambda sc, ob=ob: sc.activation(
                            sqf[:, 0:512], z_f[:, ob * 512:(ob + 1) * 512],
                            AF.Square)),
                        deps=[st_mix[(s, sub2, ob)], st_cc7,
                              st_stF.get((s, sub2, ob - 1)),
                              (st_stF.get((s, sub2 - 1, 7))
                               or st_stF.get((s - 1, 1, 7)))
                              if ob == 0 else None],
                        tag=f"sqF.{s}.{sub2}.{ob}")

                    def estF(t, ob=ob, xsl=xsl):
                        t.matmul(sF1[:, :], onescf.ap()[:, 0:1],
                                 z_f.ap()[:, ob * 512:(ob + 1) * 512],
                                 start=(ob == 0), stop=(ob == 7))
                        return t.matmul(sF2[:, :], onescf.ap()[:, 0:1],
                                        sqf.ap()[:, 0:512],
                                        start=(ob == 0), stop=(ob == 7))
                    stFdeps = [st_mix[(s, sub2, ob)], st_sqF[(s, sub2, ob)],
                               st_c7]
                    if ob == 0:
                        stFdeps += [st_bcF.get((s, sub2 - 1)),
                                    st_bcF.get((s - 1, 1))
                                    if sub2 == 0 else None]
                    st_stF[(s, sub2, ob)] = ch.add("tensor", estF,
                                                   deps=stFdeps,
                                                   tag=f"sF.{s}.{sub2}.{ob}")

                def ermF(v):
                    r = rowsF.ap()
                    i = v.tensor_single_scalar(r[0:1, 0:512], sF1[0:1, :],
                                               1.0 / 1024, ALU.mult)
                    i = v.tensor_single_scalar(r[0:1, 512:1024], sF2[0:1, :],
                                               1.0 / 1024, ALU.mult)
                    ch.gap(v, i, "vector")
                    i = v.tensor_tensor(r[0:1, 1024:1536], r[0:1, 0:512],
                                        r[0:1, 0:512], ALU.mult)
                    ch.gap(v, i, "vector")
                    return v.tensor_tensor(r[0:1, 1024:1536],
                                           r[0:1, 512:1024],
                                           r[0:1, 1024:1536], ALU.subtract)
                st_rmF = ch.add("vector", ermF, deps=[st_stF[(s, sub2, 7)]],
                                sync_all=True, tag=f"rmF.{s}.{sub2}")
                st_sdF = ch.add("scalar", (lambda sc: sc.activation(
                    rowsF.ap()[0:1, 1536:2048], rowsF.ap()[0:1, 1024:1536],
                    AF.Sqrt, bias=epsc.ap()[0:1, 0:1])),
                    deps=[st_rmF], tag=f"sdF.{s}.{sub2}")
                st_rcF = ch.add("vector", (lambda v: v.reciprocal(
                    rrowF.ap()[0:1, :], rowsF.ap()[0:1, 1536:2048])),
                    deps=[st_sdF], sync_all=True, tag=f"rcF.{s}.{sub2}")

                def ebcF(t):
                    t.matmul(me0[:, :], onesrf2.ap()[0:1, :],
                             rowsF.ap()[0:1, 0:512], start=True, stop=True)
                    return t.matmul(me1[:, :], onesrf2.ap()[0:1, :],
                                    rrowF.ap()[0:1, :], start=True, stop=True)
                st_bcF[(s, sub2)] = ch.add(
                    "tensor", ebcF,
                    deps=[st_rcF, st_rmF, st_acc[(s, 5, 1, 7)]],
                    tag=f"bcF.{s}.{sub2}")

                for ob in range(8):
                    osl = slice((ob % 2) * 512, (ob % 2) * 512 + 512)

                    def eapl(v, ob=ob, osl=osl):
                        zsl = slice(ob * 512, ob * 512 + 512)
                        i = v.tensor_tensor(ot[:, osl], z_f[:, zsl],
                                            me0[:, :], ALU.subtract)
                        ch.gap(v, i, "vector")
                        return v.tensor_tensor(ot[:, osl], ot[:, osl],
                                               me1[:, :], ALU.mult)
                    st_apl[(s, sub2, ob)] = ch.add(
                        "vector", eapl,
                        deps=[st_bcF[(s, sub2)], st_od.get((s, sub2, ob - 2)),
                              st_od.get((s, sub2 - 1, 6 + ob))
                              if ob < 2 else None,
                              st_od.get((s - 1, 1, 6 + ob))
                              if ob < 2 and sub2 == 0 else None],
                        sync_all=True, tag=f"ap.{s}.{sub2}.{ob}")
                    st_od[(s, sub2, ob)] = ch.add("sync", (
                        lambda sy, s=s, sub2=sub2, ob=ob, osl=osl, tb=tb: [
                            sy.dma_start(out=p.out_ot.ap()[
                                ob * 128:(ob + 1) * 128, tb:tb + 512],
                                in_=ot[:, osl])]),
                        deps=[st_apl[(s, sub2, ob)]], ndma=1,
                        tag=f"od.{s}.{sub2}.{ob}")
                    out_dmas.append(st_od[(s, sub2, ob)])

        if upto == "s7nb":
            _finish_dump([(p.dbg_f.ap()[:, 0:8192], nb_acc.ap()[:, :])])
            ch.build_section(blk7)
            return nc

        ch.add("gpsimd", lambda gp: gp.engine_nop(), deps=out_dmas,
               tag="fin")
        with nc.named_scope("s7"):
            ch.build_section(blk7)

    return nc


def phase1_kernel(**inputs):
    in_maps = [_prep_phase1_inputs(i, inputs) for i in range(NCORE)]
    nc = build_program(phase1_only=True)
    nc.finalize()
    res = run_bass_kernel_spmd(nc, in_maps, core_ids=list(range(NCORE)))
    return res.results[0]["dbg_g"]


def kernel(**inputs):
    in_maps = [prep_inputs(i, inputs) for i in range(NCORE)]
    nc = build_program()
    nc.finalize()
    res = run_bass_kernel_spmd(nc, in_maps, core_ids=list(range(NCORE)))
    out = np.empty((B, L, H), np.float32)
    for i in range(NCORE):
        bc, half = i >> 1, i & 1
        out[bc, half * 4096:(half + 1) * 4096, :] = np.ascontiguousarray(
            np.asarray(res.results[i]["out_ot"]).T)
    return out
